# version 04
# baseline (speedup 1.0000x reference)
from contextlib import ExitStack

import numpy as np
import ml_dtypes

import concourse.bass as bass
import concourse.mybir as mybir
from concourse.bass_utils import run_bass_kernel_spmd

F32 = mybir.dt.float32
BF16 = mybir.dt.bfloat16
I8 = mybir.dt.int8
AF = mybir.ActivationFunctionType
ALU = mybir.AluOpType
AX = mybir.AxisListType

S = 2048
D = 1024
NT = 16
NC = 8
H = 16
HD = 64
LN_EPS = 1e-5
GROUPS = [(0, 6), (6, 11), (11, 16)]
RING = 6


class Sem:
    def __init__(self, nc, es, name):
        self.h = es.enter_context(nc.semaphore(name))
        self.n = 0

    def inc(self, instr, k=1):
        instr.then_inc(self.h, k)
        self.n += k
        return self.n


def build_program(taps=(), stage=99):
    taps = set(taps)
    nc = bass.Bass("TRN2", target_bir_lowering=False, debug=False)
    es = ExitStack()
    sems = []
    pre_sems = {}

    def sem(name):
        if name in pre_sems:
            return pre_sems.pop(name)
        s = Sem(nc, es, name)
        sems.append(s)
        return s

    sb = lambda name, shape, dt=F32: es.enter_context(nc.sbuf_tensor(name, shape, dt))

    def _icount():
        n = 0
        for _ in nc.all_instructions():
            n += 1
        return n

    def dmaN(eng, s, out, in_):
        before = _icount()
        ins = eng.dma_start(out=out, in_=in_)
        ins.then_inc(s.h, 16)
        s.n += 16 * (_icount() - before)
        return ins
    dr = lambda name, shape, dt=F32, kind="ExternalInput": nc.dram_tensor(name, shape, dt, kind=kind).ap()
    tap_outs = {}

    def tap(name, shape, dt=F32):
        if name in taps:
            tap_outs[name] = dr("tap_" + name, shape, dt, kind="ExternalOutput")
            return tap_outs[name]
        return None

    x_in = dr("x", [S, D])
    out_hbm = dr("out", [S, D], kind="ExternalOutput")
    ident_in = dr("ident", [128, 128], BF16)
    masks_in = dr("masks", [128, 256], BF16)
    gate_w_in = dr("gate_w", [D, 8])
    ln1g_in = dr("ln1g", [128, D]); ln1b_in = dr("ln1b", [128, D])
    ln2g_in = dr("ln2g", [128, D], BF16); ln2b_in = dr("ln2b", [128, D], BF16)
    ffn_w1_in = dr("ffn_w1", [D, 4096]); ffn_w2_in = dr("ffn_w2", [4096, D])
    ffn_b1_in = dr("ffn_b1", [128, 32]); ffn_b2_in = dr("ffn_b2", [128, D], BF16)
    moe_w1_in = dr("moe_w1", [8, D, 4096]); moe_w2_in = dr("moe_w2", [8, 4096, D])
    moe_b1_in = dr("moe_b1", [8, 128, 32]); moe_b2_in = dr("moe_b2", [8, D])
    wk_in = dr("wk", [D, D]); wv_in = dr("wv", [D, D]); wq_in = dr("wq", [D, D])
    snw_in = dr("snw", [128, 8]); rnw_in = dr("rnw", [128, 8])
    knw_in = dr("knw", [128, D]); qnw_in = dr("qnw", [128, D])
    mw0_in = dr("mw0", [64, 256]); mw1_in = dr("mw1", [256, 64])
    atw_in = dr("atw", [D, 32]); atb_in = dr("atb", [128, 32])
    wqt_in = dr("wqt", [D, D], I8)
    wsc_in = dr("wsc", [128, D]); wqb_in = dr("wqb", [128, D])
    scr_k = dr("scr_k", [H, S, HD], BF16, kind="Internal")
    scr_q = dr("scr_q", [H, S, HD], BF16, kind="Internal")

    with es:
        x_sb = sb("x_sb", [128, NT * D])
        ar1 = sb("ar1", [128, 16384], BF16)
        fm2 = sb("fm2", [128, 16384], BF16)
        ar2 = sb("ar2", [128, 16384], BF16)
        ar3 = sb("ar3", [128, 6144])
        vring = sb("vring", [128, RING * H * 65], BF16)
        attn_sb = sb("attn_sb", [128, 1280], BF16)
        ident = sb("ident_sb", [128, 128], BF16)
        masks = sb("masks_sb", [128, 256], BF16)
        cst = sb("cst", [128, 8])
        stat0 = sb("stat0", [128, NT * 2])
        stat1 = sb("stat1", [128, NT * 2])
        rstd0 = sb("rstd0", [128, NT])
        rstd1 = sb("rstd1", [128, NT])
        srr = sb("srr", [128, NT])
        sqrtv = sb("sqrtv", [128, NT])
        comb = sb("comb", [128, NT * 8])
        negc = sb("negc", [128, NT * H])
        gwf32 = sb("gwf32", [128, NC * 8])
        b1moe = sb("b1moe", [128, 8 * 32])
        b1ffn = sb("b1ffn", [128, 32])
        atheta = sb("atheta", [128, NT * 32], BF16)
        amrep = sb("amrep", [128, 16])
        mw0b = sb("mw0b", [64, 256], BF16)
        mw1b = sb("mw1b", [128, 128], BF16)
        mw1T = sb("mw1T", [64, 256], BF16)
        onesb = sb("onesb", [128, 2], BF16)
        ident32 = sb("ident32_sb", [128, 128])
        small = sb("small", [128, 64])
        bnsc = sb("bnsc", [128, 12])

        t_xn0f = tap("xn0f", [128, NC * S], BF16)
        t_negc = tap("negc", [128, NT * H])
        t_comb = tap("comb", [128, NT * 8])
        t_g = tap("g", [128, NT * D], BF16)
        t_x1f = tap("x1f", [128, 16384], BF16)
        t_xl0 = tap("xl0", [128, NT * D])
        t_vals = tap("vals", [128, 16384], BF16)
        t_xn1f = tap("xn1f", [128, 16384], BF16)
        t_s1 = tap("s1", [128, 16384], BF16)

        LD = sem("LD"); LG = sem("LG")
        VE_PRE = sem("VE_PRE"); SC_SQ = sem("SC_SQ"); VE_RS = sem("VE_RS")
        VE_TILE = sem("VE_TILE"); PE_TR = sem("PE_TR"); VE_EVAC = sem("VE_EVAC")
        PE_GATE = sem("PE_GATE"); VE_LG = sem("VE_LG"); SC_SIG = sem("SC_SIG"); VE_COMB = sem("VE_COMB")
        VE_CBB = sem("VE_CBB"); PE_CT = sem("PE_CT"); VE_CT = sem("VE_CT")
        LW1a = sem("LW1a"); LW1b = sem("LW1b"); LW2 = sem("LW2"); FW2 = sem("FW2")
        PE_H = sem("PE_H"); SC_H = sem("SC_H"); PE_O = sem("PE_O"); VE_O = sem("VE_O")
        SC_SIL = sem("SC_SIL"); PE_XI = sem("PE_XI"); VE_XI = sem("VE_XI")
        PE_S = sem("PE_S"); SC_E = sem("SC_E"); PE_T = sem("PE_T"); VE_P = sem("VE_P")
        PE_A = sem("PE_A"); VE_Y = sem("VE_Y"); VE_L1 = sem("VE_L1"); SC_L1 = sem("SC_L1")
        VE_X1 = sem("VE_X1"); PE_X1 = sem("PE_X1"); VE_XE = sem("VE_XE"); VE_RB = sem("VE_RB")
        VE_EP = sem("VE_EP"); SC_L2 = sem("SC_L2")
        LDD = sem("LDD"); LDG = sem("LDG"); VE_D = sem("VE_D"); SC_D = sem("SC_D"); PE_D = sem("PE_D")
        VE_DE = sem("VE_DE"); SD = sem("SD"); PE_AM = sem("PE_AM"); VE_AM = sem("VE_AM")
        LKH = sem("LKH"); VE_E = sem("VE_E"); PE_E = sem("PE_E"); SC_E2 = sem("SC_E2")
        GP_E = sem("GP_E"); VE_S1 = sem("VE_S1")
        LF = sem("LF"); VE_F = sem("VE_F"); PE_F = sem("PE_F"); SC_F = sem("SC_F")
        ST = sem("ST")

        for _n in ["VE_WF", "VE_K", "SC_AT", "VE_AT", "SDa", "SDb", "LKH2", "LQH2",
                   "VE_E1", "PE_E1", "SC_E1", "GP_E1", "VE_S1b", "LF2", "LF2g",
                   "VE_F1", "PE_F1", "SC_F1", "STa", "STb"]:
            pre_sems[_n] = Sem(nc, es, _n)
            sems.append(pre_sems[_n])
        for s_ in sems:
            nc.sync.sem_clear(s_.h)
        nc.all_engine_barrier()

        PH_pre = nc.sync
        blk = es.enter_context(nc.Block())
        PH = sem("PH")
        nc.sync.sem_clear(PH.h)
        nc.all_engine_barrier()
        ph_state = [0]

        def phase_join():
            ph_state[0] += 5
            tgt = ph_state[0]
            for ename in ("vector", "scalar", "tensor", "gpsimd", "sync"):
                def mk(e, tgt=tgt):
                    e.drain().then_inc(PH.h, 1)
                    e.wait_ge(PH.h, tgt)
                getattr(blk, ename)(mk)

        @blk.vector
        def _(v):
            for _ in range(300):
                v.memset(ar1[:], 0.0)
                v.drain()

        xn0f = ar1
        xn0t = fm2[:, 0:D]
        xn0t_b = fm2[:, D:2 * D]
        xn0t32 = ar3[:, 0:D]
        sqt = ar3[:, D:2 * D]
        lgs = ar3[:, 2048:2176]
        mlg = ar3[:, 2176:2304]
        gst = ar3[:, 2304:2368]
        x32fm = [ar3[:, 2432:3456], ar3[:, 3456:4480]]
        combb = fm2[:, 2 * D:2 * D + 128]
        combT = vring[0:8, 2048:4096]
        b2moeb = vring[0:8, 4096:5120]

        es_psA = ExitStack()
        p_tr = [es_psA.enter_context(nc.psum_tensor("p_tr0", [128, 1024], BF16)),
                es_psA.enter_context(nc.psum_tensor("p_tr1", [128, 1024], BF16))]
        p_gate = es_psA.enter_context(nc.psum_tensor("p_gate", [128, 512]))
        p_t32 = [es_psA.enter_context(nc.psum_tensor("p_t32a", [128, 1024])),
                 es_psA.enter_context(nc.psum_tensor("p_t32b", [128, 1024]))]

        @blk.sync
        def _(sync):
            x3 = x_in.rearrange("(t p) d -> p t d", p=128)
            dmaN(sync, LD, x_sb[:].rearrange("p (t d) -> p t d", d=D), x3)
            dmaN(sync, LD, ident[:], ident_in)
            dmaN(sync, LD, masks[:], masks_in)
            dmaN(sync, LD, b1moe[:].rearrange("p (e c) -> p e c", c=32),
                 moe_b1_in.rearrange("e p c -> p e c"))
            dmaN(sync, LD, b1ffn[:], ffn_b1_in)

        @blk.gpsimd
        def _(gp):
            gw = gate_w_in.rearrange("(c p) e -> p c e", p=128)
            dmaN(gp, LG, gwf32[:].rearrange("p (c e) -> p c e", e=8), gw)
            dmaN(gp, LG, b2moeb.rearrange("e (o n) -> e o n", o=1),
                 moe_b2_in.rearrange("e (o n) -> e o n", o=1))

        @blk.vector
        def _(v):
            v.memset(cst[:, 0:1], LN_EPS)
            v.memset(cst[:, 1:2], 1e-6)
            v.memset(cst[:, 2:3], 1e-8)
            v.memset(cst[:, 3:4], 1e-34)
            v.memset(cst[:, 4:5], 1.0 / 64.0)
            v.memset(cst[:, 5:6], 1.0)
            v.memset(cst[:, 6:7], -0.5)
            v.memset(onesb[:], 1.0)
            v.wait_ge(LD.h, LD.n)
            v.tensor_copy(ident32[:], ident[:])
            v.drain()
            for t in range(NT):
                xt = x_sb[:, t * D:(t + 1) * D]
                v.bn_stats(bnsc[:, 0:6], xt[:, 0:512])
                v.bn_stats(bnsc[:, 6:12], xt[:, 512:1024])
                v.drain()
                v.bn_aggr(stat0[:, 2 * t:2 * t + 2], bnsc[:].rearrange("p (g s) -> p g s", s=12))
                v.drain()
            VE_PRE.inc(v.drain())

        @blk.scalar
        def _(s):
            s.wait_ge(VE_PRE.h, VE_PRE.n)
            ins = s.activation(out=sqrtv[:].rearrange("p (t o) -> p t o", o=1),
                               in_=stat0[:].rearrange("p (t s) -> p t s", s=2)[:, :, 1:2],
                               func=AF.Sqrt, bias=cst[:, 0:1], scale=1.0)
            SC_SQ.inc(ins)

        @blk.vector
        def _(v):
            v.wait_ge(SC_SQ.h, SC_SQ.n)
            VE_RS.inc(v.reciprocal(rstd0[:], sqrtv[:]))

        def comb_pt1(v, t):
            v.wait_ge(PE_GATE.h, t + 1)
            lg = lgs[:, 0:8]; ml = mlg[:, t * 8:(t + 1) * 8]
            m1 = gst[:, 4 * t:4 * t + 1]; m2 = gst[:, 4 * t + 1:4 * t + 2]
            w2c = gst[:, 4 * t + 2:4 * t + 3]
            cm = comb[:, t * 8:(t + 1) * 8]
            v.tensor_copy(lg, p_gate[:, 0:8])
            v.drain()
            ins = v.reduce_max(m1, lg, axis=AX.X)
            VE_LG.inc(ins)
            v.drain()
            v.tensor_scalar(cm, lg, m1, None, op0=ALU.is_ge)
            v.drain()
            v.scalar_tensor_tensor(ml, cm, -1e30, lg, op0=ALU.mult, op1=ALU.add)
            v.drain()
            v.reduce_max(m2, ml, axis=AX.X)
            v.drain()
            ins = v.tensor_sub(w2c, m2, m1)
            VE_LG.inc(ins)
            v.drain()

        @blk.vector
        def _(v):
            v.wait_ge(VE_RS.h, VE_RS.n)
            for t in range(NT):
                if t >= 1:
                    v.wait_ge(PE_TR.h, t)
                xt = x_sb[:, t * D:(t + 1) * D]
                mu = stat0[:, 2 * t:2 * t + 1]
                rs = rstd0[:, t:t + 1]
                dst = xn0t if t % 2 == 0 else xn0t_b
                v.tensor_scalar(dst, xt, mu, rs, op0=ALU.subtract, op1=ALU.mult)
                v.tensor_scalar(xn0t32, xt, mu, rs, op0=ALU.subtract, op1=ALU.mult)
                v.drain()
                v.tensor_mul(sqt, xn0t32, xn0t32)
                v.drain()
                ins = v.tensor_reduce(
                    negc[:, t * H:(t + 1) * H].rearrange("p (h o) -> p h o", o=1),
                    sqt.rearrange("p (h d) -> p h d", d=64),
                    axis=AX.X, op=ALU.add, negate=True)
                VE_TILE.inc(ins)
                if t >= 1:
                    v.wait_ge(PE_TR.h, t)
                    dstf = xn0f[:].rearrange("p (c s) -> p c s", s=S)[:, :, (t - 1) * 128:t * 128]
                    v.tensor_copy(dstf, p_tr[(t - 1) % 2][:].rearrange("p (c j) -> p c j", j=128))
                    if t >= 2:
                        v.wait_ge(PE_GATE.h, t - 1)
                    VE_EVAC.inc(v.tensor_copy(x32fm[(t - 1) % 2], p_t32[(t - 1) % 2][:]))
                if t >= 2:
                    comb_pt1(v, t - 2)
            v.wait_ge(PE_TR.h, NT)
            dstf = xn0f[:].rearrange("p (c s) -> p c s", s=S)[:, :, (NT - 1) * 128:NT * 128]
            v.tensor_copy(dstf, p_tr[(NT - 1) % 2][:].rearrange("p (c j) -> p c j", j=128))
            v.wait_ge(PE_GATE.h, NT - 1)
            VE_EVAC.inc(v.tensor_copy(x32fm[(NT - 1) % 2], p_t32[(NT - 1) % 2][:]))
            comb_pt1(v, NT - 2)
            comb_pt1(v, NT - 1)
            v.wait_ge(VE_TILE.h, NT)
            v.drain()
            ins = v.tensor_scalar(negc[:], negc[:], -10.0, None, op0=ALU.add)
            VE_TILE.inc(ins)
            v.drain()

        @blk.tensor
        def _(t_):
            t_.wait_ge(LG.h, LG.n)

            def gate_mm(t):
                t_.wait_ge(VE_EVAC.h, t + 1)
                if t >= 1:
                    t_.wait_ge(VE_LG.h, 2 * t - 1)
                ins = None
                for c in range(NC):
                    ins = t_.matmul(p_gate[:, 0:8], x32fm[t % 2][:, c * 128:(c + 1) * 128],
                                    gwf32[:, c * 8:(c + 1) * 8], start=(c == 0), stop=(c == NC - 1))
                PE_GATE.inc(ins)

            for t in range(NT):
                t_.wait_ge(VE_TILE.h, t + 1)
                if t >= 2:
                    t_.wait_ge(VE_EVAC.h, t - 1)
                src = xn0t if t % 2 == 0 else xn0t_b
                dstp = p_tr[t % 2]
                ins = None
                for c in range(NC):
                    ins = t_.transpose(dstp[:, c * 128:(c + 1) * 128], src[:, c * 128:(c + 1) * 128], ident[:])
                for c in range(NC):
                    ins = t_.transpose(p_t32[t % 2][:, c * 128:(c + 1) * 128],
                                       xn0t32[:, c * 128:(c + 1) * 128], ident32[:])
                PE_TR.inc(ins)
                if t >= 1:
                    gate_mm(t - 1)
            gate_mm(NT - 1)


        @blk.scalar
        def _(s):
            for t in range(NT):
                s.wait_ge(VE_LG.h, 2 * (t + 1))
                ins = s.activation(out=gst[:, 4 * t + 2:4 * t + 3],
                                   in_=gst[:, 4 * t + 2:4 * t + 3], func=AF.Sigmoid)
                SC_SIG.inc(ins)

        @blk.vector
        def _(v):
            for t in range(NT):
                v.wait_ge(SC_SIG.h, t + 1)
                m2 = gst[:, 4 * t + 1:4 * t + 2]; w2c = gst[:, 4 * t + 2:4 * t + 3]
                w1c = gst[:, 4 * t + 3:4 * t + 4]
                ml = mlg[:, t * 8:(t + 1) * 8]
                cm = comb[:, t * 8:(t + 1) * 8]
                v.tensor_scalar(w1c, w2c, -1.0, 1.0, op0=ALU.mult, op1=ALU.add)
                v.tensor_scalar(ml, ml, m2, None, op0=ALU.is_ge)
                v.drain()
                v.tensor_scalar(cm, cm, w1c, None, op0=ALU.mult)
                v.drain()
                ins = v.scalar_tensor_tensor(cm, ml, w2c, cm, op0=ALU.mult, op1=ALU.add)
                VE_COMB.inc(ins)
                v.drain()

        @blk.vector
        def _(v):
            v.wait_ge(VE_COMB.h, VE_COMB.n)
            VE_CBB.inc(v.tensor_copy(combb, comb[:]))
            v.drain()

        @blk.tensor
        def _(t_):
            t_.wait_ge(VE_CBB.h, VE_CBB.n)
            ins = None
            for t in range(NT):
                dstp = p_tr[t // 8][0:8, (t % 8) * 128:(t % 8 + 1) * 128]
                ins = t_.transpose(dstp, combb[:, t * 8:(t + 1) * 8], ident[:])
            PE_CT.inc(ins)

        @blk.vector
        def _(v):
            v.wait_ge(PE_CT.h, PE_CT.n)
            v.tensor_copy(combT[:, 0:1024], p_tr[0][0:8, :])
            VE_CT.inc(v.tensor_copy(combT[:, 1024:2048], p_tr[1][0:8, :]))
            v.drain()

        final_waits = [VE_COMB, VE_CT, VE_TILE]

        def emit_mlp(groups, n_experts, w1_dram, w2_dram, rhs_fm, hbuf, w2buf, w1buf,
                     act_fn, bias_col_fn, evac_fn, extra_mm2=None, gate_start=None,
                     tail_pe=None, tail_sc=None, tail_ve=None, h_serial=False):
            LW1P = [LW1a, LW1b]
            base = dict(LW1a=LW1a.n, LW1b=LW1b.n, LW2=LW2.n, FW2=FW2.n, PE_H=PE_H.n,
                        SC_H=SC_H.n, PE_O=PE_O.n, VE_O=VE_O.n)
            n_fq = 4
            n_mt = 8

            @blk.gpsimd
            def _(gp):
                mt_i = 0
                fq_i = 0
                for gi, (t0, t1) in enumerate(groups):
                    for e in range(n_experts):
                        for fq in range(n_fq):
                            if fq_i >= 1:
                                gp.wait_ge(FW2.h, base["FW2"] + fq_i)
                            w2s = w2_dram[e] if n_experts > 1 else w2_dram
                            src = w2s[fq * 1024:(fq + 1) * 1024, :].rearrange("(c p) n -> p c n", p=128)
                            dmaN(gp, LW2, w2buf.rearrange("p (c n) -> p c n", n=1024), src)
                            fq_i += 1
                            for mt in range(n_mt):
                                if mt_i >= 2:
                                    gp.wait_ge(PE_H.h, base["PE_H"] + mt_i - 1)
                                w1s = w1_dram[e] if n_experts > 1 else w1_dram
                                m0 = fq * 1024 + mt * 128
                                srcw = w1s[:, m0:m0 + 128].rearrange("(c p) m -> p c m", p=128)
                                par = mt_i % 2
                                dmaN(gp, LW1P[par],
                                     w1buf[:, par * 1024:(par + 1) * 1024].rearrange(
                                         "p (c m) -> p c m", m=128), srcw)
                                mt_i += 1

            @blk.tensor
            def _(t_):
                if gate_start is not None:
                    for s_, n_ in gate_start:
                        t_.wait_ge(s_.h, n_)
                mt_i = 0
                fq_i = 0
                o_i = 0
                for gi, (t0, t1) in enumerate(groups):
                    ntok = (t1 - t0) * 128
                    nspans = [(i * 512, min(512, ntok - i * 512)) for i in range((ntok + 511) // 512)]
                    if tail_pe is not None and gi >= 1:
                        tail_pe(t_, gi - 1)
                    for e in range(n_experts):
                        for fq in range(n_fq):
                            for mt in range(n_mt):
                                par_ = mt_i % 2
                                t_.wait_ge(LW1P[par_].h,
                                           base["LW1a" if par_ == 0 else "LW1b"] + 16 * (mt_i // 2 + 1))
                                if h_serial:
                                    if mt_i >= 1:
                                        t_.wait_ge(SC_H.h, base["SC_H"] + mt_i)
                                elif mt_i >= 2:
                                    t_.wait_ge(SC_H.h, base["SC_H"] + mt_i - 1)
                                par = 0 if h_serial else mt_i % 2
                                w1slice = w1buf[:, (mt_i % 2) * 1024:(mt_i % 2 + 1) * 1024]
                                ins = None
                                for kc in range(NC):
                                    for (soff, slen) in nspans:
                                        ins = t_.matmul(
                                            p_h[par][:, soff:soff + slen],
                                            w1slice[:, kc * 128:(kc + 1) * 128],
                                            rhs_fm[:, kc * S + t0 * 128 + soff:
                                                   kc * S + t0 * 128 + soff + slen],
                                            start=(kc == 0), stop=(kc == NC - 1))
                                PE_H.inc(ins)
                                mt_i += 1
                            t_.wait_ge(LW2.h, base["LW2"] + 16 * (fq_i + 1))
                            t_.wait_ge(SC_H.h, base["SC_H"] + mt_i)
                            for tt in range(t0, t1):
                                if o_i >= 2:
                                    t_.wait_ge(VE_O.h, base["VE_O"] + o_i - 1)
                                par = o_i % 2
                                lt = (tt - t0) * 128
                                ins = None
                                has_extra = (extra_mm2 is not None and e == n_experts - 1
                                             and fq == n_fq - 1)
                                for fc in range(8):
                                    for ns in range(2):
                                        stop = (fc == 7 and not has_extra)
                                        ins = t_.matmul(
                                            p_o[par][:, ns * 512:(ns + 1) * 512],
                                            hbuf[:, fc * 1024 + lt: fc * 1024 + lt + 128],
                                            w2buf[:, fc * 1024 + ns * 512: fc * 1024 + (ns + 1) * 512],
                                            start=(fc == 0), stop=stop)
                                if has_extra:
                                    ins = extra_mm2(t_, tt, par)
                                PE_O.inc(ins)
                                o_i += 1
                            FW2.inc(t_.drain())
                            fq_i += 1
                if tail_pe is not None:
                    tail_pe(t_, len(groups) - 1)

            @blk.scalar
            def _(s):
                mt_i = 0
                for gi, (t0, t1) in enumerate(groups):
                    ntok = (t1 - t0) * 128
                    if tail_sc is not None and gi >= 1:
                        tail_sc(s, gi - 1)
                    for e in range(n_experts):
                        for fq in range(n_fq):
                            for mt in range(n_mt):
                                s.wait_ge(PE_H.h, base["PE_H"] + mt_i + 1)
                                par = 0 if h_serial else mt_i % 2
                                ins = s.activation(out=hbuf[:, mt * 1024:mt * 1024 + ntok],
                                                   in_=p_h[par][:, 0:ntok],
                                                   func=act_fn, bias=bias_col_fn(e, fq, mt), scale=1.0)
                                SC_H.inc(ins)
                                mt_i += 1
                if tail_sc is not None:
                    tail_sc(s, len(groups) - 1)

            @blk.vector
            def _(v):
                o_i = 0
                for gi, (t0, t1) in enumerate(groups):
                    if tail_ve is not None and gi >= 1:
                        tail_ve(v, gi - 1)
                    for e in range(n_experts):
                        for fq in range(n_fq):
                            for tt in range(t0, t1):
                                v.wait_ge(PE_O.h, base["PE_O"] + o_i + 1)
                                par = o_i % 2
                                evac_fn(v, gi, e, fq, tt, p_o[par][:], (e == 0 and fq == 0))
                                VE_O.inc(v.drain())
                                o_i += 1
                if tail_ve is not None:
                    tail_ve(v, len(groups) - 1)
            return base

        if stage >= 2:
            phase_join()
            es_psA.close()
            es_psM = ExitStack()
            p_h = [es_psM.enter_context(nc.psum_tensor("p_h0", [128, 1024])),
                   es_psM.enter_context(nc.psum_tensor("p_h1", [128, 1024]))]
            p_o = [es_psM.enter_context(nc.psum_tensor("p_o0", [128, 1024])),
                   es_psM.enter_context(nc.psum_tensor("p_o1", [128, 1024]))]

            g_moe = ar2
            acc = ar3
            w2buf_m = fm2[:, 0:8192]
            hbuf_m = fm2[:, 8192:16384]
            w1buf_m = vring[:, 0:2048]

            sil_cum = []
            tot = 0
            for (t0, t1) in GROUPS:
                tot += (t1 - t0)
                sil_cum.append(tot)
            sc_sil_base = SC_SIL.n

            def moe_evac(v, gi, e, fq, tt, p_ap, first):
                if first and gi >= 1:
                    v.wait_ge(SC_SIL.h, sc_sil_base + sil_cum[gi - 1])
                t0 = GROUPS[gi][0]
                accs = acc[:, (tt - t0) * 1024:(tt - t0 + 1) * 1024]
                cmcol = comb[:, tt * 8 + e:tt * 8 + e + 1]
                if first:
                    v.tensor_scalar(accs, p_ap, cmcol, None, op0=ALU.mult)
                else:
                    v.scalar_tensor_tensor(accs, p_ap, cmcol, accs, op0=ALU.mult, op1=ALU.add)

            def moe_b2_mm(t_, tt, par):
                ins = None
                for ns in range(2):
                    ins = t_.matmul(p_o[par][:, ns * 512:(ns + 1) * 512],
                                    combT[:, tt * 128:(tt + 1) * 128],
                                    b2moeb[:, ns * 512:(ns + 1) * 512],
                                    start=False, stop=True)
                return ins

            ve_o_base_holder = [0]

            def moe_tail_sc(s, gi):
                t0, t1 = GROUPS[gi]
                s.wait_ge(VE_O.h, ve_o_base_holder[0] + 8 * 4 * sil_cum[gi])
                for tt in range(t0, t1):
                    ins = s.activation(out=g_moe[:, tt * D:(tt + 1) * D],
                                       in_=acc[:, (tt - t0) * 1024:(tt - t0 + 1) * 1024],
                                       func=AF.Silu)
                    SC_SIL.inc(ins)

            ve_o_base_holder[0] = VE_O.n
            moe_base = emit_mlp(GROUPS, 8, moe_w1_in, moe_w2_in, xn0f, hbuf_m, w2buf_m, w1buf_m,
                                AF.Gelu,
                                lambda e, fq, mt: b1moe[:, e * 32 + fq * 8 + mt:e * 32 + fq * 8 + mt + 1],
                                moe_evac, extra_mm2=moe_b2_mm,
                                gate_start=[(LG, LG.n), (VE_CT, VE_CT.n)],
                                tail_sc=moe_tail_sc)

            final_waits.append(SC_SIL)

        if stage >= 3:
            phase_join()
            es_psM.close()
            es_psB = ExitStack()
            p_s = es_psB.enter_context(nc.psum_tensor("p_s", [128, 1024]))
            p_tb = es_psB.enter_context(nc.psum_tensor("p_tb", [128, 1024], BF16))
            p_y = es_psB.enter_context(nc.psum_tensor("p_y", [128, 512]))
            p_x1 = es_psB.enter_context(nc.psum_tensor("p_x1", [128, 1024]))

            x1f = fm2
            ybuf = ar3[:, 0:D]
            ln1g_sb = ar3[:, D:2 * D]
            ln1b_sb = ar3[:, 2 * D:3 * D]
            vtmp = ar3[:, 3 * D:4 * D]
            p_sb = attn_sb[:, 0:640]
            pt_sb = attn_sb[:, 640:1280]

            def win(qt):
                j0 = max(0, qt - 2)
                j1 = min(NT - 1, qt + 2)
                return j0, j1

            @blk.sync
            def _(sync):
                dmaN(sync, LD, ln1g_sb, ln1g_in)
                dmaN(sync, LD, ln1b_sb, ln1b_in)
            ld_b = LD.n

            @blk.vector
            def _(v):
                v.wait_ge(LD.h, ld_b)
                v.memset(vring[:, 0:RING * H * 65], 1.0)
                v.tensor_copy(ident32[:], ident[:])
                v.drain()
                def build_vtile(j):
                    slot = j % RING
                    v.tensor_scalar(vtmp, x_sb[:, j * D:(j + 1) * D],
                                    stat0[:, 2 * j:2 * j + 1], rstd0[:, j:j + 1],
                                    op0=ALU.subtract, op1=ALU.mult)
                    v.drain()
                    ins = v.tensor_copy(
                        vring[:, slot * H * 65:(slot + 1) * H * 65].rearrange(
                            "p (h e) -> p h e", e=65)[:, :, 0:64],
                        vtmp.rearrange("p (h d) -> p h d", d=64))
                    VE_RB.inc(ins)
                    v.drain()
                for j in range(3):
                    build_vtile(j)
                for qt in range(NT):
                    j0, j1 = win(qt)
                    nw = j1 - j0 + 1
                    v.tensor_scalar(ybuf, x_sb[:, qt * D:(qt + 1) * D],
                                    stat0[:, 2 * qt:2 * qt + 1], rstd0[:, qt:qt + 1],
                                    op0=ALU.subtract, op1=ALU.mult)
                    VE_Y.inc(v.drain())
                    j = qt + 3
                    if j < NT:
                        if qt >= 3:
                            v.wait_ge(PE_A.h, qt * H)
                        build_vtile(j)
                    for h in range(H):
                        i = qt * H + h
                        v.wait_ge(PE_T.h, i + 1)
                        for jw in range(nw):
                            j_ = j0 + jw
                            srcp = p_tb[:, jw * 128:(jw + 1) * 128]
                            dstp = pt_sb[:, jw * 128:(jw + 1) * 128]
                            if j_ == qt - 2:
                                v.tensor_mul(dstp, srcp, masks[:, 0:128])
                            elif j_ == qt + 2:
                                v.tensor_mul(dstp, srcp, masks[:, 128:256])
                            else:
                                v.tensor_copy(dstp, srcp)
                        VE_P.inc(v.drain())
                        v.wait_ge(PE_A.h, i + 1)
                        iv = small[:, 0:1]
                        v.reciprocal(iv, p_y[:, 64:65])
                        v.drain()
                        ins = v.scalar_tensor_tensor(ybuf[:, h * 64:(h + 1) * 64], p_y[:, 0:64],
                                                     iv, ybuf[:, h * 64:(h + 1) * 64],
                                                     op0=ALU.mult, op1=ALU.add)
                        VE_Y.inc(ins)
                        v.drain()
                    v.bn_stats(bnsc[:, 0:6], ybuf[:, 0:512])
                    v.bn_stats(bnsc[:, 6:12], ybuf[:, 512:1024])
                    v.drain()
                    v.bn_aggr(small[:, 1:3], bnsc[:].rearrange("p (g s) -> p g s", s=12))
                    VE_L1.inc(v.drain())
                    v.wait_ge(SC_L1.h, qt + 1)
                    v.reciprocal(small[:, 4:5], small[:, 3:4])
                    v.drain()
                    v.tensor_scalar(ybuf, ybuf, small[:, 1:2], small[:, 4:5],
                                    op0=ALU.subtract, op1=ALU.mult)
                    v.drain()
                    v.tensor_mul(ybuf, ybuf, ln1g_sb)
                    v.drain()
                    v.tensor_add(ybuf, ybuf, ln1b_sb)
                    VE_X1.inc(v.drain())
                    v.wait_ge(PE_X1.h, qt + 1)
                    dstf = x1f[:].rearrange("p (c s) -> p c s", s=S)[:, :, qt * 128:(qt + 1) * 128]
                    ins = v.tensor_copy(dstf, p_x1[:].rearrange("p (c j) -> p c j", j=128))
                    VE_XE.inc(ins)
                    v.drain()

            @blk.scalar
            def _(s):
                for qt in range(NT):
                    j0, j1 = win(qt)
                    nw = j1 - j0 + 1
                    for h in range(H):
                        i = qt * H + h
                        s.wait_ge(PE_S.h, i + 1)
                        if i >= 1:
                            s.wait_ge(PE_T.h, i)
                        ins = s.activation(out=p_sb[:, 0:nw * 128], in_=p_s[:, 0:nw * 128],
                                           func=AF.Exp, bias=negc[:, qt * H + h:qt * H + h + 1],
                                           scale=1.0)
                        SC_E.inc(ins)
                    s.wait_ge(VE_L1.h, qt + 1)
                    ins = s.activation(out=small[:, 3:4], in_=small[:, 2:3],
                                       func=AF.Sqrt, bias=cst[:, 0:1], scale=1.0)
                    SC_L1.inc(ins)

            @blk.tensor
            def _(t_):
                for qt in range(NT):
                    j0, j1 = win(qt)
                    nw = j1 - j0 + 1
                    hc_of = lambda h: (h // 2) * S
                    for h in range(H):
                        i = qt * H + h
                        bp = (h % 2) * 64
                        if i >= 1:
                            t_.wait_ge(SC_E.h, i)
                        q_ap = xn0f[bp:bp + 64, hc_of(h) + qt * 128: hc_of(h) + (qt + 1) * 128]
                        ins = None
                        ncols = nw * 128
                        for soff in range(0, ncols, 512):
                            slen = min(512, ncols - soff)
                            k_ap = xn0f[bp:bp + 64,
                                        hc_of(h) + j0 * 128 + soff: hc_of(h) + j0 * 128 + soff + slen]
                            ins = t_.matmul(p_s[:, soff:soff + slen], q_ap, k_ap,
                                            start=True, stop=True)
                        PE_S.inc(ins)
                        t_.wait_ge(SC_E.h, i + 1)
                        if i >= 1:
                            t_.wait_ge(VE_P.h, i)
                        ins = None
                        for jw in range(nw):
                            ins = t_.transpose(p_tb[:, jw * 128:(jw + 1) * 128],
                                               p_sb[:, jw * 128:(jw + 1) * 128], ident[:])
                        PE_T.inc(ins)
                        t_.wait_ge(VE_P.h, i + 1)
                        t_.wait_ge(VE_RB.h, min(qt + 3, NT))
                        t_.wait_ge(VE_Y.h, (qt + 1) + i)
                        ins = None
                        for jw in range(nw):
                            j_ = j0 + jw
                            slot = j_ % RING
                            ins = t_.matmul(p_y[:, 0:65],
                                            pt_sb[:, jw * 128:(jw + 1) * 128],
                                            vring[:, slot * H * 65 + h * 65: slot * H * 65 + (h + 1) * 65],
                                            start=(jw == 0), stop=(jw == nw - 1))
                        PE_A.inc(ins)
                    t_.wait_ge(VE_X1.h, qt + 1)
                    if qt >= 1:
                        t_.wait_ge(VE_XE.h, qt)
                    ins = None
                    for c in range(NC):
                        ins = t_.transpose(p_x1[:, c * 128:(c + 1) * 128],
                                           ybuf[:, c * 128:(c + 1) * 128], ident32[:])
                    PE_X1.inc(ins)

            final_waits += [VE_XE]

        if stage >= 4:
            phase_join()
            es_psB.close()
            es_psC = ExitStack()
            p_h0c = es_psC.enter_context(nc.psum_tensor("pc_h0", [128, 1024]))
            p_h = [p_h0c, p_h0c]
            p_o = [es_psC.enter_context(nc.psum_tensor("pc_o0", [128, 1024])),
                   es_psC.enter_context(nc.psum_tensor("pc_o1", [128, 1024]))]
            p_xb = es_psC.enter_context(nc.psum_tensor("p_xb", [128, 1024], BF16))

            acc = ar3
            w2buf_c = ar1[:, 0:8192]
            hbuf_c = ar1[:, 8192:16384]
            w1buf_c = vring[:, 0:2048]
            ln2g_sb = vring[:, 2048:3072]
            ln2b_sb = vring[:, 3072:4096]

            @blk.sync
            def _(sync):
                dmaN(sync, LD, ln2g_sb, ln2g_in)
                dmaN(sync, LD, ln2b_sb, ln2b_in)
                dmaN(sync, LD, vring[:, 4096:5120], ffn_b2_in)
            ld_c = LD.n
            b2f_sb = vring[:, 4096:5120]

            ep_cum = []
            tot = 0
            for (t0, t1) in GROUPS:
                tot += (t1 - t0)
                ep_cum.append(tot)
            ve_ep_base = VE_EP.n
            pe_xi_base = PE_XI.n
            ve_o_base_c = VE_O.n
            ve_l1_base = VE_L1.n

            def ffn_evac(v, gi, e, fq, tt, p_ap, first):
                if first and gi >= 1 and tt == GROUPS[gi][0]:
                    v.wait_ge(VE_EP.h, ve_ep_base + ep_cum[gi - 1])
                t0 = GROUPS[gi][0]
                accs = acc[:, (tt - t0) * 1024:(tt - t0 + 1) * 1024]
                if first:
                    v.tensor_copy(accs, p_ap)
                else:
                    v.tensor_add(accs, accs, p_ap)

            ffn_veo_cum = []
            tot = 0
            for (t0, t1) in GROUPS:
                tot += 4 * (t1 - t0)
                ffn_veo_cum.append(tot)

            def ffn_tail_pe(t_, gi):
                t0, t1 = GROUPS[gi]
                for tt in range(t0, t1):
                    k = (sum(b - a for a, b in GROUPS[:gi]) + (tt - t0))
                    t_.wait_ge(VE_O.h, ve_o_base_c + ffn_veo_cum[gi])
                    if k >= 1:
                        t_.wait_ge(VE_XI.h, k)
                    ins = None
                    for c in range(NC):
                        ins = t_.transpose(p_xb[:, c * 128:(c + 1) * 128],
                                           x1f[:, c * S + tt * 128: c * S + (tt + 1) * 128],
                                           ident[:])
                    PE_XI.inc(ins)

            def ffn_tail_sc(s, gi):
                t0, t1 = GROUPS[gi]
                for tt in range(t0, t1):
                    k = (sum(b - a for a, b in GROUPS[:gi]) + (tt - t0))
                    s.wait_ge(VE_L1.h, ve_l1_base + k + 1)
                    ins = s.activation(out=small[:, 8:9], in_=small[:, 7:8],
                                       func=AF.Sqrt, bias=cst[:, 0:1], scale=1.0)
                    SC_L2.inc(ins)

            def ffn_tail_ve(v, gi):
                t0, t1 = GROUPS[gi]
                for tt in range(t0, t1):
                    k = (sum(b - a for a, b in GROUPS[:gi]) + (tt - t0))
                    accs = acc[:, (tt - t0) * 1024:(tt - t0 + 1) * 1024]
                    v.wait_ge(PE_XI.h, pe_xi_base + k + 1)
                    v.tensor_add(accs, accs, p_xb[:])
                    VE_XI.inc(v.drain())
                    v.tensor_add(accs, accs, b2f_sb)
                    v.drain()
                    v.bn_stats(bnsc[:, 0:6], accs[:, 0:512])
                    v.bn_stats(bnsc[:, 6:12], accs[:, 512:1024])
                    v.drain()
                    v.bn_aggr(small[:, 6:8], bnsc[:].rearrange("p (g s) -> p g s", s=12))
                    VE_L1.inc(v.drain())
                    v.wait_ge(SC_L2.h, k + 1)
                    v.reciprocal(small[:, 9:10], small[:, 8:9])
                    v.drain()
                    v.tensor_scalar(accs, accs, small[:, 6:7], small[:, 9:10],
                                    op0=ALU.subtract, op1=ALU.mult)
                    v.drain()
                    v.tensor_mul(accs, accs, ln2g_sb)
                    v.drain()
                    v.tensor_add(accs, accs, ln2b_sb)
                    v.drain()
                    v.tensor_mul(accs, accs, ar2[:, tt * D:(tt + 1) * D])
                    v.drain()
                    ins = v.tensor_add(x_sb[:, tt * D:(tt + 1) * D],
                                       x_sb[:, tt * D:(tt + 1) * D], accs)
                    VE_EP.inc(ins)
                    v.drain()

            ffn_base = emit_mlp(GROUPS, 1, ffn_w1_in, ffn_w2_in, x1f, hbuf_c, w2buf_c, w1buf_c,
                                AF.Relu, lambda e, fq, mt: b1ffn[:, fq * 8 + mt:fq * 8 + mt + 1],
                                ffn_evac, gate_start=[(LD, ld_c)],
                                tail_pe=ffn_tail_pe, tail_sc=ffn_tail_sc, tail_ve=ffn_tail_ve,
                                h_serial=True)

            final_waits += [VE_EP]

        if stage >= 5:
            phase_join()
            es_psC.close()
            es_psD = ExitStack()
            p_d = [es_psD.enter_context(nc.psum_tensor("p_d0", [128, 1024])),
                   es_psD.enter_context(nc.psum_tensor("p_d1", [128, 1024]))]
            p_dt = [es_psD.enter_context(nc.psum_tensor("p_dt0", [128, 1024], BF16)),
                    es_psD.enter_context(nc.psum_tensor("p_dt1", [128, 1024], BF16))]
            p_am = es_psD.enter_context(nc.psum_tensor("p_am", [128, 512]))

            VE_WF = sem("VE_WF"); VE_K = sem("VE_K"); SC_AT = sem("SC_AT"); VE_AT = sem("VE_AT")
            SDP = [sem("SDa"), sem("SDb")]
            sd_counts_g = [[], []]
            nc2_clear = [VE_WF, VE_K, SC_AT, VE_AT]

            xn1nF = fm2
            wbD = ar1[:, 0:8192]
            atwb = vring[:, 0:256]
            xn1nt = [vring[:, 256:1280], vring[:, 1280:2304]]
            ktile_b = [vring[:, 2304:3328], vring[:, 3328:4352]]
            onesrow = vring[0:1, 4352:4480]
            amrow = vring[0:1, 4480:4496]
            keysc = ar3[:, 0:1024]
            ksq = ar3[:, 1024:2048]
            knw_sb = ar3[:, 2048:3072]
            qnw_sb = ar3[:, 3072:4096]
            at32 = ar3[:, 4096:4128]
            atb_sb = b1ffn
            snw_c = lambda c: small[:, 32 + c:33 + c]
            rnw_c = lambda c: small[:, 40 + c:41 + c]

            @blk.sync
            def _(sync):
                dmaN(sync, LDD, atb_sb[:], atb_in)
                dmaN(sync, LDD, small[:, 32:40], snw_in)
                dmaN(sync, LDD, small[:, 40:48], rnw_in)
                dmaN(sync, LDD, knw_sb, knw_in)
                dmaN(sync, LDD, qnw_sb, qnw_in)
            ldd_n = LDD.n

            @blk.gpsimd
            def _(gp):
                wkc = wk_in.rearrange("(c p) n -> p c n", p=128)
                dmaN(gp, LDG, wbD.rearrange("p (c n) -> p c n", n=1024), wkc)
            ldg_wk = LDG.n

            @blk.vector
            def _(v):
                for t in range(NT):
                    xt = x_sb[:, t * D:(t + 1) * D]
                    v.bn_stats(bnsc[:, 0:6], xt[:, 0:512])
                    v.bn_stats(bnsc[:, 6:12], xt[:, 512:1024])
                    v.drain()
                    v.bn_aggr(stat1[:, 2 * t:2 * t + 2], bnsc[:].rearrange("p (g s) -> p g s", s=12))
                    v.drain()
                VE_D.inc(v.drain())

            @blk.scalar
            def _(s):
                s.wait_ge(VE_D.h, VE_D.n)
                ins = s.activation(out=sqrtv[:].rearrange("p (t o) -> p t o", o=1),
                                   in_=stat1[:].rearrange("p (t s) -> p t s", s=2)[:, :, 1:2],
                                   func=AF.Sqrt, bias=cst[:, 0:1], scale=1.0)
                SC_D.inc(ins)

            @blk.vector
            def _(v):
                v.wait_ge(SC_D.h, SC_D.n)
                v.reciprocal(rstd1[:], sqrtv[:])
                v.drain()
                varS = stat1[:].rearrange("p (t s) -> p t s", s=2)[:, :, 1:2]
                v.tensor_tensor(small[:, 48:64].rearrange("p (t o) -> p t o", o=1), varS,
                                rstd1[:].rearrange("p (t o) -> p t o", o=1), op=ALU.mult)
                v.drain()
                v.tensor_mul(small[:, 48:64], small[:, 48:64], rstd1[:])
                VE_D.inc(v.drain())

            @blk.scalar
            def _(s):
                s.wait_ge(VE_D.h, VE_D.n)
                ins = s.activation(out=small[:, 0:16], in_=small[:, 48:64],
                                   func=AF.Sqrt, bias=cst[:, 1:2], scale=1.0)
                SC_D.inc(ins)

            @blk.vector
            def _(v):
                v.wait_ge(SC_D.h, SC_D.n)
                v.wait_ge(LDD.h, ldd_n)
                v.reciprocal(small[:, 16:32], small[:, 0:16])
                v.drain()
                v.tensor_mul(srr[:], small[:, 16:32], rstd1[:])
                v.drain()
                v.wait_ge(LDG.h, ldg_wk)
                for c in range(NC):
                    v.tensor_scalar(wbD[:, c * 1024:(c + 1) * 1024], wbD[:, c * 1024:(c + 1) * 1024],
                                    snw_c(c), None, op0=ALU.mult)
                VE_WF.inc(v.drain())
                for t in range(NT):
                    if t >= 2:
                        v.wait_ge(PE_D.h, t - 1)
                    v.tensor_scalar(xn1nt[t % 2], x_sb[:, t * D:(t + 1) * D],
                                    stat1[:, 2 * t:2 * t + 1], srr[:, t:t + 1],
                                    op0=ALU.subtract, op1=ALU.mult)
                    VE_D.inc(v.drain())
                    if t >= 1:
                        v.wait_ge(PE_D.h, t)
                        dstf = xn1nF[:].rearrange("p (c s) -> p c s", s=S)[:, :, (t - 1) * 128:t * 128]
                        VE_DE.inc(v.tensor_copy(dstf, p_dt[(t - 1) % 2][:].rearrange("p (c j) -> p c j", j=128)))
                v.wait_ge(PE_D.h, NT)
                dstf = xn1nF[:].rearrange("p (c s) -> p c s", s=S)[:, :, (NT - 1) * 128:NT * 128]
                VE_DE.inc(v.tensor_copy(dstf, p_dt[(NT - 1) % 2][:].rearrange("p (c j) -> p c j", j=128)))

            ve_d_base = VE_D.n - NT

            @blk.tensor
            def _(t_):
                for t in range(NT):
                    t_.wait_ge(VE_D.h, ve_d_base + t + 1)
                    if t >= 2:
                        t_.wait_ge(VE_DE.h, t - 1)
                    ins = None
                    for c in range(NC):
                        ins = t_.transpose(p_dt[t % 2][:, c * 128:(c + 1) * 128],
                                           xn1nt[t % 2][:, c * 128:(c + 1) * 128], ident[:])
                    PE_D.inc(ins)

            ve_de_all = VE_DE.n

            def emit_proj(kind):
                pe_base = PE_D.n
                sc_base = SC_D.n
                ve_base = VE_K.n
                sd_counts = sd_counts_g
                for t in range(NT):
                    @blk.tensor
                    def _(t_, t=t, pe_base=pe_base):
                        if t == 0:
                            t_.wait_ge(VE_DE.h, ve_de_all)
                            t_.wait_ge(VE_WF.h, VE_WF.n)
                        if t >= 2:
                            t_.wait_ge(VE_K.h, ve_base + (t - 1 if kind == 1 else 2 * t - 3))
                        ins = None
                        for c in range(NC):
                            for ns in range(2):
                                ins = t_.matmul(p_d[t % 2][:, ns * 512:(ns + 1) * 512],
                                                xn1nF[:, c * S + t * 128:c * S + (t + 1) * 128],
                                                wbD[:, c * 1024 + ns * 512:c * 1024 + (ns + 1) * 512],
                                                start=(c == 0), stop=(c == NC - 1))
                        PE_D.inc(ins)

                    @blk.vector
                    def _(v, t=t, pe_base=pe_base, ve_base=ve_base, sc_base=sc_base):
                        v.wait_ge(PE_D.h, pe_base + t + 1)
                        if kind != 1 and t >= 1:
                            v.wait_ge(SC_D.h, sc_base + t)
                        if kind == 1:
                            ins = v.tensor_copy(ar2[:, t * 1024:(t + 1) * 1024], p_d[t % 2][:])
                            VE_K.inc(ins)
                            v.drain()
                            return
                        v.tensor_copy(keysc, p_d[t % 2][:])
                        v.drain()
                        v.tensor_mul(ksq, keysc, keysc)
                        v.drain()
                        v.tensor_reduce(small[:, 0:16].rearrange("p (h o) -> p h o", o=1),
                                        ksq.rearrange("p (h d) -> p h d", d=64),
                                        axis=AX.X, op=ALU.add)
                        VE_K.inc(v.drain())

                    if kind != 1:
                        @blk.scalar
                        def _(s, t=t, ve_base=ve_base, sc_base=sc_base):
                            s.wait_ge(VE_K.h, ve_base + 2 * t + 1)
                            ins = s.activation(out=small[:, 16:32], in_=small[:, 0:16],
                                               func=AF.Sqrt, bias=cst[:, 1:2], scale=cst[:, 4:5])
                            SC_D.inc(ins)

                        @blk.vector
                        def _(v, t=t, sc_base=sc_base):
                            v.wait_ge(SC_D.h, sc_base + t + 1)
                            v.reciprocal(small[:, 0:16], small[:, 16:32])
                            v.drain()
                            nwsb = knw_sb if kind == 0 else qnw_sb
                            v.tensor_mul(keysc, keysc, nwsb)
                            v.drain()
                            if sd_counts[t % 2]:
                                v.wait_ge(SDP[t % 2].h, sd_counts[t % 2][-1])
                            ins = v.tensor_tensor(
                                ktile_b[t % 2].rearrange("p (h d) -> p h d", d=64), 
                                keysc.rearrange("p (h d) -> p h d", d=64),
                                small[:, 0:16].rearrange("p (h o) -> p h o", o=1).to_broadcast([128, 16, 64]),
                                op=ALU.mult)
                            VE_K.inc(ins)
                            v.drain()

                        @blk.sync
                        def _(sync, t=t, ve_base=ve_base):
                            sync.wait_ge(VE_K.h, ve_base + 2 * (t + 1))
                            scr = scr_k if kind == 0 else scr_q
                            dst = scr.rearrange("h (t p) d -> t p h d", p=128)[t]
                            dmaN(sync, SDP[t % 2], dst,
                                 ktile_b[t % 2].rearrange("p (h d) -> p h d", d=64))
                            sd_counts[t % 2].append(SDP[t % 2].n)

            emit_proj(0)

            @blk.gpsimd
            def _(gp):
                gp.wait_ge(PE_D.h, PE_D.n)
                wvc = wv_in.rearrange("(c p) n -> p c n", p=128)
                dmaN(gp, LDG, wbD.rearrange("p (c n) -> p c n", n=1024), wvc)

            @blk.vector
            def _(v):
                v.wait_ge(LDG.h, LDG.n)
                for c in range(NC):
                    v.tensor_scalar(wbD[:, c * 1024:(c + 1) * 1024], wbD[:, c * 1024:(c + 1) * 1024],
                                    snw_c(c), None, op0=ALU.mult)
                VE_WF.inc(v.drain())

            @blk.tensor
            def _(t_):
                t_.wait_ge(VE_WF.h, VE_WF.n)
            emit_proj(1)

            @blk.gpsimd
            def _(gp):
                gp.wait_ge(PE_D.h, PE_D.n)
                wqc = wq_in.rearrange("(c p) n -> p c n", p=128)
                dmaN(gp, LDG, wbD.rearrange("p (c n) -> p c n", n=1024), wqc)

            @blk.vector
            def _(v):
                v.wait_ge(LDG.h, LDG.n)
                for c in range(NC):
                    v.tensor_scalar(wbD[:, c * 1024:(c + 1) * 1024], wbD[:, c * 1024:(c + 1) * 1024],
                                    rnw_c(c), None, op0=ALU.mult)
                VE_WF.inc(v.drain())

            @blk.tensor
            def _(t_):
                t_.wait_ge(VE_WF.h, VE_WF.n)
            emit_proj(2)

            @blk.gpsimd
            def _(gp):
                gp.wait_ge(PE_D.h, PE_D.n)
                atc = atw_in.rearrange("(c p) n -> p c n", p=128)
                dmaN(gp, LDG, atwb.rearrange("p (c n) -> p c n", n=32), atc)
                mw0c = mw0_in.rearrange("p n -> p n")
                dmaN(gp, LDG, mw0b[:], mw0c)
                dmaN(gp, LDG, mw1b[:].rearrange("p (k n) -> p k n", n=64),
                     mw1_in.rearrange("(k p) n -> p k n", p=128))

            @blk.vector
            def _(v):
                v.wait_ge(LDG.h, LDG.n)
                for c in range(NC):
                    v.tensor_scalar(atwb[:, c * 32:(c + 1) * 32], atwb[:, c * 32:(c + 1) * 32],
                                    snw_c(c), None, op0=ALU.mult)
                VE_WF.inc(v.drain())

            pe_at_base = PE_D.n
            sc_at_base = SC_AT.n

            @blk.tensor
            def _(t_):
                t_.wait_ge(VE_WF.h, VE_WF.n)
                for t in range(NT):
                    if t >= 1:
                        t_.wait_ge(VE_AT.h, t)
                    ins = None
                    for c in range(NC):
                        ins = t_.matmul(p_am[:, 0:32],
                                        xn1nF[:, c * S + t * 128:c * S + (t + 1) * 128],
                                        atwb[:, c * 32:(c + 1) * 32],
                                        start=(c == 0), stop=(c == NC - 1))
                    PE_D.inc(ins)

            @blk.vector
            def _(v):
                for t in range(NT):
                    v.wait_ge(PE_D.h, pe_at_base + t + 1)
                    if t >= 1:
                        v.wait_ge(SC_AT.h, sc_at_base + t)
                    v.tensor_add(at32, p_am[:, 0:32], atb_sb[:])
                    VE_AT.inc(v.drain())

            @blk.scalar
            def _(s):
                for t in range(NT):
                    s.wait_ge(VE_AT.h, t + 1)
                    ins = s.activation(out=atheta[:, t * 32:(t + 1) * 32], in_=at32,
                                       func=AF.Sigmoid)
                    SC_AT.inc(ins)


            @blk.vector
            def _(v):
                v.wait_ge(SC_AT.h, SC_AT.n)
                ins = v.tensor_scalar(
                    atheta[:].rearrange("p (t g) -> p t g", g=32)[:, :, 16:32],
                    atheta[:].rearrange("p (t g) -> p t g", g=32)[:, :, 16:32],
                    0.02, None, op0=ALU.mult)
                VE_AT.inc(ins)
                v.drain()
                v.memset(onesrow, 1.0)
                v.drain()
                VE_AT.inc(v.drain())

            @blk.tensor
            def _(t_):
                t_.wait_ge(VE_AT.h, VE_AT.n)
                ins = None
                for t in range(NT):
                    ins = t_.matmul(p_am[0:16, 32:33], atheta[:, t * 32:t * 32 + 16],
                                    onesb[:, 0:1], start=(t == 0), stop=(t == NT - 1))
                PE_AM.inc(ins)

            @blk.vector
            def _(v):
                v.wait_ge(PE_AM.h, PE_AM.n)
                v.tensor_scalar(small[0:16, 0:1], p_am[0:16, 32:33], -1.0 / 2048.0, 1.0,
                                op0=ALU.mult, op1=ALU.add)
                v.drain()
                ins = v.tensor_copy(vring[0:16, 4496:4497], small[0:16, 0:1])
                VE_AM.inc(ins)
                v.drain()

            @blk.tensor
            def _(t_):
                t_.wait_ge(VE_AM.h, VE_AM.n)
                ins = t_.transpose(p_dt[0][0:1, 0:16], vring[0:16, 4496:4497], ident[0:16, 0:16])
                PE_AM.inc(ins)

            @blk.vector
            def _(v):
                v.wait_ge(PE_AM.h, PE_AM.n)
                ins = v.tensor_copy(amrow, p_dt[0][0:1, 0:16])
                VE_AM.inc(ins)
                v.drain()

            @blk.tensor
            def _(t_):
                t_.wait_ge(VE_AM.h, VE_AM.n)
                ins = t_.matmul(p_am[:, 48:64], onesrow, amrow, start=True, stop=True)
                PE_AM.inc(ins)
                ins = t_.transpose(p_dt[0][0:64, 128:256], mw1b[:, 0:64], ident[:])
                ins = t_.transpose(p_dt[0][0:64, 256:384], mw1b[:, 64:128], ident[:])
                PE_AM.inc(ins)

            @blk.vector
            def _(v):
                v.wait_ge(PE_AM.h, PE_AM.n)
                v.tensor_copy(amrep[:], p_am[:, 48:64])
                v.tensor_copy(mw1T[:, 0:128], p_dt[0][0:64, 128:256])
                ins = v.tensor_copy(mw1T[:, 128:256], p_dt[0][0:64, 256:384])
                VE_AM.inc(ins)
                v.drain()

            final_waits += [VE_AM, SDP[0], SDP[1]]

        if stage >= 6:
            phase_join()
            es_psD.close()
            es_psE = ExitStack()
            p_a = [es_psE.enter_context(nc.psum_tensor("p_a0", [128, 512])),
                   es_psE.enter_context(nc.psum_tensor("p_a1", [128, 512]))]
            p_b = [es_psE.enter_context(nc.psum_tensor("p_b0", [128, 512])),
                   es_psE.enter_context(nc.psum_tensor("p_b1", [128, 512]))]
            p_tre = [es_psE.enter_context(nc.psum_tensor("p_tre0", [128, 1024], BF16)),
                     es_psE.enter_context(nc.psum_tensor("p_tre1", [128, 1024], BF16))]
            p_g0 = es_psE.enter_context(nc.psum_tensor("p_g0", [128, 512]))
            p_g1 = es_psE.enter_context(nc.psum_tensor("p_g1", [128, 512]))

            LKH = sem("LKH2"); LQH = sem("LQH2")
            VE_E1 = sem("VE_E1"); PE_E1 = sem("PE_E1"); SC_E1 = sem("SC_E1")
            GP_E1 = sem("GP_E1"); VE_S1b = sem("VE_S1b")

            s1tm = fm2
            kh = ar1[:, 0:1024]
            qh = ar1[:, 1024:2048]
            khf = ar1[:, 4096:6144]
            h1f = ar1[:, 6144:10240]
            pre0tm = ar1[:, 10240:14336]
            dhf = ar1[:, 14336:16384]
            d_htm = vring[:, 0:1024]
            w0a = vring[0:64, 1024:1280]
            w0n = vring[0:64, 1280:1536]
            w1a = vring[:, 1536:1664]
            w1n = vring[:, 1664:1792]
            d_h0t = [vring[:, 1792:2048], vring[:, 2048:2304]]
            ggt = [vring[:, 2304:2560], vring[:, 2560:2816]]
            e1t = [vring[:, 2816:3072], vring[:, 3072:3328]]
            cfb = vring[0:1, 3328:3329]
            pre1 = ar3[:, 0:1024]
            sq0t = [ar3[:, 1024:1280], ar3[:, 1280:1536]]
            numt = [ar3[:, 1536:1792], ar3[:, 1792:2048]]
            dA = ar3[:, 2048:3072]
            dB = ar3[:, 3072:4096]
            dC = ar3[:, 4096:5120]
            dD = ar3[:, 5120:6144]

            for h in range(H):
                lkh_n0 = LKH.n
                lqh_n0 = LQH.n

                @blk.sync
                def _(sync, h=h):
                    if h == 0:
                        sync.wait_ge(SDP[0].h, SDP[0].n)
                        sync.wait_ge(SDP[1].h, SDP[1].n)
                    if h >= 1:
                        sync.wait_ge(VE_S1b.h, h)
                    dmaN(sync, LKH, kh.rearrange("p (t d) -> p t d", d=64),
                         scr_k[h].rearrange("(t p) d -> p t d", p=128))
                    dmaN(sync, LQH, qh.rearrange("p (t d) -> p t d", d=64),
                         scr_q[h].rearrange("(t p) d -> p t d", p=128))

                pe_t0 = PE_E1.n
                ve_khf0 = VE_E1.n

                @blk.tensor
                def _(t_, h=h, pe_t0=pe_t0, lkh_n=LKH.n, ve_khf0=ve_khf0):
                    t_.wait_ge(LKH.h, lkh_n)
                    if h >= 1:
                        t_.wait_ge(VE_S1b.h, h)
                    for t in range(NT):
                        if t >= 2:
                            t_.wait_ge(VE_E1.h, ve_khf0 + t - 1)
                        ins = t_.transpose(p_tre[t % 2][0:64, 0:128], kh[:, t * 64:(t + 1) * 64],
                                           ident[:])
                        PE_E1.inc(ins)

                @blk.vector
                def _(v, h=h, pe_t0=pe_t0):
                    for t in range(NT):
                        v.wait_ge(PE_E1.h, pe_t0 + t + 1)
                        ins = v.tensor_copy(khf[0:64, t * 128:(t + 1) * 128], p_tre[t % 2][0:64, 0:128])
                        VE_E1.inc(ins)
                        v.drain()

                pe_a0 = PE_E1.n
                ve_khf_all = VE_E1.n

                sc_h1f0 = SC_E1.n

                @blk.tensor
                def _(t_, pe_a0=pe_a0, ve_khf_all=ve_khf_all, sc_h1f0=sc_h1f0):
                    t_.wait_ge(VE_E1.h, ve_khf_all)
                    k = 0
                    for mt in range(2):
                        for ns in range(4):
                            if k >= 2:
                                t_.wait_ge(SC_E1.h, sc_h1f0 + k - 1)
                            ins = t_.matmul(p_a[k % 2][:, 0:512],
                                            mw0b[:, mt * 128:(mt + 1) * 128],
                                            khf[0:64, ns * 512:(ns + 1) * 512],
                                            start=True, stop=True)
                            PE_E1.inc(ins)
                            k += 1

                @blk.scalar
                def _(s, pe_a0=pe_a0):
                    k = 0
                    for mt in range(2):
                        for ns in range(4):
                            s.wait_ge(PE_E1.h, pe_a0 + k + 1)
                            ins = s.activation(out=h1f[:, mt * 2048 + ns * 512: mt * 2048 + (ns + 1) * 512],
                                               in_=p_a[k % 2][:, 0:512], func=AF.Gelu)
                            SC_E1.inc(ins)
                            k += 1

                pe_b0 = PE_E1.n
                sc_h1f_all = SC_E1.n

                ve_p0tm0 = VE_E1.n

                @blk.tensor
                def _(t_, pe_b0=pe_b0, ve_p0tm0=ve_p0tm0):
                    t_.wait_ge(VE_E1.h, ve_p0tm0)
                    for t in range(NT):
                        if t >= 2:
                            t_.wait_ge(VE_E1.h, ve_p0tm0 + t - 1)
                        ins = t_.matmul(p_b[t % 2][:, 0:256], khf[0:64, t * 128:(t + 1) * 128],
                                        mw0b[:], start=True, stop=True)
                        PE_E1.inc(ins)

                @blk.vector
                def _(v, pe_b0=pe_b0):
                    for t in range(NT):
                        v.wait_ge(PE_E1.h, pe_b0 + t + 1)
                        ins = v.tensor_copy(pre0tm[:, t * 256:(t + 1) * 256], p_b[t % 2][:, 0:256])
                        VE_E1.inc(ins)
                        v.drain()

                pe_c0 = PE_E1.n
                ve_p0tm_all = VE_E1.n

                ve_pre10 = VE_E1.n

                @blk.tensor
                def _(t_, pe_c0=pe_c0, sc_h1f_all=sc_h1f_all, ve_pre10=ve_pre10):
                    t_.wait_ge(SC_E1.h, sc_h1f_all)
                    t_.wait_ge(VE_E1.h, ve_pre10)
                    for t in range(NT):
                        if t >= 2:
                            t_.wait_ge(VE_E1.h, ve_pre10 + t - 1)
                        ins = None
                        for kc in range(2):
                            ins = t_.matmul(p_b[t % 2][:, 0:64],
                                            h1f[:, kc * 2048 + t * 128: kc * 2048 + (t + 1) * 128],
                                            mw1b[:, kc * 64:(kc + 1) * 64],
                                            start=(kc == 0), stop=(kc == 1))
                        PE_E1.inc(ins)

                @blk.vector
                def _(v, pe_c0=pe_c0):
                    for t in range(NT):
                        v.wait_ge(PE_E1.h, pe_c0 + t + 1)
                        ins = v.tensor_copy(pre1[:, t * 64:(t + 1) * 64], p_b[t % 2][:, 0:64])
                        VE_E1.inc(ins)
                        v.drain()

                ve_pre1_all = VE_E1.n

                @blk.scalar
                def _(s, h=h, ve_pre1_all=ve_pre1_all):
                    s.wait_ge(VE_E1.h, ve_pre1_all)
                    ins = s.activation(out=h1f[:], in_=pre0tm[:], func=AF.Gelu)
                    SC_E1.inc(ins)
                h1tm = h1f

                sc_h1tm = SC_E1.n

                @blk.vector
                def _(v, h=h, sc_h1tm=sc_h1tm, lqh_n=LQH.n):
                    v.wait_ge(SC_E1.h, sc_h1tm)
                    v.wait_ge(LQH.h, lqh_n)
                    pre1v = pre1.rearrange("p (t d) -> p t d", d=64)
                    v.tensor_mul(dA, pre1[:], pre1[:])
                    v.drain()
                    v.tensor_reduce(small[:, 0:16].rearrange("p (t o) -> p t o", o=1),
                                    dA.rearrange("p (t d) -> p t d", d=64), axis=AX.X, op=ALU.add)
                    VE_E1.inc(v.drain())

                @blk.scalar
                def _(s):
                    s.wait_ge(VE_E1.h, VE_E1.n)
                    ins = s.activation(out=small[:, 16:32], in_=small[:, 0:16],
                                       func=AF.Sqrt, bias=cst[:, 2:3], scale=cst[:, 4:5])
                    SC_E1.inc(ins)

                @blk.vector
                def _(v, h=h):
                    v.wait_ge(SC_E1.h, SC_E1.n)
                    v.reciprocal(small[:, 0:16], small[:, 16:32])
                    v.drain()
                    irb = small[:, 0:16].rearrange("p (t o) -> p t o", o=1).to_broadcast([128, NT, 64])
                    v3 = lambda x: x.rearrange("p (t d) -> p t d", d=64)
                    v.tensor_tensor(v3(dA), v3(pre1[:]), irb, op=ALU.mult)
                    v.drain()
                    v.tensor_add(dB, dA, kh)
                    v.drain()
                    vals_h = ar2[:].rearrange("p (t n) -> p t n", n=1024)[:, :, h * 64:(h + 1) * 64]
                    v.tensor_tensor(v3(dC), v3(dB), vals_h, op=ALU.subtract)
                    v.drain()
                    thb = atheta[:].rearrange("p (t g) -> p t g", g=32)[:, :, 16 + h:17 + h].to_broadcast([128, NT, 64])
                    v.tensor_tensor(v3(dB), v3(dC), thb, op=ALU.mult)
                    v.drain()
                    v.tensor_mul(dC, dB, dA)
                    v.drain()
                    v.tensor_reduce(small[:, 16:32].rearrange("p (t o) -> p t o", o=1),
                                    v3(dC), axis=AX.X, op=ALU.add)
                    v.drain()
                    v.tensor_scalar(small[:, 16:32], small[:, 16:32], 1.0 / 64.0, None, op0=ALU.mult)
                    v.drain()
                    dyb = small[:, 16:32].rearrange("p (t o) -> p t o", o=1).to_broadcast([128, NT, 64])
                    v.tensor_tensor(v3(dC), v3(dA), dyb, op=ALU.mult)
                    v.drain()
                    v.tensor_sub(dC, dB, dC)
                    v.drain()
                    irb2 = small[:, 0:16].rearrange("p (t o) -> p t o", o=1).to_broadcast([128, NT, 64])
                    ins = v.tensor_tensor(v3(d_htm), v3(dC), irb2, op=ALU.mult)
                    VE_E1.inc(ins)
                    v.drain()

                pe_d0 = PE_E1.n
                ve_dh = VE_E1.n

                ve_dhf0 = VE_E1.n

                @blk.tensor
                def _(t_, pe_d0=pe_d0, ve_dh=ve_dh, ve_dhf0=ve_dhf0):
                    t_.wait_ge(VE_E1.h, ve_dh)
                    for t in range(NT):
                        if t >= 2:
                            t_.wait_ge(VE_E1.h, ve_dhf0 + t - 1)
                        ins = t_.transpose(p_tre[t % 2][0:64, 0:128], d_htm[:, t * 64:(t + 1) * 64],
                                           ident[:])
                        PE_E1.inc(ins)

                @blk.vector
                def _(v, pe_d0=pe_d0):
                    for t in range(NT):
                        v.wait_ge(PE_E1.h, pe_d0 + t + 1)
                        ins = v.tensor_copy(dhf[0:64, t * 128:(t + 1) * 128], p_tre[t % 2][0:64, 0:128])
                        VE_E1.inc(ins)
                        v.drain()

                pe_e0 = PE_E1.n
                ve_dhf_all = VE_E1.n
                sc_e0 = SC_E1.n
                gp_e0 = GP_E1.n
                ve_f0 = VE_E1.n

                ve_dh00 = VE_E1.n

                @blk.tensor
                def _(t_, pe_e0=pe_e0, ve_dhf_all=ve_dhf_all, ve_dh00=ve_dh00):
                    t_.wait_ge(VE_E1.h, ve_dhf_all)
                    for t in range(NT + 1):
                        if t < NT:
                            if t >= 2:
                                t_.wait_ge(VE_E1.h, ve_dh00 + 2 * t - 2)
                            ins = t_.matmul(p_b[t % 2][:, 0:256], dhf[0:64, t * 128:(t + 1) * 128],
                                            mw1T[:], start=True, stop=True)
                            PE_E1.inc(ins)
                        if t >= 1:
                            tt = t - 1
                            t_.wait_ge(VE_E1.h, ve_dh00 + 2 * tt + 2)
                            ins = t_.matmul(p_g0[0:64, 0:256], kh[:, tt * 64:(tt + 1) * 64],
                                            d_h0t[tt % 2], start=(tt == 0), stop=(tt == NT - 1))
                            PE_E1.inc(ins)

                rect = [ar3[:, 3072:3328], ar3[:, 3328:3584]]

                @blk.scalar
                def _(s, sc_e0=sc_e0, gp_e0=gp_e0, ve_gg0=ve_dh00):
                    for t in range(NT):
                        if t >= 2:
                            s.wait_ge(GP_E1.h, gp_e0 + 4 * (t - 1))
                            s.wait_ge(VE_E1.h, ve_gg0 + 2 * t - 3)
                        ins = s.activation(out=sq0t[t % 2], in_=pre0tm[:, t * 256:(t + 1) * 256],
                                           func=AF.Square)
                        SC_E1.inc(ins)
                        s.drain()
                        ins = s.activation(out=e1t[t % 2], in_=sq0t[t % 2], func=AF.Exp,
                                           bias=0.0, scale=cst[:, 6:7])
                        SC_E1.inc(ins)
                        s.drain()

                ve_gg0 = VE_E1.n

                @blk.gpsimd
                def _(gp, h=h, sc_e0=sc_e0, ve_gg0=ve_gg0):
                    for t in range(NT):
                        gp.wait_ge(SC_E1.h, sc_e0 + 2 * (t + 1))
                        gp.wait_ge(VE_E1.h, ve_gg0 + 2 * t + 1)
                        if t >= 2:
                            gp.wait_ge(VE_E1.h, ve_gg0 + 2 * t - 2)
                        gp.tensor_tensor(numt[t % 2], h1tm[:, t * 256:(t + 1) * 256],
                                         pre0tm[:, t * 256:(t + 1) * 256], op=ALU.mult)
                        gp.drain()
                        gp.tensor_tensor(numt[t % 2], numt[t % 2], rect[t % 2], op=ALU.mult)
                        gp.drain()
                        gp.tensor_tensor(ggt[t % 2], pre0tm[:, t * 256:(t + 1) * 256], e1t[t % 2],
                                         op=ALU.mult)
                        gp.drain()
                        ins = gp.tensor_tensor(ggt[t % 2], ggt[t % 2], numt[t % 2], op=ALU.add)
                        GP_E1.inc(ins, 4)
                        gp.drain()

                @blk.vector
                def _(v, pe_e0=pe_e0, gp_e0=gp_e0, sc_e0=sc_e0):
                    for t in range(NT):
                        v.wait_ge(SC_E1.h, sc_e0 + 2 * t + 2)
                        if t >= 2:
                            v.wait_ge(GP_E1.h, gp_e0 + 4 * (t - 1))
                        v.tensor_scalar(rect[t % 2], sq0t[t % 2], 1e-34, None, op0=ALU.add)
                        v.drain()
                        v.tensor_scalar(e1t[t % 2], e1t[t % 2], 0.3989422804014327, None,
                                        op0=ALU.mult)
                        v.drain()
                        ins = v.reciprocal(rect[t % 2], rect[t % 2])
                        VE_E1.inc(ins)
                        v.drain()
                        v.wait_ge(PE_E1.h, pe_e0 + max(1, 2 * t))
                        v.wait_ge(GP_E1.h, gp_e0 + 4 * (t + 1))
                        ins = v.tensor_tensor(d_h0t[t % 2], p_b[t % 2][:, 0:256], ggt[t % 2],
                                              op=ALU.mult)
                        VE_E1.inc(ins)
                        v.drain()

                ve_dh0_all = VE_E1.n

                @blk.tensor
                def _(t_, h=h):
                    for mh in range(2):
                        for t in range(NT):
                            ins = t_.matmul(p_g1[:, mh * 64:(mh + 1) * 64],
                                            h1tm[:, t * 256 + mh * 128: t * 256 + (mh + 1) * 128],
                                            d_htm[:, t * 64:(t + 1) * 64],
                                            start=(t == 0), stop=(t == NT - 1))
                        PE_E1.inc(ins)

                pe_gall = PE_E1.n

                @blk.scalar
                def _(s, pe_gall=pe_gall):
                    s.wait_ge(PE_E1.h, pe_gall)
                    ins = s.activation(out=dA[0:64, 0:256], in_=p_g0[0:64, 0:256], func=AF.Square)
                    SC_E1.inc(ins)
                    s.drain()
                    ins = s.activation(out=dA[:, 256:384], in_=p_g1[:, 0:128], func=AF.Square)
                    SC_E1.inc(ins)
                    s.drain()
                sc_clip = SC_E1.n

                @blk.vector
                def _(v, sc_clip=sc_clip):
                    v.wait_ge(SC_E1.h, sc_clip)
                    v.tensor_reduce(small[0:64, 32:33], dA[0:64, 0:256], axis=AX.X, op=ALU.add)
                    v.tensor_reduce(small[:, 33:34], dA[:, 256:384], axis=AX.X, op=ALU.add)
                    VE_E1.inc(v.drain())

                @blk.tensor
                def _(t_):
                    t_.wait_ge(VE_E1.h, VE_E1.n)
                    ins = t_.matmul(p_a[0][0:1, 0:1], small[0:64, 32:33], cst[0:64, 5:6],
                                    start=True, stop=False)
                    ins = t_.matmul(p_a[0][0:1, 0:1], small[:, 33:34], cst[:, 5:6],
                                    start=False, stop=True)
                    PE_E1.inc(ins)

                @blk.scalar
                def _(s):
                    s.wait_ge(PE_E1.h, PE_E1.n)
                    ins = s.activation(out=small[0:1, 34:35], in_=p_a[0][0:1, 0:1], func=AF.Sqrt)
                    SC_E1.inc(ins)

                @blk.vector
                def _(v):
                    v.wait_ge(SC_E1.h, SC_E1.n)
                    v.tensor_scalar(small[0:1, 34:35], small[0:1, 34:35], 1e-6, None, op0=ALU.add)
                    v.drain()
                    v.reciprocal(small[0:1, 35:36], small[0:1, 34:35])
                    v.drain()
                    v.tensor_scalar(small[0:1, 35:36], small[0:1, 35:36], 10.0, None, op0=ALU.mult)
                    v.drain()
                    v.tensor_scalar(small[0:1, 35:36], small[0:1, 35:36], 1.0, None, op0=ALU.min)
                    v.drain()
                    v.tensor_scalar(small[0:1, 35:36], small[0:1, 35:36], -1.0, None, op0=ALU.mult)
                    v.drain()
                    ins = v.tensor_copy(cfb, small[0:1, 35:36])
                    VE_E1.inc(ins)
                    v.drain()

                @blk.tensor
                def _(t_):
                    t_.wait_ge(VE_E1.h, VE_E1.n)
                    ins = t_.matmul(p_a[0][:, 8:9], onesrow, cfb, start=True, stop=True)
                    PE_E1.inc(ins)

                @blk.vector
                def _(v, h=h):
                    v.wait_ge(PE_E1.h, PE_E1.n)
                    v.tensor_copy(small[:, 36:37], p_a[0][:, 8:9])
                    v.drain()
                    v.tensor_scalar(w0a, mw0b[:], amrep[0:64, h:h + 1], None, op0=ALU.mult)
                    v.tensor_scalar(w1a, mw1b[:], amrep[:, h:h + 1], None, op0=ALU.mult)
                    v.drain()
                    v.scalar_tensor_tensor(w0n, p_g0[0:64, 0:256], small[0:64, 36:37], w0a,
                                           op0=ALU.mult, op1=ALU.add)
                    ins = v.scalar_tensor_tensor(w1n, p_g1[:, 0:128], small[:, 36:37], w1a,
                                                 op0=ALU.mult, op1=ALU.add)
                    VE_E1.inc(ins)
                    v.drain()

                pe_q0 = PE_E1.n
                ve_w = VE_E1.n

                ve_qhf0 = VE_E1.n

                @blk.tensor
                def _(t_, pe_q0=pe_q0, ve_w=ve_w, lqh_n=LQH.n, ve_qhf0=ve_qhf0):
                    t_.wait_ge(LQH.h, lqh_n)
                    t_.wait_ge(VE_E1.h, ve_w)
                    for t in range(NT):
                        if t >= 2:
                            t_.wait_ge(VE_E1.h, ve_qhf0 + t - 1)
                        ins = t_.transpose(p_tre[t % 2][0:64, 0:128], qh[:, t * 64:(t + 1) * 64],
                                           ident[:])
                        PE_E1.inc(ins)

                @blk.vector
                def _(v, pe_q0=pe_q0):
                    for t in range(NT):
                        v.wait_ge(PE_E1.h, pe_q0 + t + 1)
                        ins = v.tensor_copy(khf[0:64, t * 128:(t + 1) * 128], p_tre[t % 2][0:64, 0:128])
                        VE_E1.inc(ins)
                        v.drain()
                qhf = khf

                pe_m0 = PE_E1.n
                ve_qhf_all = VE_E1.n

                sc_a0f0 = SC_E1.n

                @blk.tensor
                def _(t_, pe_m0=pe_m0, ve_qhf_all=ve_qhf_all, sc_a0f0=sc_a0f0):
                    t_.wait_ge(VE_E1.h, ve_qhf_all)
                    k = 0
                    for mt in range(2):
                        for ns in range(4):
                            if k >= 2:
                                t_.wait_ge(SC_E1.h, sc_a0f0 + k - 1)
                            ins = t_.matmul(p_a[k % 2][:, 0:512],
                                            w0n[:, mt * 128:(mt + 1) * 128],
                                            qhf[0:64, ns * 512:(ns + 1) * 512],
                                            start=True, stop=True)
                            PE_E1.inc(ins)
                            k += 1

                a0f = pre0tm

                @blk.scalar
                def _(s, pe_m0=pe_m0):
                    k = 0
                    for mt in range(2):
                        for ns in range(4):
                            s.wait_ge(PE_E1.h, pe_m0 + k + 1)
                            ins = s.activation(out=a0f[:, mt * 2048 + ns * 512: mt * 2048 + (ns + 1) * 512],
                                               in_=p_a[k % 2][:, 0:512], func=AF.Gelu)
                            SC_E1.inc(ins)
                            k += 1

                pe_m1 = PE_E1.n
                sc_a0f_all = SC_E1.n
                m1tm = pre1

                ve_m10 = VE_E1.n

                @blk.tensor
                def _(t_, pe_m1=pe_m1, sc_a0f_all=sc_a0f_all, ve_m10=ve_m10):
                    t_.wait_ge(SC_E1.h, sc_a0f_all)
                    t_.wait_ge(VE_E1.h, ve_m10)
                    for t in range(NT):
                        if t >= 2:
                            t_.wait_ge(VE_E1.h, ve_m10 + t - 1)
                        ins = None
                        for kc in range(2):
                            ins = t_.matmul(p_b[t % 2][:, 0:64],
                                            a0f[:, kc * 2048 + t * 128: kc * 2048 + (t + 1) * 128],
                                            w1n[:, kc * 64:(kc + 1) * 64],
                                            start=(kc == 0), stop=(kc == 1))
                        PE_E1.inc(ins)

                @blk.vector
                def _(v, pe_m1=pe_m1):
                    for t in range(NT):
                        v.wait_ge(PE_E1.h, pe_m1 + t + 1)
                        ins = v.tensor_copy(m1tm[:, t * 64:(t + 1) * 64], p_b[t % 2][:, 0:64])
                        VE_E1.inc(ins)
                        v.drain()

                @blk.vector
                def _(v, h=h):
                    v.wait_ge(VE_E1.h, VE_E1.n)
                    v3 = lambda x: x.rearrange("p (t d) -> p t d", d=64)
                    v.tensor_mul(dA, m1tm[:], m1tm[:])
                    v.drain()
                    v.tensor_reduce(small[:, 0:16].rearrange("p (t o) -> p t o", o=1),
                                    v3(dA), axis=AX.X, op=ALU.add)
                    VE_E1.inc(v.drain())

                @blk.scalar
                def _(s):
                    s.wait_ge(VE_E1.h, VE_E1.n)
                    ins = s.activation(out=small[:, 16:32], in_=small[:, 0:16],
                                       func=AF.Sqrt, bias=cst[:, 2:3], scale=cst[:, 4:5])
                    SC_E1.inc(ins)

                @blk.vector
                def _(v, h=h):
                    v.wait_ge(SC_E1.h, SC_E1.n)
                    v.reciprocal(small[:, 0:16], small[:, 16:32])
                    v.drain()
                    v3 = lambda x: x.rearrange("p (t d) -> p t d", d=64)
                    irb = small[:, 0:16].rearrange("p (t o) -> p t o", o=1).to_broadcast([128, NT, 64])
                    v.tensor_tensor(v3(dA), v3(m1tm[:]), irb, op=ALU.mult)
                    v.drain()
                    s1h = s1tm[:].rearrange("p (t n) -> p t n", n=1024)[:, :, h * 64:(h + 1) * 64]
                    ins = v.tensor_tensor(s1h, v3(dA), qh.rearrange("p (t d) -> p t d", d=64),
                                          op=ALU.add)
                    VE_S1b.inc(ins)
                    v.drain()

            final_waits += [VE_S1b]

        if stage >= 7:
            phase_join()
            es_psE.close()
            es_psF = ExitStack()
            p_tf = es_psF.enter_context(nc.psum_tensor("p_tf", [128, 1024], BF16))
            p_q = es_psF.enter_context(nc.psum_tensor("p_q", [128, 1024]))

            LF2 = sem("LF2"); LF2g = sem("LF2g")
            VE_F1 = sem("VE_F1"); PE_F1 = sem("PE_F1"); SC_F1 = sem("SC_F1")
            STP = [sem("STa"), sem("STb")]
            st_counts = [[], []]

            wqtb = ar1[:, 0:8192]
            xn1t_f = ar3[:, 0:1024]
            t1f = ar3[:, 1024:2048]
            xof = [ar3[:, 2048:3072], ar3[:, 3072:4096]]
            wscb = ar3[:, 4096:5120]
            wqbb = ar3[:, 5120:6144]
            xqt = vring[:, 0:1024]
            xqf = vring[:, 1024:2048]
            g1t = vring[:, 2048:3072]
            s1g = vring[:, 3072:4096]

            @blk.sync
            def _(sync):
                dmaN(sync, LF2, wscb, wsc_in)
                dmaN(sync, LF2, wqbb, wqb_in)

            @blk.gpsimd
            def _(gp):
                dmaN(gp, LF2g, wqtb.rearrange("p (c n) -> p c n", n=1024),
                     wqt_in.rearrange("(c p) n -> p c n", p=128))
            lf_n = LF2.n
            lf_ng = LF2g.n

            for t in range(NT):
                @blk.vector
                def _(v, t=t):
                    if t == 0:
                        v.wait_ge(LF2.h, lf_n)
                        v.wait_ge(LF2g.h, lf_ng)
                    if t >= 2 and st_counts[t % 2]:
                        v.wait_ge(STP[t % 2].h, st_counts[t % 2][-1])
                    mu = stat1[:, 2 * t:2 * t + 1]
                    rs = rstd1[:, t:t + 1]
                    v.tensor_scalar(xn1t_f, x_sb[:, t * D:(t + 1) * D], mu, rs,
                                    op0=ALU.subtract, op1=ALU.mult)
                    v.drain()
                    v.tensor_reduce(small[:, 0:1], xn1t_f, axis=AX.X, op=ALU.max,
                                    apply_absolute_value=True)
                    v.drain()
                    v.tensor_scalar(small[:, 1:2], small[:, 0:1], 1e-6, 1.0 / 127.0,
                                    op0=ALU.max, op1=ALU.mult)
                    v.drain()
                    v.reciprocal(small[:, 2:3], small[:, 1:2])
                    v.drain()
                    v.tensor_scalar(xn1t_f, xn1t_f, small[:, 2:3], None, op0=ALU.mult)
                    v.drain()
                    v.tensor_scalar(xn1t_f, xn1t_f, 12582912.0, 12582912.0,
                                    op0=ALU.add, op1=ALU.subtract)
                    v.drain()
                    v.tensor_scalar(xn1t_f, xn1t_f, 127.0, -127.0, op0=ALU.min, op1=ALU.max)
                    v.drain()
                    ins = v.tensor_copy(xqt, xn1t_f)
                    VE_F1.inc(ins)
                    v.drain()

                @blk.tensor
                def _(t_, t=t):
                    t_.wait_ge(VE_F1.h, VE_F1.n)
                    ins = None
                    for c in range(NC):
                        ins = t_.transpose(p_tf[:, c * 128:(c + 1) * 128],
                                           xqt[:, c * 128:(c + 1) * 128], ident[:])
                    PE_F1.inc(ins)

                @blk.vector
                def _(v, t=t):
                    v.wait_ge(PE_F1.h, PE_F1.n)
                    ins = v.tensor_copy(xqf, p_tf[:])
                    VE_F1.inc(ins)
                    v.drain()

                @blk.tensor
                def _(t_, t=t):
                    t_.wait_ge(VE_F1.h, VE_F1.n)
                    ins = None
                    for c in range(NC):
                        for ns in range(2):
                            ins = t_.matmul(p_q[:, ns * 512:(ns + 1) * 512],
                                            xqf[:, c * 128:(c + 1) * 128],
                                            wqtb[:, c * 1024 + ns * 512: c * 1024 + (ns + 1) * 512],
                                            start=(c == 0), stop=(c == NC - 1))
                    PE_F1.inc(ins)

                @blk.vector
                def _(v, t=t):
                    v.wait_ge(PE_F1.h, PE_F1.n)
                    v.tensor_scalar(t1f, p_q[:], small[:, 1:2], None, op0=ALU.mult)
                    v.drain()
                    v.tensor_mul(t1f, t1f, wscb)
                    v.drain()
                    ins = v.tensor_add(t1f, t1f, wqbb)
                    VE_F1.inc(ins)
                    v.drain()

                @blk.scalar
                def _(s, t=t):
                    s.wait_ge(VE_F1.h, VE_F1.n)
                    ins = s.activation(out=g1t, in_=t1f, func=AF.Silu)
                    SC_F1.inc(ins)

                @blk.vector
                def _(v, t=t):
                    v.wait_ge(SC_F1.h, SC_F1.n)
                    v.tensor_mul(s1g, s1tm[:, t * D:(t + 1) * D], g1t)
                    v.drain()
                    ins = v.tensor_add(xof[t % 2], x_sb[:, t * D:(t + 1) * D], s1g)
                    VE_F1.inc(ins)
                    v.drain()

                @blk.sync
                def _(sync, t=t):
                    sync.wait_ge(VE_F1.h, VE_F1.n)
                    dst = out_hbm.rearrange("(t p) d -> t p d", p=128)[t]
                    dmaN(sync, STP[t % 2], dst, xof[t % 2])
                    st_counts[t % 2].append(STP[t % 2].n)

            final_waits += [STP[0], STP[1]]

        @blk.sync
        def _(sync):
            for s_ in final_waits:
                sync.wait_ge(s_.h, s_.n)

            def store(dst, src_ap):
                dmaN(sync, ST, dst, src_ap)
            if t_xn0f is not None:
                store(t_xn0f, xn0f[:])
            if t_negc is not None:
                store(t_negc, negc[:])
            if t_comb is not None:
                store(t_comb, comb[:])
            if t_g is not None and stage >= 2:
                store(t_g, ar2[:, 0:NT * D])
            if t_x1f is not None and stage >= 3:
                store(t_x1f, fm2[:])
            if t_vals is not None and stage >= 5:
                store(t_vals, ar2[:])
            if t_s1 is not None and stage >= 6:
                store(t_s1, fm2[:])
            if stage < 7:
                store(out_hbm.rearrange("(t p) d -> p t d", p=128),
                      x_sb[:].rearrange("p (t d) -> p t d", d=D))
            sync.wait_ge(ST.h, ST.n)

    return nc, tap_outs



def _host_inputs(core, inp):
    bf = ml_dtypes.bfloat16
    f32 = np.float32
    A = lambda x: np.ascontiguousarray(np.asarray(x, f32))
    d = {}
    d["x"] = A(inp["L0_attn_src"][core])
    d["ident"] = np.eye(128, dtype=bf)
    jj = np.arange(128)[:, None]; ii = np.arange(128)[None, :]
    d["masks"] = np.concatenate([(jj >= ii).astype(bf), (jj <= ii).astype(bf)], axis=1)
    d["gate_w"] = A(inp["L0_ffn_gate_w"])
    bcast = lambda vec: np.ascontiguousarray(np.broadcast_to(np.asarray(vec, f32)[None, :], (128, 1024)))
    bcastb = lambda vec: np.ascontiguousarray(np.broadcast_to(
        np.asarray(vec, f32).astype(bf)[None, :], (128, 1024)))
    d["ln1g"] = bcast(inp["L0_attn_ln1_g"]); d["ln1b"] = bcast(inp["L0_attn_ln1_b"])
    d["ln2g"] = bcastb(inp["L0_attn_ln2_g"]); d["ln2b"] = bcastb(inp["L0_attn_ln2_b"])
    d["ffn_w1"] = A(inp["L0_attn_W1"]); d["ffn_w2"] = A(inp["L0_attn_W2"])
    d["ffn_b1"] = np.ascontiguousarray(np.asarray(inp["L0_attn_b1"], f32).reshape(32, 128).T)
    d["ffn_b2"] = bcastb(inp["L0_attn_b2"])
    d["moe_w1"] = A(inp["L0_ffn_w1"]); d["moe_w2"] = A(inp["L0_ffn_w2"])
    d["moe_b1"] = np.ascontiguousarray(np.asarray(inp["L0_ffn_b1"], f32).reshape(8, 32, 128).transpose(0, 2, 1))
    d["moe_b2"] = A(inp["L0_ffn_b2"])
    d["wk"] = A(inp["L1_attn_W_K"]); d["wv"] = A(inp["L1_attn_W_V"]); d["wq"] = A(inp["L1_attn_W_Q"])
    fmcols = lambda vec: np.ascontiguousarray(np.asarray(vec, f32).reshape(8, 128).T)
    d["snw"] = fmcols(inp["L1_attn_store_norm_w"])
    d["rnw"] = fmcols(inp["L1_attn_retrieve_norm_w"])
    d["knw"] = np.ascontiguousarray(np.broadcast_to(
        np.tile(np.asarray(inp["L1_attn_k_norm_w"], f32), 16)[None, :], (128, 1024)))
    d["qnw"] = np.ascontiguousarray(np.broadcast_to(
        np.tile(np.asarray(inp["L1_attn_q_norm_w"], f32), 16)[None, :], (128, 1024)))
    d["mw0"] = A(inp["L1_attn_mem_W0"]); d["mw1"] = A(inp["L1_attn_mem_W1"])
    d["atw"] = np.ascontiguousarray(np.concatenate(
        [np.asarray(inp["L1_attn_alpha_w"], f32), np.asarray(inp["L1_attn_theta_w"], f32)], axis=1))
    atb = np.concatenate([np.asarray(inp["L1_attn_alpha_b"], f32),
                          np.asarray(inp["L1_attn_theta_b"], f32)])
    d["atb"] = np.ascontiguousarray(np.broadcast_to(atb[None, :], (128, 32)))
    d["wqt"] = np.ascontiguousarray(np.asarray(inp["L1_ffn_weight_q"]).T)
    d["wsc"] = np.ascontiguousarray(np.broadcast_to(
        np.asarray(inp["L1_ffn_weight_scale"], f32)[None, :], (128, 1024)))
    d["wqb"] = np.ascontiguousarray(np.broadcast_to(
        np.asarray(inp["L1_ffn_bias"], f32)[None, :], (128, 1024)))
    return d


def kernel(**inputs):
    nc, _ = build_program()
    in_maps = [_host_inputs(c, inputs) for c in range(8)]
    res = run_bass_kernel_spmd(nc, in_maps, core_ids=list(range(8)))
    return np.stack([r["out"] for r in res.results]).astype(np.float32)


# version 05
# speedup vs baseline: 1.2628x; 1.2628x over previous
from contextlib import ExitStack

import numpy as np
import ml_dtypes

import concourse.bass as bass
import concourse.mybir as mybir
from concourse.bass_utils import run_bass_kernel_spmd

F32 = mybir.dt.float32
BF16 = mybir.dt.bfloat16
I8 = mybir.dt.int8
AF = mybir.ActivationFunctionType
ALU = mybir.AluOpType
AX = mybir.AxisListType

S = 2048
D = 1024
NT = 16
NC = 8
H = 16
HD = 64
LN_EPS = 1e-5
GROUPS = [(0, 6), (6, 11), (11, 16)]
RING = 6


class Sem:
    def __init__(self, nc, es, name):
        self.h = es.enter_context(nc.semaphore(name))
        self.n = 0

    def inc(self, instr, k=1):
        instr.then_inc(self.h, k)
        self.n += k
        return self.n


def build_program(taps=(), stage=99):
    taps = set(taps)
    nc = bass.Bass("TRN2", target_bir_lowering=False, debug=False)
    es = ExitStack()
    sems = []
    pre_sems = {}

    def sem(name):
        if name in pre_sems:
            return pre_sems.pop(name)
        s = Sem(nc, es, name)
        sems.append(s)
        return s

    sb = lambda name, shape, dt=F32: es.enter_context(nc.sbuf_tensor(name, shape, dt))

    def _icount():
        n = 0
        for _ in nc.all_instructions():
            n += 1
        return n

    def dmaN(eng, s, out, in_):
        before = _icount()
        ins = eng.dma_start(out=out, in_=in_)
        ins.then_inc(s.h, 16)
        s.n += 16 * (_icount() - before)
        return ins
    dr = lambda name, shape, dt=F32, kind="ExternalInput": nc.dram_tensor(name, shape, dt, kind=kind).ap()
    tap_outs = {}

    def tap(name, shape, dt=F32):
        if name in taps:
            tap_outs[name] = dr("tap_" + name, shape, dt, kind="ExternalOutput")
            return tap_outs[name]
        return None

    x_in = dr("x", [S, D])
    out_hbm = dr("out", [S, D], kind="ExternalOutput")
    ident_in = dr("ident", [128, 128], BF16)
    masks_in = dr("masks", [128, 256], BF16)
    gate_w_in = dr("gate_w", [D, 8])
    ln1g_in = dr("ln1g", [128, D]); ln1b_in = dr("ln1b", [128, D])
    ln2g_in = dr("ln2g", [128, D], BF16); ln2b_in = dr("ln2b", [128, D], BF16)
    ffn_w1_in = dr("ffn_w1", [D, 4096]); ffn_w2_in = dr("ffn_w2", [4096, D])
    ffn_b1_in = dr("ffn_b1", [128, 32]); ffn_b2_in = dr("ffn_b2", [128, D], BF16)
    moe_w1_in = dr("moe_w1", [8, D, 4096]); moe_w2_in = dr("moe_w2", [8, 4096, D])
    moe_b1_in = dr("moe_b1", [8, 128, 32]); moe_b2_in = dr("moe_b2", [8, D])
    wk_in = dr("wk", [D, D]); wv_in = dr("wv", [D, D]); wq_in = dr("wq", [D, D])
    snw_in = dr("snw", [128, 8]); rnw_in = dr("rnw", [128, 8])
    knw_in = dr("knw", [128, D]); qnw_in = dr("qnw", [128, D])
    mw0_in = dr("mw0", [64, 256]); mw1_in = dr("mw1", [256, 64])
    atw_in = dr("atw", [D, 32]); atb_in = dr("atb", [128, 32])
    wqt_in = dr("wqt", [D, D], I8)
    wsc_in = dr("wsc", [128, D]); wqb_in = dr("wqb", [128, D])
    scr_k = dr("scr_k", [H, S, HD], BF16, kind="Internal")
    scr_q = dr("scr_q", [H, S, HD], BF16, kind="Internal")

    with es:
        x_sb = sb("x_sb", [128, NT * D])
        ar1 = sb("ar1", [128, 16384], BF16)
        fm2 = sb("fm2", [128, 16384], BF16)
        ar2 = sb("ar2", [128, 16384], BF16)
        ar3 = sb("ar3", [128, 6144])
        vring = sb("vring", [128, RING * H * 65], BF16)
        attn_sb = sb("attn_sb", [128, 1280], BF16)
        ident = sb("ident_sb", [128, 128], BF16)
        masks = sb("masks_sb", [128, 256], BF16)
        cst = sb("cst", [128, 8])
        stat0 = sb("stat0", [128, NT * 2])
        stat1 = sb("stat1", [128, NT * 2])
        rstd0 = sb("rstd0", [128, NT])
        rstd1 = sb("rstd1", [128, NT])
        srr = sb("srr", [128, NT])
        sqrtv = sb("sqrtv", [128, NT])
        comb = sb("comb", [128, NT * 8])
        negc = sb("negc", [128, NT * H])
        gwf32 = sb("gwf32", [128, NC * 8])
        b1moe = sb("b1moe", [128, 8 * 32])
        b1ffn = sb("b1ffn", [128, 32])
        atheta = sb("atheta", [128, NT * 32], BF16)
        amrep = sb("amrep", [128, 16])
        mw0b = sb("mw0b", [64, 256], BF16)
        mw1b = sb("mw1b", [128, 128], BF16)
        mw1T = sb("mw1T", [64, 256], BF16)
        onesb = sb("onesb", [128, 2], BF16)
        ident32 = sb("ident32_sb", [128, 128])
        small = sb("small", [128, 64])
        bnsc = sb("bnsc", [128, 12])

        t_xn0f = tap("xn0f", [128, NC * S], BF16)
        t_negc = tap("negc", [128, NT * H])
        t_comb = tap("comb", [128, NT * 8])
        t_g = tap("g", [128, NT * D], BF16)
        t_x1f = tap("x1f", [128, 16384], BF16)
        t_xl0 = tap("xl0", [128, NT * D])
        t_vals = tap("vals", [128, 16384], BF16)
        t_xn1f = tap("xn1f", [128, 16384], BF16)
        t_s1 = tap("s1", [128, 16384], BF16)

        LD = sem("LD"); LG = sem("LG")
        VE_PRE = sem("VE_PRE"); SC_SQ = sem("SC_SQ"); VE_RS = sem("VE_RS")
        VE_TILE = sem("VE_TILE"); PE_TR = sem("PE_TR"); VE_EVAC = sem("VE_EVAC")
        PE_GATE = sem("PE_GATE"); VE_LG = sem("VE_LG"); SC_SIG = sem("SC_SIG"); VE_COMB = sem("VE_COMB")
        VE_CBB = sem("VE_CBB"); PE_CT = sem("PE_CT"); VE_CT = sem("VE_CT")
        LW1a = sem("LW1a"); LW1b = sem("LW1b"); LW2 = sem("LW2"); FW2 = sem("FW2")
        PE_H = sem("PE_H"); SC_H = sem("SC_H"); PE_O = sem("PE_O"); VE_O = sem("VE_O")
        SC_SIL = sem("SC_SIL"); PE_XI = sem("PE_XI"); VE_XI = sem("VE_XI")
        PE_S = sem("PE_S"); SC_E = sem("SC_E"); PE_T = sem("PE_T"); VE_P = sem("VE_P")
        PE_A = sem("PE_A"); VE_Y = sem("VE_Y"); VE_L1 = sem("VE_L1"); SC_L1 = sem("SC_L1")
        VE_X1 = sem("VE_X1"); PE_X1 = sem("PE_X1"); VE_XE = sem("VE_XE"); VE_RB = sem("VE_RB")
        VE_EP = sem("VE_EP"); SC_L2 = sem("SC_L2")
        LDD = sem("LDD"); LDG = sem("LDG"); VE_D = sem("VE_D"); SC_D = sem("SC_D"); PE_D = sem("PE_D")
        VE_DE = sem("VE_DE"); SD = sem("SD"); PE_AM = sem("PE_AM"); VE_AM = sem("VE_AM")
        LKH = sem("LKH"); VE_E = sem("VE_E"); PE_E = sem("PE_E"); SC_E2 = sem("SC_E2")
        GP_E = sem("GP_E"); VE_S1 = sem("VE_S1")
        LF = sem("LF"); VE_F = sem("VE_F"); PE_F = sem("PE_F"); SC_F = sem("SC_F")
        ST = sem("ST")

        for _n in ["VE_WF", "VE_K", "SC_AT", "VE_AT", "SDa", "SDb", "LKH2", "LQH2",
                   "VE_E1", "PE_E1", "SC_E1", "GP_E1", "VE_S1b", "LF2", "LF2g",
                   "VE_F1", "PE_F1", "SC_F1", "STa", "STb"]:
            pre_sems[_n] = Sem(nc, es, _n)
            sems.append(pre_sems[_n])
        for s_ in sems:
            nc.sync.sem_clear(s_.h)
        nc.all_engine_barrier()

        PH_pre = nc.sync
        blk = es.enter_context(nc.Block())
        PH = sem("PH")
        nc.sync.sem_clear(PH.h)
        nc.all_engine_barrier()
        ph_state = [0]

        def phase_join():
            ph_state[0] += 5
            tgt = ph_state[0]
            for ename in ("vector", "scalar", "tensor", "gpsimd", "sync"):
                def mk(e, tgt=tgt):
                    e.drain().then_inc(PH.h, 1)
                    e.wait_ge(PH.h, tgt)
                getattr(blk, ename)(mk)

        @blk.vector
        def _(v):
            for _ in range(80):
                v.memset(ar1[:], 0.0)
                v.drain()

        xn0f = ar1
        xn0t = fm2[:, 0:D]
        xn0t_b = fm2[:, D:2 * D]
        xn0t32 = ar3[:, 0:D]
        sqt = ar3[:, D:2 * D]
        lgs = ar3[:, 2048:2176]
        mlg = ar3[:, 2176:2304]
        gst = ar3[:, 2304:2368]
        x32fm = [ar3[:, 2432:3456], ar3[:, 3456:4480]]
        combb = fm2[:, 2 * D:2 * D + 128]
        combT = vring[0:8, 2048:4096]
        b2moeb = vring[0:8, 4096:5120]

        es_psA = ExitStack()
        p_tr = [es_psA.enter_context(nc.psum_tensor("p_tr0", [128, 1024], BF16)),
                es_psA.enter_context(nc.psum_tensor("p_tr1", [128, 1024], BF16))]
        p_gate = es_psA.enter_context(nc.psum_tensor("p_gate", [128, 512]))
        p_t32 = [es_psA.enter_context(nc.psum_tensor("p_t32a", [128, 1024])),
                 es_psA.enter_context(nc.psum_tensor("p_t32b", [128, 1024]))]

        @blk.sync
        def _(sync):
            x3 = x_in.rearrange("(t p) d -> p t d", p=128)
            dmaN(sync, LD, x_sb[:].rearrange("p (t d) -> p t d", d=D), x3)
            dmaN(sync, LD, ident[:], ident_in)
            dmaN(sync, LD, masks[:], masks_in)
            dmaN(sync, LD, b1moe[:].rearrange("p (e c) -> p e c", c=32),
                 moe_b1_in.rearrange("e p c -> p e c"))
            dmaN(sync, LD, b1ffn[:], ffn_b1_in)

        @blk.gpsimd
        def _(gp):
            gw = gate_w_in.rearrange("(c p) e -> p c e", p=128)
            dmaN(gp, LG, gwf32[:].rearrange("p (c e) -> p c e", e=8), gw)
            dmaN(gp, LG, b2moeb.rearrange("e (o n) -> e o n", o=1),
                 moe_b2_in.rearrange("e (o n) -> e o n", o=1))

        @blk.vector
        def _(v):
            v.memset(cst[:, 0:1], LN_EPS)
            v.memset(cst[:, 1:2], 1e-6)
            v.memset(cst[:, 2:3], 1e-8)
            v.memset(cst[:, 3:4], 1e-34)
            v.memset(cst[:, 4:5], 1.0 / 64.0)
            v.memset(cst[:, 5:6], 1.0)
            v.memset(cst[:, 6:7], -0.5)
            v.memset(onesb[:], 1.0)
            v.wait_ge(LD.h, LD.n)
            v.tensor_copy(ident32[:], ident[:])
            v.drain()
            for t in range(NT):
                xt = x_sb[:, t * D:(t + 1) * D]
                v.bn_stats(bnsc[:, 0:6], xt[:, 0:512])
                v.bn_stats(bnsc[:, 6:12], xt[:, 512:1024])
                v.drain()
                v.bn_aggr(stat0[:, 2 * t:2 * t + 2], bnsc[:].rearrange("p (g s) -> p g s", s=12))
                v.drain()
            VE_PRE.inc(v.drain())

        @blk.scalar
        def _(s):
            s.wait_ge(VE_PRE.h, VE_PRE.n)
            ins = s.activation(out=sqrtv[:].rearrange("p (t o) -> p t o", o=1),
                               in_=stat0[:].rearrange("p (t s) -> p t s", s=2)[:, :, 1:2],
                               func=AF.Sqrt, bias=cst[:, 0:1], scale=1.0)
            SC_SQ.inc(ins)

        @blk.vector
        def _(v):
            v.wait_ge(SC_SQ.h, SC_SQ.n)
            VE_RS.inc(v.reciprocal(rstd0[:], sqrtv[:]))

        def comb_pt1(v, t):
            v.wait_ge(PE_GATE.h, t + 1)
            lg = lgs[:, 0:8]; ml = mlg[:, t * 8:(t + 1) * 8]
            m1 = gst[:, 4 * t:4 * t + 1]; m2 = gst[:, 4 * t + 1:4 * t + 2]
            w2c = gst[:, 4 * t + 2:4 * t + 3]
            cm = comb[:, t * 8:(t + 1) * 8]
            v.tensor_copy(lg, p_gate[:, 0:8])
            v.drain()
            ins = v.reduce_max(m1, lg, axis=AX.X)
            VE_LG.inc(ins)
            v.drain()
            v.tensor_scalar(cm, lg, m1, None, op0=ALU.is_ge)
            v.drain()
            v.scalar_tensor_tensor(ml, cm, -1e30, lg, op0=ALU.mult, op1=ALU.add)
            v.drain()
            v.reduce_max(m2, ml, axis=AX.X)
            v.drain()
            ins = v.tensor_sub(w2c, m2, m1)
            VE_LG.inc(ins)
            v.drain()

        @blk.vector
        def _(v):
            v.wait_ge(VE_RS.h, VE_RS.n)
            for t in range(NT):
                if t >= 1:
                    v.wait_ge(PE_TR.h, t)
                xt = x_sb[:, t * D:(t + 1) * D]
                mu = stat0[:, 2 * t:2 * t + 1]
                rs = rstd0[:, t:t + 1]
                dst = xn0t if t % 2 == 0 else xn0t_b
                v.tensor_scalar(dst, xt, mu, rs, op0=ALU.subtract, op1=ALU.mult)
                v.tensor_scalar(xn0t32, xt, mu, rs, op0=ALU.subtract, op1=ALU.mult)
                v.drain()
                v.tensor_mul(sqt, xn0t32, xn0t32)
                v.drain()
                ins = v.tensor_reduce(
                    negc[:, t * H:(t + 1) * H].rearrange("p (h o) -> p h o", o=1),
                    sqt.rearrange("p (h d) -> p h d", d=64),
                    axis=AX.X, op=ALU.add, negate=True)
                VE_TILE.inc(ins)
                if t >= 1:
                    v.wait_ge(PE_TR.h, t)
                    dstf = xn0f[:].rearrange("p (c s) -> p c s", s=S)[:, :, (t - 1) * 128:t * 128]
                    v.tensor_copy(dstf, p_tr[(t - 1) % 2][:].rearrange("p (c j) -> p c j", j=128))
                    if t >= 2:
                        v.wait_ge(PE_GATE.h, t - 1)
                    VE_EVAC.inc(v.tensor_copy(x32fm[(t - 1) % 2], p_t32[(t - 1) % 2][:]))
                if t >= 2:
                    comb_pt1(v, t - 2)
            v.wait_ge(PE_TR.h, NT)
            dstf = xn0f[:].rearrange("p (c s) -> p c s", s=S)[:, :, (NT - 1) * 128:NT * 128]
            v.tensor_copy(dstf, p_tr[(NT - 1) % 2][:].rearrange("p (c j) -> p c j", j=128))
            v.wait_ge(PE_GATE.h, NT - 1)
            VE_EVAC.inc(v.tensor_copy(x32fm[(NT - 1) % 2], p_t32[(NT - 1) % 2][:]))
            comb_pt1(v, NT - 2)
            comb_pt1(v, NT - 1)
            v.wait_ge(VE_TILE.h, NT)
            v.drain()
            ins = v.tensor_scalar(negc[:], negc[:], -10.0, None, op0=ALU.add)
            VE_TILE.inc(ins)
            v.drain()

        @blk.tensor
        def _(t_):
            t_.wait_ge(LG.h, LG.n)

            def gate_mm(t):
                t_.wait_ge(VE_EVAC.h, t + 1)
                if t >= 1:
                    t_.wait_ge(VE_LG.h, 2 * t - 1)
                ins = None
                for c in range(NC):
                    ins = t_.matmul(p_gate[:, 0:8], x32fm[t % 2][:, c * 128:(c + 1) * 128],
                                    gwf32[:, c * 8:(c + 1) * 8], start=(c == 0), stop=(c == NC - 1))
                PE_GATE.inc(ins)

            for t in range(NT):
                t_.wait_ge(VE_TILE.h, t + 1)
                if t >= 2:
                    t_.wait_ge(VE_EVAC.h, t - 1)
                src = xn0t if t % 2 == 0 else xn0t_b
                dstp = p_tr[t % 2]
                ins = None
                for c in range(NC):
                    ins = t_.transpose(dstp[:, c * 128:(c + 1) * 128], src[:, c * 128:(c + 1) * 128], ident[:])
                for c in range(NC):
                    ins = t_.transpose(p_t32[t % 2][:, c * 128:(c + 1) * 128],
                                       xn0t32[:, c * 128:(c + 1) * 128], ident32[:])
                PE_TR.inc(ins)
                if t >= 1:
                    gate_mm(t - 1)
            gate_mm(NT - 1)


        @blk.scalar
        def _(s):
            for t in range(NT):
                s.wait_ge(VE_LG.h, 2 * (t + 1))
                ins = s.activation(out=gst[:, 4 * t + 2:4 * t + 3],
                                   in_=gst[:, 4 * t + 2:4 * t + 3], func=AF.Sigmoid)
                SC_SIG.inc(ins)

        @blk.vector
        def _(v):
            for t in range(NT):
                v.wait_ge(SC_SIG.h, t + 1)
                m2 = gst[:, 4 * t + 1:4 * t + 2]; w2c = gst[:, 4 * t + 2:4 * t + 3]
                w1c = gst[:, 4 * t + 3:4 * t + 4]
                ml = mlg[:, t * 8:(t + 1) * 8]
                cm = comb[:, t * 8:(t + 1) * 8]
                v.tensor_scalar(w1c, w2c, -1.0, 1.0, op0=ALU.mult, op1=ALU.add)
                v.tensor_scalar(ml, ml, m2, None, op0=ALU.is_ge)
                v.drain()
                v.tensor_scalar(cm, cm, w1c, None, op0=ALU.mult)
                v.drain()
                ins = v.scalar_tensor_tensor(cm, ml, w2c, cm, op0=ALU.mult, op1=ALU.add)
                VE_COMB.inc(ins)
                v.drain()

        @blk.vector
        def _(v):
            v.wait_ge(VE_COMB.h, VE_COMB.n)
            VE_CBB.inc(v.tensor_copy(combb, comb[:]))
            v.drain()

        @blk.tensor
        def _(t_):
            t_.wait_ge(VE_CBB.h, VE_CBB.n)
            ins = None
            for t in range(NT):
                dstp = p_tr[t // 8][0:8, (t % 8) * 128:(t % 8 + 1) * 128]
                ins = t_.transpose(dstp, combb[:, t * 8:(t + 1) * 8], ident[:])
            PE_CT.inc(ins)

        @blk.vector
        def _(v):
            v.wait_ge(PE_CT.h, PE_CT.n)
            v.tensor_copy(combT[:, 0:1024], p_tr[0][0:8, :])
            VE_CT.inc(v.tensor_copy(combT[:, 1024:2048], p_tr[1][0:8, :]))
            v.drain()

        final_waits = [VE_COMB, VE_CT, VE_TILE]

        def emit_mlp(groups, n_experts, w1_dram, w2_dram, rhs_fm, hbuf, w2buf, w1buf,
                     act_fn, bias_col_fn, evac_fn, extra_mm2=None, gate_start=None,
                     tail_pe=None, tail_sc=None, tail_ve=None, h_serial=False):
            LW1P = [LW1a, LW1b]
            base = dict(LW1a=LW1a.n, LW1b=LW1b.n, LW2=LW2.n, FW2=FW2.n, PE_H=PE_H.n,
                        SC_H=SC_H.n, PE_O=PE_O.n, VE_O=VE_O.n)
            n_fq = 4
            n_mt = 8

            @blk.gpsimd
            def _(gp):
                mt_i = 0
                fq_i = 0
                for gi, (t0, t1) in enumerate(groups):
                    for e in range(n_experts):
                        for fq in range(n_fq):
                            if fq_i >= 1:
                                gp.wait_ge(FW2.h, base["FW2"] + fq_i)
                            w2s = w2_dram[e] if n_experts > 1 else w2_dram
                            src = w2s[fq * 1024:(fq + 1) * 1024, :].rearrange("(c p) n -> p c n", p=128)
                            dmaN(gp, LW2, w2buf.rearrange("p (c n) -> p c n", n=1024), src)
                            fq_i += 1
                            for mt in range(n_mt):
                                if mt_i >= 2:
                                    gp.wait_ge(PE_H.h, base["PE_H"] + mt_i - 1)
                                w1s = w1_dram[e] if n_experts > 1 else w1_dram
                                m0 = fq * 1024 + mt * 128
                                srcw = w1s[:, m0:m0 + 128].rearrange("(c p) m -> p c m", p=128)
                                par = mt_i % 2
                                dmaN(gp, LW1P[par],
                                     w1buf[:, par * 1024:(par + 1) * 1024].rearrange(
                                         "p (c m) -> p c m", m=128), srcw)
                                mt_i += 1

            @blk.tensor
            def _(t_):
                if gate_start is not None:
                    for s_, n_ in gate_start:
                        t_.wait_ge(s_.h, n_)
                mt_i = 0
                fq_i = 0
                o_i = 0
                for gi, (t0, t1) in enumerate(groups):
                    ntok = (t1 - t0) * 128
                    nspans = [(i * 512, min(512, ntok - i * 512)) for i in range((ntok + 511) // 512)]
                    if tail_pe is not None and gi >= 1:
                        tail_pe(t_, gi - 1)
                    for e in range(n_experts):
                        for fq in range(n_fq):
                            for mt in range(n_mt):
                                par_ = mt_i % 2
                                t_.wait_ge(LW1P[par_].h,
                                           base["LW1a" if par_ == 0 else "LW1b"] + 16 * (mt_i // 2 + 1))
                                if h_serial:
                                    if mt_i >= 1:
                                        t_.wait_ge(SC_H.h, base["SC_H"] + mt_i)
                                elif mt_i >= 2:
                                    t_.wait_ge(SC_H.h, base["SC_H"] + mt_i - 1)
                                par = 0 if h_serial else mt_i % 2
                                w1slice = w1buf[:, (mt_i % 2) * 1024:(mt_i % 2 + 1) * 1024]
                                ins = None
                                for (soff, slen) in nspans:
                                    for kc in range(NC):
                                        ins = t_.matmul(
                                            p_h[par][:, soff:soff + slen],
                                            w1slice[:, kc * 128:(kc + 1) * 128],
                                            rhs_fm[:, kc * S + t0 * 128 + soff:
                                                   kc * S + t0 * 128 + soff + slen],
                                            start=(kc == 0), stop=(kc == NC - 1))
                                PE_H.inc(ins)
                                mt_i += 1
                            t_.wait_ge(LW2.h, base["LW2"] + 16 * (fq_i + 1))
                            t_.wait_ge(SC_H.h, base["SC_H"] + mt_i)
                            for tt in range(t0, t1):
                                if o_i >= 2:
                                    t_.wait_ge(VE_O.h, base["VE_O"] + o_i - 1)
                                par = o_i % 2
                                lt = (tt - t0) * 128
                                ins = None
                                has_extra = (extra_mm2 is not None and e == n_experts - 1
                                             and fq == n_fq - 1)
                                for ns in range(2):
                                    for fc in range(8):
                                        stop = (fc == 7 and not has_extra)
                                        ins = t_.matmul(
                                            p_o[par][:, ns * 512:(ns + 1) * 512],
                                            hbuf[:, fc * 1024 + lt: fc * 1024 + lt + 128],
                                            w2buf[:, fc * 1024 + ns * 512: fc * 1024 + (ns + 1) * 512],
                                            start=(fc == 0), stop=stop)
                                if has_extra:
                                    ins = extra_mm2(t_, tt, par)
                                PE_O.inc(ins)
                                o_i += 1
                            FW2.inc(t_.drain())
                            fq_i += 1
                if tail_pe is not None:
                    tail_pe(t_, len(groups) - 1)

            @blk.scalar
            def _(s):
                mt_i = 0
                for gi, (t0, t1) in enumerate(groups):
                    ntok = (t1 - t0) * 128
                    if tail_sc is not None and gi >= 1:
                        tail_sc(s, gi - 1)
                    for e in range(n_experts):
                        for fq in range(n_fq):
                            for mt in range(n_mt):
                                s.wait_ge(PE_H.h, base["PE_H"] + mt_i + 1)
                                par = 0 if h_serial else mt_i % 2
                                ins = s.activation(out=hbuf[:, mt * 1024:mt * 1024 + ntok],
                                                   in_=p_h[par][:, 0:ntok],
                                                   func=act_fn, bias=bias_col_fn(e, fq, mt), scale=1.0)
                                SC_H.inc(ins)
                                mt_i += 1
                if tail_sc is not None:
                    tail_sc(s, len(groups) - 1)

            @blk.vector
            def _(v):
                o_i = 0
                for gi, (t0, t1) in enumerate(groups):
                    if tail_ve is not None and gi >= 1:
                        tail_ve(v, gi - 1)
                    for e in range(n_experts):
                        for fq in range(n_fq):
                            for tt in range(t0, t1):
                                v.wait_ge(PE_O.h, base["PE_O"] + o_i + 1)
                                par = o_i % 2
                                evac_fn(v, gi, e, fq, tt, p_o[par][:], (e == 0 and fq == 0))
                                VE_O.inc(v.drain())
                                o_i += 1
                if tail_ve is not None:
                    tail_ve(v, len(groups) - 1)
            return base

        if stage >= 2:
            phase_join()
            es_psA.close()
            es_psM = ExitStack()
            p_h = [es_psM.enter_context(nc.psum_tensor("p_h0", [128, 1024])),
                   es_psM.enter_context(nc.psum_tensor("p_h1", [128, 1024]))]
            p_o = [es_psM.enter_context(nc.psum_tensor("p_o0", [128, 1024])),
                   es_psM.enter_context(nc.psum_tensor("p_o1", [128, 1024]))]

            g_moe = ar2
            acc = ar3
            w2buf_m = fm2[:, 0:8192]
            hbuf_m = fm2[:, 8192:16384]
            w1buf_m = vring[:, 0:2048]

            sil_cum = []
            tot = 0
            for (t0, t1) in GROUPS:
                tot += (t1 - t0)
                sil_cum.append(tot)
            sc_sil_base = SC_SIL.n

            def moe_evac(v, gi, e, fq, tt, p_ap, first):
                if first and gi >= 1:
                    v.wait_ge(SC_SIL.h, sc_sil_base + sil_cum[gi - 1])
                t0 = GROUPS[gi][0]
                accs = acc[:, (tt - t0) * 1024:(tt - t0 + 1) * 1024]
                cmcol = comb[:, tt * 8 + e:tt * 8 + e + 1]
                if first:
                    v.tensor_scalar(accs, p_ap, cmcol, None, op0=ALU.mult)
                else:
                    v.scalar_tensor_tensor(accs, p_ap, cmcol, accs, op0=ALU.mult, op1=ALU.add)

            def moe_b2_mm(t_, tt, par):
                ins = None
                for ns in range(2):
                    ins = t_.matmul(p_o[par][:, ns * 512:(ns + 1) * 512],
                                    combT[:, tt * 128:(tt + 1) * 128],
                                    b2moeb[:, ns * 512:(ns + 1) * 512],
                                    start=False, stop=True)
                return ins

            ve_o_base_holder = [0]

            def moe_tail_sc(s, gi):
                t0, t1 = GROUPS[gi]
                s.wait_ge(VE_O.h, ve_o_base_holder[0] + 8 * 4 * sil_cum[gi])
                for tt in range(t0, t1):
                    ins = s.activation(out=g_moe[:, tt * D:(tt + 1) * D],
                                       in_=acc[:, (tt - t0) * 1024:(tt - t0 + 1) * 1024],
                                       func=AF.Silu)
                    SC_SIL.inc(ins)

            ve_o_base_holder[0] = VE_O.n
            moe_base = emit_mlp(GROUPS, 8, moe_w1_in, moe_w2_in, xn0f, hbuf_m, w2buf_m, w1buf_m,
                                AF.Gelu,
                                lambda e, fq, mt: b1moe[:, e * 32 + fq * 8 + mt:e * 32 + fq * 8 + mt + 1],
                                moe_evac, extra_mm2=moe_b2_mm,
                                gate_start=[(LG, LG.n), (VE_CT, VE_CT.n)],
                                tail_sc=moe_tail_sc)

            final_waits.append(SC_SIL)

        if stage >= 3:
            phase_join()
            es_psM.close()
            es_psB = ExitStack()
            p_s = es_psB.enter_context(nc.psum_tensor("p_s", [128, 1024]))
            p_tb = es_psB.enter_context(nc.psum_tensor("p_tb", [128, 1024], BF16))
            p_y = es_psB.enter_context(nc.psum_tensor("p_y", [128, 512]))
            p_x1 = es_psB.enter_context(nc.psum_tensor("p_x1", [128, 1024]))

            x1f = fm2
            ybuf = ar3[:, 0:D]
            ln1g_sb = ar3[:, D:2 * D]
            ln1b_sb = ar3[:, 2 * D:3 * D]
            vtmp = ar3[:, 3 * D:4 * D]
            p_sb = attn_sb[:, 0:640]
            pt_sb = attn_sb[:, 640:1280]

            def win(qt):
                j0 = max(0, qt - 2)
                j1 = min(NT - 1, qt + 2)
                return j0, j1

            @blk.sync
            def _(sync):
                dmaN(sync, LD, ln1g_sb, ln1g_in)
                dmaN(sync, LD, ln1b_sb, ln1b_in)
            ld_b = LD.n

            @blk.vector
            def _(v):
                v.wait_ge(LD.h, ld_b)
                v.memset(vring[:, 0:RING * H * 65], 1.0)
                v.tensor_copy(ident32[:], ident[:])
                v.drain()
                def build_vtile(j):
                    slot = j % RING
                    v.tensor_scalar(vtmp, x_sb[:, j * D:(j + 1) * D],
                                    stat0[:, 2 * j:2 * j + 1], rstd0[:, j:j + 1],
                                    op0=ALU.subtract, op1=ALU.mult)
                    v.drain()
                    ins = v.tensor_copy(
                        vring[:, slot * H * 65:(slot + 1) * H * 65].rearrange(
                            "p (h e) -> p h e", e=65)[:, :, 0:64],
                        vtmp.rearrange("p (h d) -> p h d", d=64))
                    VE_RB.inc(ins)
                    v.drain()
                for j in range(3):
                    build_vtile(j)
                for qt in range(NT):
                    j0, j1 = win(qt)
                    nw = j1 - j0 + 1
                    v.tensor_scalar(ybuf, x_sb[:, qt * D:(qt + 1) * D],
                                    stat0[:, 2 * qt:2 * qt + 1], rstd0[:, qt:qt + 1],
                                    op0=ALU.subtract, op1=ALU.mult)
                    VE_Y.inc(v.drain())
                    j = qt + 3
                    if j < NT:
                        if qt >= 3:
                            v.wait_ge(PE_A.h, qt * H)
                        build_vtile(j)
                    for h in range(H):
                        i = qt * H + h
                        v.wait_ge(PE_T.h, i + 1)
                        for jw in range(nw):
                            j_ = j0 + jw
                            srcp = p_tb[:, jw * 128:(jw + 1) * 128]
                            dstp = pt_sb[:, jw * 128:(jw + 1) * 128]
                            if j_ == qt - 2:
                                v.tensor_mul(dstp, srcp, masks[:, 0:128])
                            elif j_ == qt + 2:
                                v.tensor_mul(dstp, srcp, masks[:, 128:256])
                            else:
                                v.tensor_copy(dstp, srcp)
                        VE_P.inc(v.drain())
                        v.wait_ge(PE_A.h, i + 1)
                        iv = small[:, 0:1]
                        v.reciprocal(iv, p_y[:, 64:65])
                        v.drain()
                        ins = v.scalar_tensor_tensor(ybuf[:, h * 64:(h + 1) * 64], p_y[:, 0:64],
                                                     iv, ybuf[:, h * 64:(h + 1) * 64],
                                                     op0=ALU.mult, op1=ALU.add)
                        VE_Y.inc(ins)
                        v.drain()
                    v.bn_stats(bnsc[:, 0:6], ybuf[:, 0:512])
                    v.bn_stats(bnsc[:, 6:12], ybuf[:, 512:1024])
                    v.drain()
                    v.bn_aggr(small[:, 1:3], bnsc[:].rearrange("p (g s) -> p g s", s=12))
                    VE_L1.inc(v.drain())
                    v.wait_ge(SC_L1.h, qt + 1)
                    v.reciprocal(small[:, 4:5], small[:, 3:4])
                    v.drain()
                    v.tensor_scalar(ybuf, ybuf, small[:, 1:2], small[:, 4:5],
                                    op0=ALU.subtract, op1=ALU.mult)
                    v.drain()
                    v.tensor_mul(ybuf, ybuf, ln1g_sb)
                    v.drain()
                    v.tensor_add(ybuf, ybuf, ln1b_sb)
                    VE_X1.inc(v.drain())
                    v.wait_ge(PE_X1.h, qt + 1)
                    dstf = x1f[:].rearrange("p (c s) -> p c s", s=S)[:, :, qt * 128:(qt + 1) * 128]
                    ins = v.tensor_copy(dstf, p_x1[:].rearrange("p (c j) -> p c j", j=128))
                    VE_XE.inc(ins)
                    v.drain()

            @blk.scalar
            def _(s):
                for qt in range(NT):
                    j0, j1 = win(qt)
                    nw = j1 - j0 + 1
                    for h in range(H):
                        i = qt * H + h
                        s.wait_ge(PE_S.h, i + 1)
                        if i >= 1:
                            s.wait_ge(PE_T.h, i)
                        ins = s.activation(out=p_sb[:, 0:nw * 128], in_=p_s[:, 0:nw * 128],
                                           func=AF.Exp, bias=negc[:, qt * H + h:qt * H + h + 1],
                                           scale=1.0)
                        SC_E.inc(ins)
                    s.wait_ge(VE_L1.h, qt + 1)
                    ins = s.activation(out=small[:, 3:4], in_=small[:, 2:3],
                                       func=AF.Sqrt, bias=cst[:, 0:1], scale=1.0)
                    SC_L1.inc(ins)

            @blk.tensor
            def _(t_):
                for qt in range(NT):
                    j0, j1 = win(qt)
                    nw = j1 - j0 + 1
                    hc_of = lambda h: (h // 2) * S
                    for h in range(H):
                        i = qt * H + h
                        bp = (h % 2) * 64
                        if i >= 1:
                            t_.wait_ge(SC_E.h, i)
                        q_ap = xn0f[bp:bp + 64, hc_of(h) + qt * 128: hc_of(h) + (qt + 1) * 128]
                        ins = None
                        ncols = nw * 128
                        for soff in range(0, ncols, 512):
                            slen = min(512, ncols - soff)
                            k_ap = xn0f[bp:bp + 64,
                                        hc_of(h) + j0 * 128 + soff: hc_of(h) + j0 * 128 + soff + slen]
                            ins = t_.matmul(p_s[:, soff:soff + slen], q_ap, k_ap,
                                            start=True, stop=True)
                        PE_S.inc(ins)
                        t_.wait_ge(SC_E.h, i + 1)
                        if i >= 1:
                            t_.wait_ge(VE_P.h, i)
                        ins = None
                        for jw in range(nw):
                            ins = t_.transpose(p_tb[:, jw * 128:(jw + 1) * 128],
                                               p_sb[:, jw * 128:(jw + 1) * 128], ident[:])
                        PE_T.inc(ins)
                        t_.wait_ge(VE_P.h, i + 1)
                        t_.wait_ge(VE_RB.h, min(qt + 3, NT))
                        t_.wait_ge(VE_Y.h, (qt + 1) + i)
                        ins = None
                        for jw in range(nw):
                            j_ = j0 + jw
                            slot = j_ % RING
                            ins = t_.matmul(p_y[:, 0:65],
                                            pt_sb[:, jw * 128:(jw + 1) * 128],
                                            vring[:, slot * H * 65 + h * 65: slot * H * 65 + (h + 1) * 65],
                                            start=(jw == 0), stop=(jw == nw - 1))
                        PE_A.inc(ins)
                    t_.wait_ge(VE_X1.h, qt + 1)
                    if qt >= 1:
                        t_.wait_ge(VE_XE.h, qt)
                    ins = None
                    for c in range(NC):
                        ins = t_.transpose(p_x1[:, c * 128:(c + 1) * 128],
                                           ybuf[:, c * 128:(c + 1) * 128], ident32[:])
                    PE_X1.inc(ins)

            final_waits += [VE_XE]

        if stage >= 4:
            phase_join()
            es_psB.close()
            es_psC = ExitStack()
            p_h0c = es_psC.enter_context(nc.psum_tensor("pc_h0", [128, 1024]))
            p_h = [p_h0c, p_h0c]
            p_o = [es_psC.enter_context(nc.psum_tensor("pc_o0", [128, 1024])),
                   es_psC.enter_context(nc.psum_tensor("pc_o1", [128, 1024]))]
            p_xb = es_psC.enter_context(nc.psum_tensor("p_xb", [128, 1024], BF16))

            acc = ar3
            w2buf_c = ar1[:, 0:8192]
            hbuf_c = ar1[:, 8192:16384]
            w1buf_c = vring[:, 0:2048]
            ln2g_sb = vring[:, 2048:3072]
            ln2b_sb = vring[:, 3072:4096]

            @blk.sync
            def _(sync):
                dmaN(sync, LD, ln2g_sb, ln2g_in)
                dmaN(sync, LD, ln2b_sb, ln2b_in)
                dmaN(sync, LD, vring[:, 4096:5120], ffn_b2_in)
            ld_c = LD.n
            b2f_sb = vring[:, 4096:5120]

            ep_cum = []
            tot = 0
            for (t0, t1) in GROUPS:
                tot += (t1 - t0)
                ep_cum.append(tot)
            ve_ep_base = VE_EP.n
            pe_xi_base = PE_XI.n
            ve_o_base_c = VE_O.n
            ve_l1_base = VE_L1.n

            def ffn_evac(v, gi, e, fq, tt, p_ap, first):
                if first and gi >= 1 and tt == GROUPS[gi][0]:
                    v.wait_ge(VE_EP.h, ve_ep_base + ep_cum[gi - 1])
                t0 = GROUPS[gi][0]
                accs = acc[:, (tt - t0) * 1024:(tt - t0 + 1) * 1024]
                if first:
                    v.tensor_copy(accs, p_ap)
                else:
                    v.tensor_add(accs, accs, p_ap)

            ffn_veo_cum = []
            tot = 0
            for (t0, t1) in GROUPS:
                tot += 4 * (t1 - t0)
                ffn_veo_cum.append(tot)

            def ffn_tail_pe(t_, gi):
                t0, t1 = GROUPS[gi]
                for tt in range(t0, t1):
                    k = (sum(b - a for a, b in GROUPS[:gi]) + (tt - t0))
                    t_.wait_ge(VE_O.h, ve_o_base_c + ffn_veo_cum[gi])
                    if k >= 1:
                        t_.wait_ge(VE_XI.h, k)
                    ins = None
                    for c in range(NC):
                        ins = t_.transpose(p_xb[:, c * 128:(c + 1) * 128],
                                           x1f[:, c * S + tt * 128: c * S + (tt + 1) * 128],
                                           ident[:])
                    PE_XI.inc(ins)

            def ffn_tail_sc(s, gi):
                t0, t1 = GROUPS[gi]
                for tt in range(t0, t1):
                    k = (sum(b - a for a, b in GROUPS[:gi]) + (tt - t0))
                    s.wait_ge(VE_L1.h, ve_l1_base + k + 1)
                    ins = s.activation(out=small[:, 8:9], in_=small[:, 7:8],
                                       func=AF.Sqrt, bias=cst[:, 0:1], scale=1.0)
                    SC_L2.inc(ins)

            def ffn_tail_ve(v, gi):
                t0, t1 = GROUPS[gi]
                for tt in range(t0, t1):
                    k = (sum(b - a for a, b in GROUPS[:gi]) + (tt - t0))
                    accs = acc[:, (tt - t0) * 1024:(tt - t0 + 1) * 1024]
                    v.wait_ge(PE_XI.h, pe_xi_base + k + 1)
                    v.tensor_add(accs, accs, p_xb[:])
                    VE_XI.inc(v.drain())
                    v.tensor_add(accs, accs, b2f_sb)
                    v.drain()
                    v.bn_stats(bnsc[:, 0:6], accs[:, 0:512])
                    v.bn_stats(bnsc[:, 6:12], accs[:, 512:1024])
                    v.drain()
                    v.bn_aggr(small[:, 6:8], bnsc[:].rearrange("p (g s) -> p g s", s=12))
                    VE_L1.inc(v.drain())
                    v.wait_ge(SC_L2.h, k + 1)
                    v.reciprocal(small[:, 9:10], small[:, 8:9])
                    v.drain()
                    v.tensor_scalar(accs, accs, small[:, 6:7], small[:, 9:10],
                                    op0=ALU.subtract, op1=ALU.mult)
                    v.drain()
                    v.tensor_mul(accs, accs, ln2g_sb)
                    v.drain()
                    v.tensor_add(accs, accs, ln2b_sb)
                    v.drain()
                    v.tensor_mul(accs, accs, ar2[:, tt * D:(tt + 1) * D])
                    v.drain()
                    ins = v.tensor_add(x_sb[:, tt * D:(tt + 1) * D],
                                       x_sb[:, tt * D:(tt + 1) * D], accs)
                    VE_EP.inc(ins)
                    v.drain()

            ffn_base = emit_mlp(GROUPS, 1, ffn_w1_in, ffn_w2_in, x1f, hbuf_c, w2buf_c, w1buf_c,
                                AF.Relu, lambda e, fq, mt: b1ffn[:, fq * 8 + mt:fq * 8 + mt + 1],
                                ffn_evac, gate_start=[(LD, ld_c)],
                                tail_pe=ffn_tail_pe, tail_sc=ffn_tail_sc, tail_ve=ffn_tail_ve,
                                h_serial=True)

            final_waits += [VE_EP]

        if stage >= 5:
            phase_join()
            es_psC.close()
            es_psD = ExitStack()
            p_d = [es_psD.enter_context(nc.psum_tensor("p_d0", [128, 1024])),
                   es_psD.enter_context(nc.psum_tensor("p_d1", [128, 1024]))]
            p_dt = [es_psD.enter_context(nc.psum_tensor("p_dt0", [128, 1024], BF16)),
                    es_psD.enter_context(nc.psum_tensor("p_dt1", [128, 1024], BF16))]
            p_am = es_psD.enter_context(nc.psum_tensor("p_am", [128, 512]))

            VE_WF = sem("VE_WF"); VE_K = sem("VE_K"); SC_AT = sem("SC_AT"); VE_AT = sem("VE_AT")
            SDP = [sem("SDa"), sem("SDb")]
            sd_counts_g = [[], []]
            nc2_clear = [VE_WF, VE_K, SC_AT, VE_AT]

            xn1nF = fm2
            wbD = ar1[:, 0:8192]
            atwb = vring[:, 0:256]
            xn1nt = [vring[:, 256:1280], vring[:, 1280:2304]]
            ktile_b = [vring[:, 2304:3328], vring[:, 3328:4352]]
            onesrow = vring[0:1, 4352:4480]
            amrow = vring[0:1, 4480:4496]
            keysc = ar3[:, 0:1024]
            ksq = ar3[:, 1024:2048]
            knw_sb = ar3[:, 2048:3072]
            qnw_sb = ar3[:, 3072:4096]
            at32 = ar3[:, 4096:4128]
            atb_sb = b1ffn
            snw_c = lambda c: small[:, 32 + c:33 + c]
            rnw_c = lambda c: small[:, 40 + c:41 + c]

            @blk.sync
            def _(sync):
                dmaN(sync, LDD, atb_sb[:], atb_in)
                dmaN(sync, LDD, small[:, 32:40], snw_in)
                dmaN(sync, LDD, small[:, 40:48], rnw_in)
                dmaN(sync, LDD, knw_sb, knw_in)
                dmaN(sync, LDD, qnw_sb, qnw_in)
            ldd_n = LDD.n

            @blk.gpsimd
            def _(gp):
                wkc = wk_in.rearrange("(c p) n -> p c n", p=128)
                dmaN(gp, LDG, wbD.rearrange("p (c n) -> p c n", n=1024), wkc)
            ldg_wk = LDG.n

            @blk.vector
            def _(v):
                for t in range(NT):
                    xt = x_sb[:, t * D:(t + 1) * D]
                    v.bn_stats(bnsc[:, 0:6], xt[:, 0:512])
                    v.bn_stats(bnsc[:, 6:12], xt[:, 512:1024])
                    v.drain()
                    v.bn_aggr(stat1[:, 2 * t:2 * t + 2], bnsc[:].rearrange("p (g s) -> p g s", s=12))
                    v.drain()
                VE_D.inc(v.drain())

            @blk.scalar
            def _(s):
                s.wait_ge(VE_D.h, VE_D.n)
                ins = s.activation(out=sqrtv[:].rearrange("p (t o) -> p t o", o=1),
                                   in_=stat1[:].rearrange("p (t s) -> p t s", s=2)[:, :, 1:2],
                                   func=AF.Sqrt, bias=cst[:, 0:1], scale=1.0)
                SC_D.inc(ins)

            @blk.vector
            def _(v):
                v.wait_ge(SC_D.h, SC_D.n)
                v.reciprocal(rstd1[:], sqrtv[:])
                v.drain()
                varS = stat1[:].rearrange("p (t s) -> p t s", s=2)[:, :, 1:2]
                v.tensor_tensor(small[:, 48:64].rearrange("p (t o) -> p t o", o=1), varS,
                                rstd1[:].rearrange("p (t o) -> p t o", o=1), op=ALU.mult)
                v.drain()
                v.tensor_mul(small[:, 48:64], small[:, 48:64], rstd1[:])
                VE_D.inc(v.drain())

            @blk.scalar
            def _(s):
                s.wait_ge(VE_D.h, VE_D.n)
                ins = s.activation(out=small[:, 0:16], in_=small[:, 48:64],
                                   func=AF.Sqrt, bias=cst[:, 1:2], scale=1.0)
                SC_D.inc(ins)

            @blk.vector
            def _(v):
                v.wait_ge(SC_D.h, SC_D.n)
                v.wait_ge(LDD.h, ldd_n)
                v.reciprocal(small[:, 16:32], small[:, 0:16])
                v.drain()
                v.tensor_mul(srr[:], small[:, 16:32], rstd1[:])
                v.drain()
                v.wait_ge(LDG.h, ldg_wk)
                for c in range(NC):
                    v.tensor_scalar(wbD[:, c * 1024:(c + 1) * 1024], wbD[:, c * 1024:(c + 1) * 1024],
                                    snw_c(c), None, op0=ALU.mult)
                VE_WF.inc(v.drain())
                for t in range(NT):
                    if t >= 2:
                        v.wait_ge(PE_D.h, t - 1)
                    v.tensor_scalar(xn1nt[t % 2], x_sb[:, t * D:(t + 1) * D],
                                    stat1[:, 2 * t:2 * t + 1], srr[:, t:t + 1],
                                    op0=ALU.subtract, op1=ALU.mult)
                    VE_D.inc(v.drain())
                    if t >= 1:
                        v.wait_ge(PE_D.h, t)
                        dstf = xn1nF[:].rearrange("p (c s) -> p c s", s=S)[:, :, (t - 1) * 128:t * 128]
                        VE_DE.inc(v.tensor_copy(dstf, p_dt[(t - 1) % 2][:].rearrange("p (c j) -> p c j", j=128)))
                v.wait_ge(PE_D.h, NT)
                dstf = xn1nF[:].rearrange("p (c s) -> p c s", s=S)[:, :, (NT - 1) * 128:NT * 128]
                VE_DE.inc(v.tensor_copy(dstf, p_dt[(NT - 1) % 2][:].rearrange("p (c j) -> p c j", j=128)))

            ve_d_base = VE_D.n - NT

            @blk.tensor
            def _(t_):
                for t in range(NT):
                    t_.wait_ge(VE_D.h, ve_d_base + t + 1)
                    if t >= 2:
                        t_.wait_ge(VE_DE.h, t - 1)
                    ins = None
                    for c in range(NC):
                        ins = t_.transpose(p_dt[t % 2][:, c * 128:(c + 1) * 128],
                                           xn1nt[t % 2][:, c * 128:(c + 1) * 128], ident[:])
                    PE_D.inc(ins)

            ve_de_all = VE_DE.n

            def emit_proj(kind):
                pe_base = PE_D.n
                sc_base = SC_D.n
                ve_base = VE_K.n
                sd_counts = sd_counts_g
                for t in range(NT):
                    @blk.tensor
                    def _(t_, t=t, pe_base=pe_base):
                        if t == 0:
                            t_.wait_ge(VE_DE.h, ve_de_all)
                            t_.wait_ge(VE_WF.h, VE_WF.n)
                        if t >= 2:
                            t_.wait_ge(VE_K.h, ve_base + (t - 1 if kind == 1 else 2 * t - 3))
                        ins = None
                        for ns in range(2):
                            for c in range(NC):
                                ins = t_.matmul(p_d[t % 2][:, ns * 512:(ns + 1) * 512],
                                                xn1nF[:, c * S + t * 128:c * S + (t + 1) * 128],
                                                wbD[:, c * 1024 + ns * 512:c * 1024 + (ns + 1) * 512],
                                                start=(c == 0), stop=(c == NC - 1))
                        PE_D.inc(ins)

                    @blk.vector
                    def _(v, t=t, pe_base=pe_base, ve_base=ve_base, sc_base=sc_base):
                        v.wait_ge(PE_D.h, pe_base + t + 1)
                        if kind != 1 and t >= 1:
                            v.wait_ge(SC_D.h, sc_base + t)
                        if kind == 1:
                            ins = v.tensor_copy(ar2[:, t * 1024:(t + 1) * 1024], p_d[t % 2][:])
                            VE_K.inc(ins)
                            v.drain()
                            return
                        v.tensor_copy(keysc, p_d[t % 2][:])
                        v.drain()
                        v.tensor_mul(ksq, keysc, keysc)
                        v.drain()
                        v.tensor_reduce(small[:, 0:16].rearrange("p (h o) -> p h o", o=1),
                                        ksq.rearrange("p (h d) -> p h d", d=64),
                                        axis=AX.X, op=ALU.add)
                        VE_K.inc(v.drain())

                    if kind != 1:
                        @blk.scalar
                        def _(s, t=t, ve_base=ve_base, sc_base=sc_base):
                            s.wait_ge(VE_K.h, ve_base + 2 * t + 1)
                            ins = s.activation(out=small[:, 16:32], in_=small[:, 0:16],
                                               func=AF.Sqrt, bias=cst[:, 1:2], scale=cst[:, 4:5])
                            SC_D.inc(ins)

                        @blk.vector
                        def _(v, t=t, sc_base=sc_base):
                            v.wait_ge(SC_D.h, sc_base + t + 1)
                            v.reciprocal(small[:, 0:16], small[:, 16:32])
                            v.drain()
                            nwsb = knw_sb if kind == 0 else qnw_sb
                            v.tensor_mul(keysc, keysc, nwsb)
                            v.drain()
                            if sd_counts[t % 2]:
                                v.wait_ge(SDP[t % 2].h, sd_counts[t % 2][-1])
                            ins = v.tensor_tensor(
                                ktile_b[t % 2].rearrange("p (h d) -> p h d", d=64), 
                                keysc.rearrange("p (h d) -> p h d", d=64),
                                small[:, 0:16].rearrange("p (h o) -> p h o", o=1).to_broadcast([128, 16, 64]),
                                op=ALU.mult)
                            VE_K.inc(ins)
                            v.drain()

                        @blk.sync
                        def _(sync, t=t, ve_base=ve_base):
                            sync.wait_ge(VE_K.h, ve_base + 2 * (t + 1))
                            scr = scr_k if kind == 0 else scr_q
                            dst = scr.rearrange("h (t p) d -> t p h d", p=128)[t]
                            dmaN(sync, SDP[t % 2], dst,
                                 ktile_b[t % 2].rearrange("p (h d) -> p h d", d=64))
                            sd_counts[t % 2].append(SDP[t % 2].n)

            emit_proj(0)

            @blk.gpsimd
            def _(gp):
                gp.wait_ge(PE_D.h, PE_D.n)
                wvc = wv_in.rearrange("(c p) n -> p c n", p=128)
                dmaN(gp, LDG, wbD.rearrange("p (c n) -> p c n", n=1024), wvc)

            @blk.vector
            def _(v):
                v.wait_ge(LDG.h, LDG.n)
                for c in range(NC):
                    v.tensor_scalar(wbD[:, c * 1024:(c + 1) * 1024], wbD[:, c * 1024:(c + 1) * 1024],
                                    snw_c(c), None, op0=ALU.mult)
                VE_WF.inc(v.drain())

            @blk.tensor
            def _(t_):
                t_.wait_ge(VE_WF.h, VE_WF.n)
            emit_proj(1)

            @blk.gpsimd
            def _(gp):
                gp.wait_ge(PE_D.h, PE_D.n)
                wqc = wq_in.rearrange("(c p) n -> p c n", p=128)
                dmaN(gp, LDG, wbD.rearrange("p (c n) -> p c n", n=1024), wqc)

            @blk.vector
            def _(v):
                v.wait_ge(LDG.h, LDG.n)
                for c in range(NC):
                    v.tensor_scalar(wbD[:, c * 1024:(c + 1) * 1024], wbD[:, c * 1024:(c + 1) * 1024],
                                    rnw_c(c), None, op0=ALU.mult)
                VE_WF.inc(v.drain())

            @blk.tensor
            def _(t_):
                t_.wait_ge(VE_WF.h, VE_WF.n)
            emit_proj(2)

            @blk.gpsimd
            def _(gp):
                gp.wait_ge(PE_D.h, PE_D.n)
                atc = atw_in.rearrange("(c p) n -> p c n", p=128)
                dmaN(gp, LDG, atwb.rearrange("p (c n) -> p c n", n=32), atc)
                mw0c = mw0_in.rearrange("p n -> p n")
                dmaN(gp, LDG, mw0b[:], mw0c)
                dmaN(gp, LDG, mw1b[:].rearrange("p (k n) -> p k n", n=64),
                     mw1_in.rearrange("(k p) n -> p k n", p=128))

            @blk.vector
            def _(v):
                v.wait_ge(LDG.h, LDG.n)
                for c in range(NC):
                    v.tensor_scalar(atwb[:, c * 32:(c + 1) * 32], atwb[:, c * 32:(c + 1) * 32],
                                    snw_c(c), None, op0=ALU.mult)
                VE_WF.inc(v.drain())

            pe_at_base = PE_D.n
            sc_at_base = SC_AT.n

            @blk.tensor
            def _(t_):
                t_.wait_ge(VE_WF.h, VE_WF.n)
                for t in range(NT):
                    if t >= 1:
                        t_.wait_ge(VE_AT.h, t)
                    ins = None
                    for c in range(NC):
                        ins = t_.matmul(p_am[:, 0:32],
                                        xn1nF[:, c * S + t * 128:c * S + (t + 1) * 128],
                                        atwb[:, c * 32:(c + 1) * 32],
                                        start=(c == 0), stop=(c == NC - 1))
                    PE_D.inc(ins)

            @blk.vector
            def _(v):
                for t in range(NT):
                    v.wait_ge(PE_D.h, pe_at_base + t + 1)
                    if t >= 1:
                        v.wait_ge(SC_AT.h, sc_at_base + t)
                    v.tensor_add(at32, p_am[:, 0:32], atb_sb[:])
                    VE_AT.inc(v.drain())

            @blk.scalar
            def _(s):
                for t in range(NT):
                    s.wait_ge(VE_AT.h, t + 1)
                    ins = s.activation(out=atheta[:, t * 32:(t + 1) * 32], in_=at32,
                                       func=AF.Sigmoid)
                    SC_AT.inc(ins)


            @blk.vector
            def _(v):
                v.wait_ge(SC_AT.h, SC_AT.n)
                ins = v.tensor_scalar(
                    atheta[:].rearrange("p (t g) -> p t g", g=32)[:, :, 16:32],
                    atheta[:].rearrange("p (t g) -> p t g", g=32)[:, :, 16:32],
                    0.02, None, op0=ALU.mult)
                VE_AT.inc(ins)
                v.drain()
                v.memset(onesrow, 1.0)
                v.drain()
                VE_AT.inc(v.drain())

            @blk.tensor
            def _(t_):
                t_.wait_ge(VE_AT.h, VE_AT.n)
                ins = None
                for t in range(NT):
                    ins = t_.matmul(p_am[0:16, 32:33], atheta[:, t * 32:t * 32 + 16],
                                    onesb[:, 0:1], start=(t == 0), stop=(t == NT - 1))
                PE_AM.inc(ins)

            @blk.vector
            def _(v):
                v.wait_ge(PE_AM.h, PE_AM.n)
                v.tensor_scalar(small[0:16, 0:1], p_am[0:16, 32:33], -1.0 / 2048.0, 1.0,
                                op0=ALU.mult, op1=ALU.add)
                v.drain()
                ins = v.tensor_copy(vring[0:16, 4496:4497], small[0:16, 0:1])
                VE_AM.inc(ins)
                v.drain()

            @blk.tensor
            def _(t_):
                t_.wait_ge(VE_AM.h, VE_AM.n)
                ins = t_.transpose(p_dt[0][0:1, 0:16], vring[0:16, 4496:4497], ident[0:16, 0:16])
                PE_AM.inc(ins)

            @blk.vector
            def _(v):
                v.wait_ge(PE_AM.h, PE_AM.n)
                ins = v.tensor_copy(amrow, p_dt[0][0:1, 0:16])
                VE_AM.inc(ins)
                v.drain()

            @blk.tensor
            def _(t_):
                t_.wait_ge(VE_AM.h, VE_AM.n)
                ins = t_.matmul(p_am[:, 48:64], onesrow, amrow, start=True, stop=True)
                PE_AM.inc(ins)
                ins = t_.transpose(p_dt[0][0:64, 128:256], mw1b[:, 0:64], ident[:])
                ins = t_.transpose(p_dt[0][0:64, 256:384], mw1b[:, 64:128], ident[:])
                PE_AM.inc(ins)

            @blk.vector
            def _(v):
                v.wait_ge(PE_AM.h, PE_AM.n)
                v.tensor_copy(amrep[:], p_am[:, 48:64])
                v.tensor_copy(mw1T[:, 0:128], p_dt[0][0:64, 128:256])
                ins = v.tensor_copy(mw1T[:, 128:256], p_dt[0][0:64, 256:384])
                VE_AM.inc(ins)
                v.drain()

            final_waits += [VE_AM, SDP[0], SDP[1]]

        if stage >= 6:
            phase_join()
            es_psD.close()
            es_psE = ExitStack()
            p_a = [es_psE.enter_context(nc.psum_tensor("p_a0", [128, 512])),
                   es_psE.enter_context(nc.psum_tensor("p_a1", [128, 512]))]
            p_b = [es_psE.enter_context(nc.psum_tensor("p_b0", [128, 512])),
                   es_psE.enter_context(nc.psum_tensor("p_b1", [128, 512]))]
            p_tre = [es_psE.enter_context(nc.psum_tensor("p_tre0", [128, 1024], BF16)),
                     es_psE.enter_context(nc.psum_tensor("p_tre1", [128, 1024], BF16))]
            p_g0 = es_psE.enter_context(nc.psum_tensor("p_g0", [128, 512]))
            p_g1 = es_psE.enter_context(nc.psum_tensor("p_g1", [128, 512]))

            LKH = sem("LKH2"); LQH = sem("LQH2")
            VE_E1 = sem("VE_E1"); PE_E1 = sem("PE_E1"); SC_E1 = sem("SC_E1")
            GP_E1 = sem("GP_E1"); VE_S1b = sem("VE_S1b")

            s1tm = fm2
            kh = ar1[:, 0:1024]
            qh = ar1[:, 1024:2048]
            khf = ar1[:, 4096:6144]
            h1f = ar1[:, 6144:10240]
            pre0tm = ar1[:, 10240:14336]
            dhf = ar1[:, 14336:16384]
            d_htm = vring[:, 0:1024]
            w0a = vring[0:64, 1024:1280]
            w0n = vring[0:64, 1280:1536]
            w1a = vring[:, 1536:1664]
            w1n = vring[:, 1664:1792]
            d_h0t = [vring[:, 1792:2048], vring[:, 2048:2304]]
            ggt = [vring[:, 2304:2560], vring[:, 2560:2816]]
            e1t = [vring[:, 2816:3072], vring[:, 3072:3328]]
            cfb = vring[0:1, 3328:3329]
            pre1 = ar3[:, 0:1024]
            sq0t = [ar3[:, 1024:1280], ar3[:, 1280:1536]]
            numt = [ar3[:, 1536:1792], ar3[:, 1792:2048]]
            dA = ar3[:, 2048:3072]
            dB = ar3[:, 3072:4096]
            dC = ar3[:, 4096:5120]
            dD = ar3[:, 5120:6144]

            for h in range(H):
                lkh_n0 = LKH.n
                lqh_n0 = LQH.n

                @blk.sync
                def _(sync, h=h):
                    if h == 0:
                        sync.wait_ge(SDP[0].h, SDP[0].n)
                        sync.wait_ge(SDP[1].h, SDP[1].n)
                    if h >= 1:
                        sync.wait_ge(VE_S1b.h, h)
                    dmaN(sync, LKH, kh.rearrange("p (t d) -> p t d", d=64),
                         scr_k[h].rearrange("(t p) d -> p t d", p=128))
                    dmaN(sync, LQH, qh.rearrange("p (t d) -> p t d", d=64),
                         scr_q[h].rearrange("(t p) d -> p t d", p=128))

                pe_t0 = PE_E1.n
                ve_khf0 = VE_E1.n

                @blk.tensor
                def _(t_, h=h, pe_t0=pe_t0, lkh_n=LKH.n, ve_khf0=ve_khf0):
                    t_.wait_ge(LKH.h, lkh_n)
                    if h >= 1:
                        t_.wait_ge(VE_S1b.h, h)
                    for t in range(NT):
                        if t >= 2:
                            t_.wait_ge(VE_E1.h, ve_khf0 + t - 1)
                        ins = t_.transpose(p_tre[t % 2][0:64, 0:128], kh[:, t * 64:(t + 1) * 64],
                                           ident[:])
                        PE_E1.inc(ins)

                @blk.vector
                def _(v, h=h, pe_t0=pe_t0):
                    for t in range(NT):
                        v.wait_ge(PE_E1.h, pe_t0 + t + 1)
                        ins = v.tensor_copy(khf[0:64, t * 128:(t + 1) * 128], p_tre[t % 2][0:64, 0:128])
                        VE_E1.inc(ins)
                        v.drain()

                pe_a0 = PE_E1.n
                ve_khf_all = VE_E1.n

                sc_h1f0 = SC_E1.n

                @blk.tensor
                def _(t_, pe_a0=pe_a0, ve_khf_all=ve_khf_all, sc_h1f0=sc_h1f0):
                    t_.wait_ge(VE_E1.h, ve_khf_all)
                    k = 0
                    for mt in range(2):
                        for ns in range(4):
                            if k >= 2:
                                t_.wait_ge(SC_E1.h, sc_h1f0 + k - 1)
                            ins = t_.matmul(p_a[k % 2][:, 0:512],
                                            mw0b[:, mt * 128:(mt + 1) * 128],
                                            khf[0:64, ns * 512:(ns + 1) * 512],
                                            start=True, stop=True)
                            PE_E1.inc(ins)
                            k += 1

                @blk.scalar
                def _(s, pe_a0=pe_a0):
                    k = 0
                    for mt in range(2):
                        for ns in range(4):
                            s.wait_ge(PE_E1.h, pe_a0 + k + 1)
                            ins = s.activation(out=h1f[:, mt * 2048 + ns * 512: mt * 2048 + (ns + 1) * 512],
                                               in_=p_a[k % 2][:, 0:512], func=AF.Gelu)
                            SC_E1.inc(ins)
                            k += 1

                pe_b0 = PE_E1.n
                sc_h1f_all = SC_E1.n

                ve_p0tm0 = VE_E1.n

                @blk.tensor
                def _(t_, pe_b0=pe_b0, ve_p0tm0=ve_p0tm0):
                    t_.wait_ge(VE_E1.h, ve_p0tm0)
                    for t in range(NT):
                        if t >= 2:
                            t_.wait_ge(VE_E1.h, ve_p0tm0 + t - 1)
                        ins = t_.matmul(p_b[t % 2][:, 0:256], khf[0:64, t * 128:(t + 1) * 128],
                                        mw0b[:], start=True, stop=True)
                        PE_E1.inc(ins)

                @blk.vector
                def _(v, pe_b0=pe_b0):
                    for t in range(NT):
                        v.wait_ge(PE_E1.h, pe_b0 + t + 1)
                        ins = v.tensor_copy(pre0tm[:, t * 256:(t + 1) * 256], p_b[t % 2][:, 0:256])
                        VE_E1.inc(ins)
                        v.drain()

                pe_c0 = PE_E1.n
                ve_p0tm_all = VE_E1.n

                ve_pre10 = VE_E1.n

                @blk.tensor
                def _(t_, pe_c0=pe_c0, sc_h1f_all=sc_h1f_all, ve_pre10=ve_pre10):
                    t_.wait_ge(SC_E1.h, sc_h1f_all)
                    t_.wait_ge(VE_E1.h, ve_pre10)
                    for t in range(NT):
                        if t >= 2:
                            t_.wait_ge(VE_E1.h, ve_pre10 + t - 1)
                        ins = None
                        for kc in range(2):
                            ins = t_.matmul(p_b[t % 2][:, 0:64],
                                            h1f[:, kc * 2048 + t * 128: kc * 2048 + (t + 1) * 128],
                                            mw1b[:, kc * 64:(kc + 1) * 64],
                                            start=(kc == 0), stop=(kc == 1))
                        PE_E1.inc(ins)

                @blk.vector
                def _(v, pe_c0=pe_c0):
                    for t in range(NT):
                        v.wait_ge(PE_E1.h, pe_c0 + t + 1)
                        ins = v.tensor_copy(pre1[:, t * 64:(t + 1) * 64], p_b[t % 2][:, 0:64])
                        VE_E1.inc(ins)
                        v.drain()

                ve_pre1_all = VE_E1.n

                @blk.scalar
                def _(s, h=h, ve_pre1_all=ve_pre1_all):
                    s.wait_ge(VE_E1.h, ve_pre1_all)
                    ins = s.activation(out=h1f[:], in_=pre0tm[:], func=AF.Gelu)
                    SC_E1.inc(ins)
                h1tm = h1f

                sc_h1tm = SC_E1.n

                @blk.vector
                def _(v, h=h, sc_h1tm=sc_h1tm, lqh_n=LQH.n):
                    v.wait_ge(SC_E1.h, sc_h1tm)
                    v.wait_ge(LQH.h, lqh_n)
                    pre1v = pre1.rearrange("p (t d) -> p t d", d=64)
                    v.tensor_mul(dA, pre1[:], pre1[:])
                    v.drain()
                    v.tensor_reduce(small[:, 0:16].rearrange("p (t o) -> p t o", o=1),
                                    dA.rearrange("p (t d) -> p t d", d=64), axis=AX.X, op=ALU.add)
                    VE_E1.inc(v.drain())

                @blk.scalar
                def _(s):
                    s.wait_ge(VE_E1.h, VE_E1.n)
                    ins = s.activation(out=small[:, 16:32], in_=small[:, 0:16],
                                       func=AF.Sqrt, bias=cst[:, 2:3], scale=cst[:, 4:5])
                    SC_E1.inc(ins)

                @blk.vector
                def _(v, h=h):
                    v.wait_ge(SC_E1.h, SC_E1.n)
                    v.reciprocal(small[:, 0:16], small[:, 16:32])
                    v.drain()
                    irb = small[:, 0:16].rearrange("p (t o) -> p t o", o=1).to_broadcast([128, NT, 64])
                    v3 = lambda x: x.rearrange("p (t d) -> p t d", d=64)
                    v.tensor_tensor(v3(dA), v3(pre1[:]), irb, op=ALU.mult)
                    v.drain()
                    v.tensor_add(dB, dA, kh)
                    v.drain()
                    vals_h = ar2[:].rearrange("p (t n) -> p t n", n=1024)[:, :, h * 64:(h + 1) * 64]
                    v.tensor_tensor(v3(dC), v3(dB), vals_h, op=ALU.subtract)
                    v.drain()
                    thb = atheta[:].rearrange("p (t g) -> p t g", g=32)[:, :, 16 + h:17 + h].to_broadcast([128, NT, 64])
                    v.tensor_tensor(v3(dB), v3(dC), thb, op=ALU.mult)
                    v.drain()
                    v.tensor_mul(dC, dB, dA)
                    v.drain()
                    v.tensor_reduce(small[:, 16:32].rearrange("p (t o) -> p t o", o=1),
                                    v3(dC), axis=AX.X, op=ALU.add)
                    v.drain()
                    v.tensor_scalar(small[:, 16:32], small[:, 16:32], 1.0 / 64.0, None, op0=ALU.mult)
                    v.drain()
                    dyb = small[:, 16:32].rearrange("p (t o) -> p t o", o=1).to_broadcast([128, NT, 64])
                    v.tensor_tensor(v3(dC), v3(dA), dyb, op=ALU.mult)
                    v.drain()
                    v.tensor_sub(dC, dB, dC)
                    v.drain()
                    irb2 = small[:, 0:16].rearrange("p (t o) -> p t o", o=1).to_broadcast([128, NT, 64])
                    ins = v.tensor_tensor(v3(d_htm), v3(dC), irb2, op=ALU.mult)
                    VE_E1.inc(ins)
                    v.drain()

                pe_d0 = PE_E1.n
                ve_dh = VE_E1.n

                ve_dhf0 = VE_E1.n

                @blk.tensor
                def _(t_, pe_d0=pe_d0, ve_dh=ve_dh, ve_dhf0=ve_dhf0):
                    t_.wait_ge(VE_E1.h, ve_dh)
                    for t in range(NT):
                        if t >= 2:
                            t_.wait_ge(VE_E1.h, ve_dhf0 + t - 1)
                        ins = t_.transpose(p_tre[t % 2][0:64, 0:128], d_htm[:, t * 64:(t + 1) * 64],
                                           ident[:])
                        PE_E1.inc(ins)

                @blk.vector
                def _(v, pe_d0=pe_d0):
                    for t in range(NT):
                        v.wait_ge(PE_E1.h, pe_d0 + t + 1)
                        ins = v.tensor_copy(dhf[0:64, t * 128:(t + 1) * 128], p_tre[t % 2][0:64, 0:128])
                        VE_E1.inc(ins)
                        v.drain()

                pe_e0 = PE_E1.n
                ve_dhf_all = VE_E1.n
                sc_e0 = SC_E1.n
                gp_e0 = GP_E1.n
                ve_f0 = VE_E1.n

                ve_dh00 = VE_E1.n

                @blk.tensor
                def _(t_, pe_e0=pe_e0, ve_dhf_all=ve_dhf_all, ve_dh00=ve_dh00):
                    t_.wait_ge(VE_E1.h, ve_dhf_all)
                    for t in range(NT + 1):
                        if t < NT:
                            if t >= 2:
                                t_.wait_ge(VE_E1.h, ve_dh00 + 2 * t - 2)
                            ins = t_.matmul(p_b[t % 2][:, 0:256], dhf[0:64, t * 128:(t + 1) * 128],
                                            mw1T[:], start=True, stop=True)
                            PE_E1.inc(ins)
                        if t >= 1:
                            tt = t - 1
                            t_.wait_ge(VE_E1.h, ve_dh00 + 2 * tt + 2)
                            ins = t_.matmul(p_g0[0:64, 0:256], kh[:, tt * 64:(tt + 1) * 64],
                                            d_h0t[tt % 2], start=(tt == 0), stop=(tt == NT - 1))
                            PE_E1.inc(ins)

                rect = [ar3[:, 3072:3328], ar3[:, 3328:3584]]

                @blk.scalar
                def _(s, sc_e0=sc_e0, gp_e0=gp_e0, ve_gg0=ve_dh00):
                    for t in range(NT):
                        if t >= 2:
                            s.wait_ge(GP_E1.h, gp_e0 + 4 * (t - 1))
                            s.wait_ge(VE_E1.h, ve_gg0 + 2 * t - 3)
                        ins = s.activation(out=sq0t[t % 2], in_=pre0tm[:, t * 256:(t + 1) * 256],
                                           func=AF.Square)
                        SC_E1.inc(ins)
                        s.drain()
                        ins = s.activation(out=e1t[t % 2], in_=sq0t[t % 2], func=AF.Exp,
                                           bias=0.0, scale=cst[:, 6:7])
                        SC_E1.inc(ins)
                        s.drain()

                ve_gg0 = VE_E1.n

                @blk.gpsimd
                def _(gp, h=h, sc_e0=sc_e0, ve_gg0=ve_gg0):
                    for t in range(NT):
                        gp.wait_ge(SC_E1.h, sc_e0 + 2 * (t + 1))
                        gp.wait_ge(VE_E1.h, ve_gg0 + 2 * t + 1)
                        if t >= 2:
                            gp.wait_ge(VE_E1.h, ve_gg0 + 2 * t - 2)
                        gp.tensor_tensor(numt[t % 2], h1tm[:, t * 256:(t + 1) * 256],
                                         pre0tm[:, t * 256:(t + 1) * 256], op=ALU.mult)
                        gp.drain()
                        gp.tensor_tensor(numt[t % 2], numt[t % 2], rect[t % 2], op=ALU.mult)
                        gp.drain()
                        gp.tensor_tensor(ggt[t % 2], pre0tm[:, t * 256:(t + 1) * 256], e1t[t % 2],
                                         op=ALU.mult)
                        gp.drain()
                        ins = gp.tensor_tensor(ggt[t % 2], ggt[t % 2], numt[t % 2], op=ALU.add)
                        GP_E1.inc(ins, 4)
                        gp.drain()

                @blk.vector
                def _(v, pe_e0=pe_e0, gp_e0=gp_e0, sc_e0=sc_e0):
                    for t in range(NT):
                        v.wait_ge(SC_E1.h, sc_e0 + 2 * t + 2)
                        if t >= 2:
                            v.wait_ge(GP_E1.h, gp_e0 + 4 * (t - 1))
                        v.tensor_scalar(rect[t % 2], sq0t[t % 2], 1e-34, None, op0=ALU.add)
                        v.drain()
                        v.tensor_scalar(e1t[t % 2], e1t[t % 2], 0.3989422804014327, None,
                                        op0=ALU.mult)
                        v.drain()
                        ins = v.reciprocal(rect[t % 2], rect[t % 2])
                        VE_E1.inc(ins)
                        v.drain()
                        v.wait_ge(PE_E1.h, pe_e0 + max(1, 2 * t))
                        v.wait_ge(GP_E1.h, gp_e0 + 4 * (t + 1))
                        ins = v.tensor_tensor(d_h0t[t % 2], p_b[t % 2][:, 0:256], ggt[t % 2],
                                              op=ALU.mult)
                        VE_E1.inc(ins)
                        v.drain()

                ve_dh0_all = VE_E1.n

                @blk.tensor
                def _(t_, h=h):
                    for mh in range(2):
                        for t in range(NT):
                            ins = t_.matmul(p_g1[:, mh * 64:(mh + 1) * 64],
                                            h1tm[:, t * 256 + mh * 128: t * 256 + (mh + 1) * 128],
                                            d_htm[:, t * 64:(t + 1) * 64],
                                            start=(t == 0), stop=(t == NT - 1))
                        PE_E1.inc(ins)

                pe_gall = PE_E1.n

                @blk.scalar
                def _(s, pe_gall=pe_gall):
                    s.wait_ge(PE_E1.h, pe_gall)
                    ins = s.activation(out=dA[0:64, 0:256], in_=p_g0[0:64, 0:256], func=AF.Square)
                    SC_E1.inc(ins)
                    s.drain()
                    ins = s.activation(out=dA[:, 256:384], in_=p_g1[:, 0:128], func=AF.Square)
                    SC_E1.inc(ins)
                    s.drain()
                sc_clip = SC_E1.n

                @blk.vector
                def _(v, sc_clip=sc_clip):
                    v.wait_ge(SC_E1.h, sc_clip)
                    v.tensor_reduce(small[0:64, 32:33], dA[0:64, 0:256], axis=AX.X, op=ALU.add)
                    v.tensor_reduce(small[:, 33:34], dA[:, 256:384], axis=AX.X, op=ALU.add)
                    VE_E1.inc(v.drain())

                @blk.tensor
                def _(t_):
                    t_.wait_ge(VE_E1.h, VE_E1.n)
                    ins = t_.matmul(p_a[0][0:1, 0:1], small[0:64, 32:33], cst[0:64, 5:6],
                                    start=True, stop=False)
                    ins = t_.matmul(p_a[0][0:1, 0:1], small[:, 33:34], cst[:, 5:6],
                                    start=False, stop=True)
                    PE_E1.inc(ins)

                @blk.scalar
                def _(s):
                    s.wait_ge(PE_E1.h, PE_E1.n)
                    ins = s.activation(out=small[0:1, 34:35], in_=p_a[0][0:1, 0:1], func=AF.Sqrt)
                    SC_E1.inc(ins)

                @blk.vector
                def _(v):
                    v.wait_ge(SC_E1.h, SC_E1.n)
                    v.tensor_scalar(small[0:1, 34:35], small[0:1, 34:35], 1e-6, None, op0=ALU.add)
                    v.drain()
                    v.reciprocal(small[0:1, 35:36], small[0:1, 34:35])
                    v.drain()
                    v.tensor_scalar(small[0:1, 35:36], small[0:1, 35:36], 10.0, None, op0=ALU.mult)
                    v.drain()
                    v.tensor_scalar(small[0:1, 35:36], small[0:1, 35:36], 1.0, None, op0=ALU.min)
                    v.drain()
                    v.tensor_scalar(small[0:1, 35:36], small[0:1, 35:36], -1.0, None, op0=ALU.mult)
                    v.drain()
                    ins = v.tensor_copy(cfb, small[0:1, 35:36])
                    VE_E1.inc(ins)
                    v.drain()

                @blk.tensor
                def _(t_):
                    t_.wait_ge(VE_E1.h, VE_E1.n)
                    ins = t_.matmul(p_a[0][:, 8:9], onesrow, cfb, start=True, stop=True)
                    PE_E1.inc(ins)

                @blk.vector
                def _(v, h=h):
                    v.wait_ge(PE_E1.h, PE_E1.n)
                    v.tensor_copy(small[:, 36:37], p_a[0][:, 8:9])
                    v.drain()
                    v.tensor_scalar(w0a, mw0b[:], amrep[0:64, h:h + 1], None, op0=ALU.mult)
                    v.tensor_scalar(w1a, mw1b[:], amrep[:, h:h + 1], None, op0=ALU.mult)
                    v.drain()
                    v.scalar_tensor_tensor(w0n, p_g0[0:64, 0:256], small[0:64, 36:37], w0a,
                                           op0=ALU.mult, op1=ALU.add)
                    ins = v.scalar_tensor_tensor(w1n, p_g1[:, 0:128], small[:, 36:37], w1a,
                                                 op0=ALU.mult, op1=ALU.add)
                    VE_E1.inc(ins)
                    v.drain()

                pe_q0 = PE_E1.n
                ve_w = VE_E1.n

                ve_qhf0 = VE_E1.n

                @blk.tensor
                def _(t_, pe_q0=pe_q0, ve_w=ve_w, lqh_n=LQH.n, ve_qhf0=ve_qhf0):
                    t_.wait_ge(LQH.h, lqh_n)
                    t_.wait_ge(VE_E1.h, ve_w)
                    for t in range(NT):
                        if t >= 2:
                            t_.wait_ge(VE_E1.h, ve_qhf0 + t - 1)
                        ins = t_.transpose(p_tre[t % 2][0:64, 0:128], qh[:, t * 64:(t + 1) * 64],
                                           ident[:])
                        PE_E1.inc(ins)

                @blk.vector
                def _(v, pe_q0=pe_q0):
                    for t in range(NT):
                        v.wait_ge(PE_E1.h, pe_q0 + t + 1)
                        ins = v.tensor_copy(khf[0:64, t * 128:(t + 1) * 128], p_tre[t % 2][0:64, 0:128])
                        VE_E1.inc(ins)
                        v.drain()
                qhf = khf

                pe_m0 = PE_E1.n
                ve_qhf_all = VE_E1.n

                sc_a0f0 = SC_E1.n

                @blk.tensor
                def _(t_, pe_m0=pe_m0, ve_qhf_all=ve_qhf_all, sc_a0f0=sc_a0f0):
                    t_.wait_ge(VE_E1.h, ve_qhf_all)
                    k = 0
                    for mt in range(2):
                        for ns in range(4):
                            if k >= 2:
                                t_.wait_ge(SC_E1.h, sc_a0f0 + k - 1)
                            ins = t_.matmul(p_a[k % 2][:, 0:512],
                                            w0n[:, mt * 128:(mt + 1) * 128],
                                            qhf[0:64, ns * 512:(ns + 1) * 512],
                                            start=True, stop=True)
                            PE_E1.inc(ins)
                            k += 1

                a0f = pre0tm

                @blk.scalar
                def _(s, pe_m0=pe_m0):
                    k = 0
                    for mt in range(2):
                        for ns in range(4):
                            s.wait_ge(PE_E1.h, pe_m0 + k + 1)
                            ins = s.activation(out=a0f[:, mt * 2048 + ns * 512: mt * 2048 + (ns + 1) * 512],
                                               in_=p_a[k % 2][:, 0:512], func=AF.Gelu)
                            SC_E1.inc(ins)
                            k += 1

                pe_m1 = PE_E1.n
                sc_a0f_all = SC_E1.n
                m1tm = pre1

                ve_m10 = VE_E1.n

                @blk.tensor
                def _(t_, pe_m1=pe_m1, sc_a0f_all=sc_a0f_all, ve_m10=ve_m10):
                    t_.wait_ge(SC_E1.h, sc_a0f_all)
                    t_.wait_ge(VE_E1.h, ve_m10)
                    for t in range(NT):
                        if t >= 2:
                            t_.wait_ge(VE_E1.h, ve_m10 + t - 1)
                        ins = None
                        for kc in range(2):
                            ins = t_.matmul(p_b[t % 2][:, 0:64],
                                            a0f[:, kc * 2048 + t * 128: kc * 2048 + (t + 1) * 128],
                                            w1n[:, kc * 64:(kc + 1) * 64],
                                            start=(kc == 0), stop=(kc == 1))
                        PE_E1.inc(ins)

                @blk.vector
                def _(v, pe_m1=pe_m1):
                    for t in range(NT):
                        v.wait_ge(PE_E1.h, pe_m1 + t + 1)
                        ins = v.tensor_copy(m1tm[:, t * 64:(t + 1) * 64], p_b[t % 2][:, 0:64])
                        VE_E1.inc(ins)
                        v.drain()

                @blk.vector
                def _(v, h=h):
                    v.wait_ge(VE_E1.h, VE_E1.n)
                    v3 = lambda x: x.rearrange("p (t d) -> p t d", d=64)
                    v.tensor_mul(dA, m1tm[:], m1tm[:])
                    v.drain()
                    v.tensor_reduce(small[:, 0:16].rearrange("p (t o) -> p t o", o=1),
                                    v3(dA), axis=AX.X, op=ALU.add)
                    VE_E1.inc(v.drain())

                @blk.scalar
                def _(s):
                    s.wait_ge(VE_E1.h, VE_E1.n)
                    ins = s.activation(out=small[:, 16:32], in_=small[:, 0:16],
                                       func=AF.Sqrt, bias=cst[:, 2:3], scale=cst[:, 4:5])
                    SC_E1.inc(ins)

                @blk.vector
                def _(v, h=h):
                    v.wait_ge(SC_E1.h, SC_E1.n)
                    v.reciprocal(small[:, 0:16], small[:, 16:32])
                    v.drain()
                    v3 = lambda x: x.rearrange("p (t d) -> p t d", d=64)
                    irb = small[:, 0:16].rearrange("p (t o) -> p t o", o=1).to_broadcast([128, NT, 64])
                    v.tensor_tensor(v3(dA), v3(m1tm[:]), irb, op=ALU.mult)
                    v.drain()
                    s1h = s1tm[:].rearrange("p (t n) -> p t n", n=1024)[:, :, h * 64:(h + 1) * 64]
                    ins = v.tensor_tensor(s1h, v3(dA), qh.rearrange("p (t d) -> p t d", d=64),
                                          op=ALU.add)
                    VE_S1b.inc(ins)
                    v.drain()

            final_waits += [VE_S1b]

        if stage >= 7:
            phase_join()
            es_psE.close()
            es_psF = ExitStack()
            p_tf = es_psF.enter_context(nc.psum_tensor("p_tf", [128, 1024], BF16))
            p_q = es_psF.enter_context(nc.psum_tensor("p_q", [128, 1024]))

            LF2 = sem("LF2"); LF2g = sem("LF2g")
            VE_F1 = sem("VE_F1"); PE_F1 = sem("PE_F1"); SC_F1 = sem("SC_F1")
            STP = [sem("STa"), sem("STb")]
            st_counts = [[], []]

            wqtb = ar1[:, 0:8192]
            xn1t_f = ar3[:, 0:1024]
            t1f = ar3[:, 1024:2048]
            xof = [ar3[:, 2048:3072], ar3[:, 3072:4096]]
            wscb = ar3[:, 4096:5120]
            wqbb = ar3[:, 5120:6144]
            xqt = vring[:, 0:1024]
            xqf = vring[:, 1024:2048]
            g1t = vring[:, 2048:3072]
            s1g = vring[:, 3072:4096]

            @blk.sync
            def _(sync):
                dmaN(sync, LF2, wscb, wsc_in)
                dmaN(sync, LF2, wqbb, wqb_in)

            @blk.gpsimd
            def _(gp):
                dmaN(gp, LF2g, wqtb.rearrange("p (c n) -> p c n", n=1024),
                     wqt_in.rearrange("(c p) n -> p c n", p=128))
            lf_n = LF2.n
            lf_ng = LF2g.n

            for t in range(NT):
                @blk.vector
                def _(v, t=t):
                    if t == 0:
                        v.wait_ge(LF2.h, lf_n)
                        v.wait_ge(LF2g.h, lf_ng)
                    if t >= 2 and st_counts[t % 2]:
                        v.wait_ge(STP[t % 2].h, st_counts[t % 2][-1])
                    mu = stat1[:, 2 * t:2 * t + 1]
                    rs = rstd1[:, t:t + 1]
                    v.tensor_scalar(xn1t_f, x_sb[:, t * D:(t + 1) * D], mu, rs,
                                    op0=ALU.subtract, op1=ALU.mult)
                    v.drain()
                    v.tensor_reduce(small[:, 0:1], xn1t_f, axis=AX.X, op=ALU.max,
                                    apply_absolute_value=True)
                    v.drain()
                    v.tensor_scalar(small[:, 1:2], small[:, 0:1], 1e-6, 1.0 / 127.0,
                                    op0=ALU.max, op1=ALU.mult)
                    v.drain()
                    v.reciprocal(small[:, 2:3], small[:, 1:2])
                    v.drain()
                    v.tensor_scalar(xn1t_f, xn1t_f, small[:, 2:3], None, op0=ALU.mult)
                    v.drain()
                    v.tensor_scalar(xn1t_f, xn1t_f, 12582912.0, 12582912.0,
                                    op0=ALU.add, op1=ALU.subtract)
                    v.drain()
                    v.tensor_scalar(xn1t_f, xn1t_f, 127.0, -127.0, op0=ALU.min, op1=ALU.max)
                    v.drain()
                    ins = v.tensor_copy(xqt, xn1t_f)
                    VE_F1.inc(ins)
                    v.drain()

                @blk.tensor
                def _(t_, t=t):
                    t_.wait_ge(VE_F1.h, VE_F1.n)
                    ins = None
                    for c in range(NC):
                        ins = t_.transpose(p_tf[:, c * 128:(c + 1) * 128],
                                           xqt[:, c * 128:(c + 1) * 128], ident[:])
                    PE_F1.inc(ins)

                @blk.vector
                def _(v, t=t):
                    v.wait_ge(PE_F1.h, PE_F1.n)
                    ins = v.tensor_copy(xqf, p_tf[:])
                    VE_F1.inc(ins)
                    v.drain()

                @blk.tensor
                def _(t_, t=t):
                    t_.wait_ge(VE_F1.h, VE_F1.n)
                    ins = None
                    for ns in range(2):
                        for c in range(NC):
                            ins = t_.matmul(p_q[:, ns * 512:(ns + 1) * 512],
                                            xqf[:, c * 128:(c + 1) * 128],
                                            wqtb[:, c * 1024 + ns * 512: c * 1024 + (ns + 1) * 512],
                                            start=(c == 0), stop=(c == NC - 1))
                    PE_F1.inc(ins)

                @blk.vector
                def _(v, t=t):
                    v.wait_ge(PE_F1.h, PE_F1.n)
                    v.tensor_scalar(t1f, p_q[:], small[:, 1:2], None, op0=ALU.mult)
                    v.drain()
                    v.tensor_mul(t1f, t1f, wscb)
                    v.drain()
                    ins = v.tensor_add(t1f, t1f, wqbb)
                    VE_F1.inc(ins)
                    v.drain()

                @blk.scalar
                def _(s, t=t):
                    s.wait_ge(VE_F1.h, VE_F1.n)
                    ins = s.activation(out=g1t, in_=t1f, func=AF.Silu)
                    SC_F1.inc(ins)

                @blk.vector
                def _(v, t=t):
                    v.wait_ge(SC_F1.h, SC_F1.n)
                    v.tensor_mul(s1g, s1tm[:, t * D:(t + 1) * D], g1t)
                    v.drain()
                    ins = v.tensor_add(xof[t % 2], x_sb[:, t * D:(t + 1) * D], s1g)
                    VE_F1.inc(ins)
                    v.drain()

                @blk.sync
                def _(sync, t=t):
                    sync.wait_ge(VE_F1.h, VE_F1.n)
                    dst = out_hbm.rearrange("(t p) d -> t p d", p=128)[t]
                    dmaN(sync, STP[t % 2], dst, xof[t % 2])
                    st_counts[t % 2].append(STP[t % 2].n)

            final_waits += [STP[0], STP[1]]

        @blk.sync
        def _(sync):
            for s_ in final_waits:
                sync.wait_ge(s_.h, s_.n)

            def store(dst, src_ap):
                dmaN(sync, ST, dst, src_ap)
            if t_xn0f is not None:
                store(t_xn0f, xn0f[:])
            if t_negc is not None:
                store(t_negc, negc[:])
            if t_comb is not None:
                store(t_comb, comb[:])
            if t_g is not None and stage >= 2:
                store(t_g, ar2[:, 0:NT * D])
            if t_x1f is not None and stage >= 3:
                store(t_x1f, fm2[:])
            if t_vals is not None and stage >= 5:
                store(t_vals, ar2[:])
            if t_s1 is not None and stage >= 6:
                store(t_s1, fm2[:])
            if stage < 7:
                store(out_hbm.rearrange("(t p) d -> p t d", p=128),
                      x_sb[:].rearrange("p (t d) -> p t d", d=D))
            sync.wait_ge(ST.h, ST.n)

    return nc, tap_outs



def _host_inputs(core, inp):
    bf = ml_dtypes.bfloat16
    f32 = np.float32
    A = lambda x: np.ascontiguousarray(np.asarray(x, f32))
    d = {}
    d["x"] = A(inp["L0_attn_src"][core])
    d["ident"] = np.eye(128, dtype=bf)
    jj = np.arange(128)[:, None]; ii = np.arange(128)[None, :]
    d["masks"] = np.concatenate([(jj >= ii).astype(bf), (jj <= ii).astype(bf)], axis=1)
    d["gate_w"] = A(inp["L0_ffn_gate_w"])
    bcast = lambda vec: np.ascontiguousarray(np.broadcast_to(np.asarray(vec, f32)[None, :], (128, 1024)))
    bcastb = lambda vec: np.ascontiguousarray(np.broadcast_to(
        np.asarray(vec, f32).astype(bf)[None, :], (128, 1024)))
    d["ln1g"] = bcast(inp["L0_attn_ln1_g"]); d["ln1b"] = bcast(inp["L0_attn_ln1_b"])
    d["ln2g"] = bcastb(inp["L0_attn_ln2_g"]); d["ln2b"] = bcastb(inp["L0_attn_ln2_b"])
    d["ffn_w1"] = A(inp["L0_attn_W1"]); d["ffn_w2"] = A(inp["L0_attn_W2"])
    d["ffn_b1"] = np.ascontiguousarray(np.asarray(inp["L0_attn_b1"], f32).reshape(32, 128).T)
    d["ffn_b2"] = bcastb(inp["L0_attn_b2"])
    d["moe_w1"] = A(inp["L0_ffn_w1"]); d["moe_w2"] = A(inp["L0_ffn_w2"])
    d["moe_b1"] = np.ascontiguousarray(np.asarray(inp["L0_ffn_b1"], f32).reshape(8, 32, 128).transpose(0, 2, 1))
    d["moe_b2"] = A(inp["L0_ffn_b2"])
    d["wk"] = A(inp["L1_attn_W_K"]); d["wv"] = A(inp["L1_attn_W_V"]); d["wq"] = A(inp["L1_attn_W_Q"])
    fmcols = lambda vec: np.ascontiguousarray(np.asarray(vec, f32).reshape(8, 128).T)
    d["snw"] = fmcols(inp["L1_attn_store_norm_w"])
    d["rnw"] = fmcols(inp["L1_attn_retrieve_norm_w"])
    d["knw"] = np.ascontiguousarray(np.broadcast_to(
        np.tile(np.asarray(inp["L1_attn_k_norm_w"], f32), 16)[None, :], (128, 1024)))
    d["qnw"] = np.ascontiguousarray(np.broadcast_to(
        np.tile(np.asarray(inp["L1_attn_q_norm_w"], f32), 16)[None, :], (128, 1024)))
    d["mw0"] = A(inp["L1_attn_mem_W0"]); d["mw1"] = A(inp["L1_attn_mem_W1"])
    d["atw"] = np.ascontiguousarray(np.concatenate(
        [np.asarray(inp["L1_attn_alpha_w"], f32), np.asarray(inp["L1_attn_theta_w"], f32)], axis=1))
    atb = np.concatenate([np.asarray(inp["L1_attn_alpha_b"], f32),
                          np.asarray(inp["L1_attn_theta_b"], f32)])
    d["atb"] = np.ascontiguousarray(np.broadcast_to(atb[None, :], (128, 32)))
    d["wqt"] = np.ascontiguousarray(np.asarray(inp["L1_ffn_weight_q"]).T)
    d["wsc"] = np.ascontiguousarray(np.broadcast_to(
        np.asarray(inp["L1_ffn_weight_scale"], f32)[None, :], (128, 1024)))
    d["wqb"] = np.ascontiguousarray(np.broadcast_to(
        np.asarray(inp["L1_ffn_bias"], f32)[None, :], (128, 1024)))
    return d


def kernel(**inputs):
    nc, _ = build_program()
    in_maps = [_host_inputs(c, inputs) for c in range(8)]
    res = run_bass_kernel_spmd(nc, in_maps, core_ids=list(range(8)))
    return np.stack([r["out"] for r in res.results]).astype(np.float32)


# version 06
# speedup vs baseline: 1.3151x; 1.0414x over previous
from contextlib import ExitStack

import numpy as np
import ml_dtypes

import concourse.bass as bass
import concourse.mybir as mybir
from concourse.bass_utils import run_bass_kernel_spmd

F32 = mybir.dt.float32
BF16 = mybir.dt.bfloat16
I8 = mybir.dt.int8
AF = mybir.ActivationFunctionType
ALU = mybir.AluOpType
AX = mybir.AxisListType

S = 2048
D = 1024
NT = 16
NC = 8
H = 16
HD = 64
LN_EPS = 1e-5
GROUPS = [(0, 6), (6, 11), (11, 16)]
RING = 6


class Sem:
    def __init__(self, nc, es, name):
        self.h = es.enter_context(nc.semaphore(name))
        self.n = 0

    def inc(self, instr, k=1):
        instr.then_inc(self.h, k)
        self.n += k
        return self.n


def build_program(taps=(), stage=99):
    taps = set(taps)
    nc = bass.Bass("TRN2", target_bir_lowering=False, debug=False)
    es = ExitStack()
    sems = []
    pre_sems = {}

    def sem(name):
        if name in pre_sems:
            return pre_sems.pop(name)
        s = Sem(nc, es, name)
        sems.append(s)
        return s

    sb = lambda name, shape, dt=F32: es.enter_context(nc.sbuf_tensor(name, shape, dt))

    def _icount():
        n = 0
        for _ in nc.all_instructions():
            n += 1
        return n

    def dmaN(eng, s, out, in_):
        before = _icount()
        ins = eng.dma_start(out=out, in_=in_)
        ins.then_inc(s.h, 16)
        s.n += 16 * (_icount() - before)
        return ins
    dr = lambda name, shape, dt=F32, kind="ExternalInput": nc.dram_tensor(name, shape, dt, kind=kind).ap()
    tap_outs = {}

    def tap(name, shape, dt=F32):
        if name in taps:
            tap_outs[name] = dr("tap_" + name, shape, dt, kind="ExternalOutput")
            return tap_outs[name]
        return None

    x_in = dr("x", [S, D])
    out_hbm = dr("out", [S, D], kind="ExternalOutput")
    ident_in = dr("ident", [128, 128], BF16)
    masks_in = dr("masks", [128, 256], BF16)
    gate_w_in = dr("gate_w", [D, 8])
    ln1g_in = dr("ln1g", [128, D]); ln1b_in = dr("ln1b", [128, D])
    ln2g_in = dr("ln2g", [128, D], BF16); ln2b_in = dr("ln2b", [128, D], BF16)
    ffn_w1_in = dr("ffn_w1", [D, 4096]); ffn_w2_in = dr("ffn_w2", [4096, D])
    ffn_b1_in = dr("ffn_b1", [128, 32]); ffn_b2_in = dr("ffn_b2", [128, D], BF16)
    moe_w1_in = dr("moe_w1", [8, D, 4096]); moe_w2_in = dr("moe_w2", [8, 4096, D])
    moe_b1_in = dr("moe_b1", [8, 128, 32]); moe_b2_in = dr("moe_b2", [8, D])
    wk_in = dr("wk", [D, D]); wv_in = dr("wv", [D, D]); wq_in = dr("wq", [D, D])
    snw_in = dr("snw", [128, 8]); rnw_in = dr("rnw", [128, 8])
    knw_in = dr("knw", [128, D]); qnw_in = dr("qnw", [128, D])
    mw0_in = dr("mw0", [64, 256]); mw1_in = dr("mw1", [256, 64])
    atw_in = dr("atw", [D, 32]); atb_in = dr("atb", [128, 32])
    wqt_in = dr("wqt", [D, D], I8)
    wsc_in = dr("wsc", [128, D]); wqb_in = dr("wqb", [128, D])
    scr_k = dr("scr_k", [H, S, HD], BF16, kind="Internal")
    scr_q = dr("scr_q", [H, S, HD], BF16, kind="Internal")

    with es:
        x_sb = sb("x_sb", [128, NT * D])
        ar1 = sb("ar1", [128, 16384], BF16)
        fm2 = sb("fm2", [128, 16384], BF16)
        ar2 = sb("ar2", [128, 16384], BF16)
        ar3 = sb("ar3", [128, 6144])
        vring = sb("vring", [128, RING * H * 65], BF16)
        attn_sb = sb("attn_sb", [128, 1280], BF16)
        ident = sb("ident_sb", [128, 128], BF16)
        masks = sb("masks_sb", [128, 256], BF16)
        cst = sb("cst", [128, 8])
        stat0 = sb("stat0", [128, NT * 2])
        stat1 = sb("stat1", [128, NT * 2])
        rstd0 = sb("rstd0", [128, NT])
        rstd1 = sb("rstd1", [128, NT])
        srr = sb("srr", [128, NT])
        sqrtv = sb("sqrtv", [128, NT])
        comb = sb("comb", [128, NT * 8])
        negc = sb("negc", [128, NT * H])
        gwf32 = sb("gwf32", [128, NC * 8])
        b1moe = sb("b1moe", [128, 8 * 32])
        b1ffn = sb("b1ffn", [128, 32])
        atheta = sb("atheta", [128, NT * 32], BF16)
        amrep = sb("amrep", [128, 16])
        mw0b = sb("mw0b", [64, 256], BF16)
        mw1b = sb("mw1b", [128, 128], BF16)
        mw1T = sb("mw1T", [64, 256], BF16)
        onesb = sb("onesb", [128, 2], BF16)
        ident32 = sb("ident32_sb", [128, 128])
        small = sb("small", [128, 64])
        bnsc = sb("bnsc", [128, 12])

        t_xn0f = tap("xn0f", [128, NC * S], BF16)
        t_negc = tap("negc", [128, NT * H])
        t_comb = tap("comb", [128, NT * 8])
        t_g = tap("g", [128, NT * D], BF16)
        t_x1f = tap("x1f", [128, 16384], BF16)
        t_xl0 = tap("xl0", [128, NT * D])
        t_vals = tap("vals", [128, 16384], BF16)
        t_xn1f = tap("xn1f", [128, 16384], BF16)
        t_s1 = tap("s1", [128, 16384], BF16)

        LD = sem("LD"); LG = sem("LG")
        VE_PRE = sem("VE_PRE"); SC_SQ = sem("SC_SQ"); VE_RS = sem("VE_RS")
        VE_TILE = sem("VE_TILE"); PE_TR = sem("PE_TR"); VE_EVAC = sem("VE_EVAC")
        PE_GATE = sem("PE_GATE"); VE_LG = sem("VE_LG"); SC_SIG = sem("SC_SIG"); VE_COMB = sem("VE_COMB")
        VE_CBB = sem("VE_CBB"); PE_CT = sem("PE_CT"); VE_CT = sem("VE_CT")
        LW1a = sem("LW1a"); LW1b = sem("LW1b"); LW2 = sem("LW2"); FW2 = sem("FW2")
        PE_H = sem("PE_H"); SC_H = sem("SC_H"); PE_O = sem("PE_O"); VE_O = sem("VE_O")
        SC_SIL = sem("SC_SIL"); PE_XI = sem("PE_XI"); VE_XI = sem("VE_XI")
        PE_S = sem("PE_S"); SC_E = sem("SC_E"); PE_T = sem("PE_T"); VE_P = sem("VE_P")
        PE_A = sem("PE_A"); VE_Y = sem("VE_Y"); VE_L1 = sem("VE_L1"); SC_L1 = sem("SC_L1")
        VE_X1 = sem("VE_X1"); PE_X1 = sem("PE_X1"); VE_XE = sem("VE_XE"); VE_RB = sem("VE_RB")
        VE_EP = sem("VE_EP"); SC_L2 = sem("SC_L2")
        LDD = sem("LDD"); LDG = sem("LDG"); VE_D = sem("VE_D"); SC_D = sem("SC_D"); PE_D = sem("PE_D")
        VE_DE = sem("VE_DE"); SD = sem("SD"); PE_AM = sem("PE_AM"); VE_AM = sem("VE_AM")
        LKH = sem("LKH"); VE_E = sem("VE_E"); PE_E = sem("PE_E"); SC_E2 = sem("SC_E2")
        GP_E = sem("GP_E"); VE_S1 = sem("VE_S1")
        LF = sem("LF"); VE_F = sem("VE_F"); PE_F = sem("PE_F"); SC_F = sem("SC_F")
        ST = sem("ST")

        for _n in ["VE_WF", "VE_K", "SC_AT", "VE_AT", "SDa", "SDb", "LKH2", "LQH2",
                   "VE_E1", "PE_E1", "SC_E1", "GP_E1", "VE_S1b", "LF2", "LF2g",
                   "VE_F1", "PE_F1", "SC_F1", "STa", "STb"]:
            pre_sems[_n] = Sem(nc, es, _n)
            sems.append(pre_sems[_n])
        for s_ in sems:
            nc.sync.sem_clear(s_.h)
        nc.all_engine_barrier()

        PH_pre = nc.sync
        blk = es.enter_context(nc.Block())
        PH = sem("PH")
        nc.sync.sem_clear(PH.h)
        nc.all_engine_barrier()
        ph_state = [0]

        def phase_join():
            ph_state[0] += 5
            tgt = ph_state[0]
            for ename in ("vector", "scalar", "tensor", "gpsimd", "sync"):
                def mk(e, tgt=tgt):
                    e.drain().then_inc(PH.h, 1)
                    e.wait_ge(PH.h, tgt)
                getattr(blk, ename)(mk)

        @blk.vector
        def _(v):
            for _ in range(48):
                v.memset(ar1[:], 0.0)
                v.drain()

        xn0f = ar1
        xn0t = fm2[:, 0:D]
        xn0t_b = fm2[:, D:2 * D]
        xn0t32 = ar3[:, 0:D]
        sqt = ar3[:, D:2 * D]
        lgs = ar3[:, 2048:2176]
        mlg = ar3[:, 2176:2304]
        gst = ar3[:, 2304:2368]
        x32fm = [ar3[:, 2432:3456], ar3[:, 3456:4480]]
        combb = fm2[:, 2 * D:2 * D + 128]
        combT = vring[0:8, 2048:4096]
        b2moeb = vring[0:8, 4096:5120]

        es_psA = ExitStack()
        p_tr = [es_psA.enter_context(nc.psum_tensor("p_tr0", [128, 1024], BF16)),
                es_psA.enter_context(nc.psum_tensor("p_tr1", [128, 1024], BF16))]
        p_gate = es_psA.enter_context(nc.psum_tensor("p_gate", [128, 512]))
        p_t32 = [es_psA.enter_context(nc.psum_tensor("p_t32a", [128, 1024])),
                 es_psA.enter_context(nc.psum_tensor("p_t32b", [128, 1024]))]

        @blk.sync
        def _(sync):
            x3 = x_in.rearrange("(t p) d -> p t d", p=128)
            dmaN(sync, LD, x_sb[:].rearrange("p (t d) -> p t d", d=D), x3)
            dmaN(sync, LD, ident[:], ident_in)
            dmaN(sync, LD, masks[:], masks_in)
            dmaN(sync, LD, b1moe[:].rearrange("p (e c) -> p e c", c=32),
                 moe_b1_in.rearrange("e p c -> p e c"))
            dmaN(sync, LD, b1ffn[:], ffn_b1_in)

        @blk.gpsimd
        def _(gp):
            gw = gate_w_in.rearrange("(c p) e -> p c e", p=128)
            dmaN(gp, LG, gwf32[:].rearrange("p (c e) -> p c e", e=8), gw)
            dmaN(gp, LG, b2moeb.rearrange("e (o n) -> e o n", o=1),
                 moe_b2_in.rearrange("e (o n) -> e o n", o=1))

        @blk.vector
        def _(v):
            v.memset(cst[:, 0:1], LN_EPS)
            v.memset(cst[:, 1:2], 1e-6)
            v.memset(cst[:, 2:3], 1e-8)
            v.memset(cst[:, 3:4], 1e-34)
            v.memset(cst[:, 4:5], 1.0 / 64.0)
            v.memset(cst[:, 5:6], 1.0)
            v.memset(cst[:, 6:7], -0.5)
            v.memset(cst[:, 7:8], 0.3989422804014327)
            v.memset(onesb[:], 1.0)
            v.wait_ge(LD.h, LD.n)
            v.tensor_copy(ident32[:], ident[:])
            v.drain()
            for t in range(NT):
                xt = x_sb[:, t * D:(t + 1) * D]
                v.bn_stats(bnsc[:, 0:6], xt[:, 0:512])
                v.bn_stats(bnsc[:, 6:12], xt[:, 512:1024])
                v.drain()
                v.bn_aggr(stat0[:, 2 * t:2 * t + 2], bnsc[:].rearrange("p (g s) -> p g s", s=12))
                v.drain()
            VE_PRE.inc(v.drain())

        @blk.scalar
        def _(s):
            s.wait_ge(VE_PRE.h, VE_PRE.n)
            ins = s.activation(out=sqrtv[:].rearrange("p (t o) -> p t o", o=1),
                               in_=stat0[:].rearrange("p (t s) -> p t s", s=2)[:, :, 1:2],
                               func=AF.Sqrt, bias=cst[:, 0:1], scale=1.0)
            SC_SQ.inc(ins)

        @blk.vector
        def _(v):
            v.wait_ge(SC_SQ.h, SC_SQ.n)
            VE_RS.inc(v.reciprocal(rstd0[:], sqrtv[:]))

        def comb_pt1(v, t):
            v.wait_ge(PE_GATE.h, t + 1)
            lg = lgs[:, 0:8]; ml = mlg[:, t * 8:(t + 1) * 8]
            m1 = gst[:, 4 * t:4 * t + 1]; m2 = gst[:, 4 * t + 1:4 * t + 2]
            w2c = gst[:, 4 * t + 2:4 * t + 3]
            cm = comb[:, t * 8:(t + 1) * 8]
            v.tensor_copy(lg, p_gate[:, 0:8])
            v.drain()
            ins = v.reduce_max(m1, lg, axis=AX.X)
            VE_LG.inc(ins)
            v.drain()
            v.tensor_scalar(cm, lg, m1, None, op0=ALU.is_ge)
            v.drain()
            v.scalar_tensor_tensor(ml, cm, -1e30, lg, op0=ALU.mult, op1=ALU.add)
            v.drain()
            v.reduce_max(m2, ml, axis=AX.X)
            v.drain()
            ins = v.tensor_sub(w2c, m2, m1)
            VE_LG.inc(ins)
            v.drain()

        @blk.vector
        def _(v):
            v.wait_ge(VE_RS.h, VE_RS.n)
            for t in range(NT):
                if t >= 1:
                    v.wait_ge(PE_TR.h, t)
                xt = x_sb[:, t * D:(t + 1) * D]
                mu = stat0[:, 2 * t:2 * t + 1]
                rs = rstd0[:, t:t + 1]
                dst = xn0t if t % 2 == 0 else xn0t_b
                v.tensor_scalar(dst, xt, mu, rs, op0=ALU.subtract, op1=ALU.mult)
                v.tensor_scalar(xn0t32, xt, mu, rs, op0=ALU.subtract, op1=ALU.mult)
                v.drain()
                v.tensor_mul(sqt, xn0t32, xn0t32)
                v.drain()
                ins = v.tensor_reduce(
                    negc[:, t * H:(t + 1) * H].rearrange("p (h o) -> p h o", o=1),
                    sqt.rearrange("p (h d) -> p h d", d=64),
                    axis=AX.X, op=ALU.add, negate=True)
                VE_TILE.inc(ins)
                if t >= 1:
                    v.wait_ge(PE_TR.h, t)
                    dstf = xn0f[:].rearrange("p (c s) -> p c s", s=S)[:, :, (t - 1) * 128:t * 128]
                    v.tensor_copy(dstf, p_tr[(t - 1) % 2][:].rearrange("p (c j) -> p c j", j=128))
                    if t >= 2:
                        v.wait_ge(PE_GATE.h, t - 1)
                    VE_EVAC.inc(v.tensor_copy(x32fm[(t - 1) % 2], p_t32[(t - 1) % 2][:]))
                if t >= 2:
                    comb_pt1(v, t - 2)
            v.wait_ge(PE_TR.h, NT)
            dstf = xn0f[:].rearrange("p (c s) -> p c s", s=S)[:, :, (NT - 1) * 128:NT * 128]
            v.tensor_copy(dstf, p_tr[(NT - 1) % 2][:].rearrange("p (c j) -> p c j", j=128))
            v.wait_ge(PE_GATE.h, NT - 1)
            VE_EVAC.inc(v.tensor_copy(x32fm[(NT - 1) % 2], p_t32[(NT - 1) % 2][:]))
            comb_pt1(v, NT - 2)
            comb_pt1(v, NT - 1)
            v.wait_ge(VE_TILE.h, NT)
            v.drain()
            ins = v.tensor_scalar(negc[:], negc[:], -10.0, None, op0=ALU.add)
            VE_TILE.inc(ins)
            v.drain()

        @blk.tensor
        def _(t_):
            t_.wait_ge(LG.h, LG.n)

            def gate_mm(t):
                t_.wait_ge(VE_EVAC.h, t + 1)
                if t >= 1:
                    t_.wait_ge(VE_LG.h, 2 * t - 1)
                ins = None
                for c in range(NC):
                    ins = t_.matmul(p_gate[:, 0:8], x32fm[t % 2][:, c * 128:(c + 1) * 128],
                                    gwf32[:, c * 8:(c + 1) * 8], start=(c == 0), stop=(c == NC - 1))
                PE_GATE.inc(ins)

            for t in range(NT):
                t_.wait_ge(VE_TILE.h, t + 1)
                if t >= 2:
                    t_.wait_ge(VE_EVAC.h, t - 1)
                src = xn0t if t % 2 == 0 else xn0t_b
                dstp = p_tr[t % 2]
                ins = None
                for c in range(NC):
                    ins = t_.transpose(dstp[:, c * 128:(c + 1) * 128], src[:, c * 128:(c + 1) * 128], ident[:])
                for c in range(NC):
                    ins = t_.transpose(p_t32[t % 2][:, c * 128:(c + 1) * 128],
                                       xn0t32[:, c * 128:(c + 1) * 128], ident32[:])
                PE_TR.inc(ins)
                if t >= 1:
                    gate_mm(t - 1)
            gate_mm(NT - 1)


        @blk.scalar
        def _(s):
            for t in range(NT):
                s.wait_ge(VE_LG.h, 2 * (t + 1))
                ins = s.activation(out=gst[:, 4 * t + 2:4 * t + 3],
                                   in_=gst[:, 4 * t + 2:4 * t + 3], func=AF.Sigmoid)
                SC_SIG.inc(ins)

        @blk.vector
        def _(v):
            for t in range(NT):
                v.wait_ge(SC_SIG.h, t + 1)
                m2 = gst[:, 4 * t + 1:4 * t + 2]; w2c = gst[:, 4 * t + 2:4 * t + 3]
                w1c = gst[:, 4 * t + 3:4 * t + 4]
                ml = mlg[:, t * 8:(t + 1) * 8]
                cm = comb[:, t * 8:(t + 1) * 8]
                v.tensor_scalar(w1c, w2c, -1.0, 1.0, op0=ALU.mult, op1=ALU.add)
                v.tensor_scalar(ml, ml, m2, None, op0=ALU.is_ge)
                v.drain()
                v.tensor_scalar(cm, cm, w1c, None, op0=ALU.mult)
                v.drain()
                ins = v.scalar_tensor_tensor(cm, ml, w2c, cm, op0=ALU.mult, op1=ALU.add)
                VE_COMB.inc(ins)
                v.drain()

        @blk.vector
        def _(v):
            v.wait_ge(VE_COMB.h, VE_COMB.n)
            VE_CBB.inc(v.tensor_copy(combb, comb[:]))
            v.drain()

        @blk.tensor
        def _(t_):
            t_.wait_ge(VE_CBB.h, VE_CBB.n)
            ins = None
            for t in range(NT):
                dstp = p_tr[t // 8][0:8, (t % 8) * 128:(t % 8 + 1) * 128]
                ins = t_.transpose(dstp, combb[:, t * 8:(t + 1) * 8], ident[:])
            PE_CT.inc(ins)

        @blk.vector
        def _(v):
            v.wait_ge(PE_CT.h, PE_CT.n)
            v.tensor_copy(combT[:, 0:1024], p_tr[0][0:8, :])
            VE_CT.inc(v.tensor_copy(combT[:, 1024:2048], p_tr[1][0:8, :]))
            v.drain()

        final_waits = [VE_COMB, VE_CT, VE_TILE]

        def emit_mlp(groups, n_experts, w1_dram, w2_dram, rhs_fm, hbuf, w2buf, w1buf,
                     act_fn, bias_col_fn, evac_fn, extra_mm2=None, gate_start=None,
                     tail_pe=None, tail_sc=None, tail_ve=None, h_serial=False):
            LW1P = [LW1a, LW1b]
            base = dict(LW1a=LW1a.n, LW1b=LW1b.n, LW2=LW2.n, FW2=FW2.n, PE_H=PE_H.n,
                        SC_H=SC_H.n, PE_O=PE_O.n, VE_O=VE_O.n)
            n_fq = 4
            n_mt = 8

            @blk.gpsimd
            def _(gp):
                mt_i = 0
                fq_i = 0
                for gi, (t0, t1) in enumerate(groups):
                    for e in range(n_experts):
                        for fq in range(n_fq):
                            if fq_i >= 1:
                                gp.wait_ge(FW2.h, base["FW2"] + fq_i)
                            w2s = w2_dram[e] if n_experts > 1 else w2_dram
                            src = w2s[fq * 1024:(fq + 1) * 1024, :].rearrange("(c p) n -> p c n", p=128)
                            dmaN(gp, LW2, w2buf.rearrange("p (c n) -> p c n", n=1024), src)
                            fq_i += 1
                            for mt in range(n_mt):
                                if mt_i >= 2:
                                    gp.wait_ge(PE_H.h, base["PE_H"] + mt_i - 1)
                                w1s = w1_dram[e] if n_experts > 1 else w1_dram
                                m0 = fq * 1024 + mt * 128
                                srcw = w1s[:, m0:m0 + 128].rearrange("(c p) m -> p c m", p=128)
                                par = mt_i % 2
                                dmaN(gp, LW1P[par],
                                     w1buf[:, par * 1024:(par + 1) * 1024].rearrange(
                                         "p (c m) -> p c m", m=128), srcw)
                                mt_i += 1

            @blk.tensor
            def _(t_):
                if gate_start is not None:
                    for s_, n_ in gate_start:
                        t_.wait_ge(s_.h, n_)
                mt_i = 0
                fq_i = 0
                o_i = 0
                for gi, (t0, t1) in enumerate(groups):
                    ntok = (t1 - t0) * 128
                    nspans = [(i * 512, min(512, ntok - i * 512)) for i in range((ntok + 511) // 512)]
                    if tail_pe is not None and gi >= 1:
                        tail_pe(t_, gi - 1)
                    for e in range(n_experts):
                        for fq in range(n_fq):
                            for mt in range(n_mt):
                                par_ = mt_i % 2
                                t_.wait_ge(LW1P[par_].h,
                                           base["LW1a" if par_ == 0 else "LW1b"] + 16 * (mt_i // 2 + 1))
                                if h_serial:
                                    if mt_i >= 1:
                                        t_.wait_ge(SC_H.h, base["SC_H"] + mt_i)
                                elif mt_i >= 2:
                                    t_.wait_ge(SC_H.h, base["SC_H"] + mt_i - 1)
                                par = 0 if h_serial else mt_i % 2
                                w1slice = w1buf[:, (mt_i % 2) * 1024:(mt_i % 2 + 1) * 1024]
                                ins = None
                                for (soff, slen) in nspans:
                                    for kc in range(NC):
                                        ins = t_.matmul(
                                            p_h[par][:, soff:soff + slen],
                                            w1slice[:, kc * 128:(kc + 1) * 128],
                                            rhs_fm[:, kc * S + t0 * 128 + soff:
                                                   kc * S + t0 * 128 + soff + slen],
                                            start=(kc == 0), stop=(kc == NC - 1))
                                PE_H.inc(ins)
                                mt_i += 1
                            t_.wait_ge(LW2.h, base["LW2"] + 16 * (fq_i + 1))
                            t_.wait_ge(SC_H.h, base["SC_H"] + mt_i)
                            for tt in range(t0, t1):
                                if o_i >= 2:
                                    t_.wait_ge(VE_O.h, base["VE_O"] + o_i - 1)
                                par = o_i % 2
                                lt = (tt - t0) * 128
                                ins = None
                                has_extra = (extra_mm2 is not None and e == n_experts - 1
                                             and fq == n_fq - 1)
                                for ns in range(2):
                                    for fc in range(8):
                                        stop = (fc == 7 and not has_extra)
                                        ins = t_.matmul(
                                            p_o[par][:, ns * 512:(ns + 1) * 512],
                                            hbuf[:, fc * 1024 + lt: fc * 1024 + lt + 128],
                                            w2buf[:, fc * 1024 + ns * 512: fc * 1024 + (ns + 1) * 512],
                                            start=(fc == 0), stop=stop)
                                if has_extra:
                                    ins = extra_mm2(t_, tt, par)
                                PE_O.inc(ins)
                                o_i += 1
                            FW2.inc(t_.drain())
                            fq_i += 1
                if tail_pe is not None:
                    tail_pe(t_, len(groups) - 1)

            @blk.scalar
            def _(s):
                mt_i = 0
                for gi, (t0, t1) in enumerate(groups):
                    ntok = (t1 - t0) * 128
                    if tail_sc is not None and gi >= 1:
                        tail_sc(s, gi - 1)
                    for e in range(n_experts):
                        for fq in range(n_fq):
                            for mt in range(n_mt):
                                s.wait_ge(PE_H.h, base["PE_H"] + mt_i + 1)
                                par = 0 if h_serial else mt_i % 2
                                ins = s.activation(out=hbuf[:, mt * 1024:mt * 1024 + ntok],
                                                   in_=p_h[par][:, 0:ntok],
                                                   func=act_fn, bias=bias_col_fn(e, fq, mt), scale=1.0)
                                SC_H.inc(ins)
                                mt_i += 1
                if tail_sc is not None:
                    tail_sc(s, len(groups) - 1)

            @blk.vector
            def _(v):
                o_i = 0
                for gi, (t0, t1) in enumerate(groups):
                    if tail_ve is not None and gi >= 1:
                        tail_ve(v, gi - 1)
                    for e in range(n_experts):
                        for fq in range(n_fq):
                            for tt in range(t0, t1):
                                v.wait_ge(PE_O.h, base["PE_O"] + o_i + 1)
                                par = o_i % 2
                                evac_fn(v, gi, e, fq, tt, p_o[par][:], (e == 0 and fq == 0))
                                VE_O.inc(v.drain())
                                o_i += 1
                if tail_ve is not None:
                    tail_ve(v, len(groups) - 1)
            return base

        if stage >= 2:
            phase_join()
            es_psA.close()
            es_psM = ExitStack()
            p_h = [es_psM.enter_context(nc.psum_tensor("p_h0", [128, 1024])),
                   es_psM.enter_context(nc.psum_tensor("p_h1", [128, 1024]))]
            p_o = [es_psM.enter_context(nc.psum_tensor("p_o0", [128, 1024])),
                   es_psM.enter_context(nc.psum_tensor("p_o1", [128, 1024]))]

            g_moe = ar2
            acc = ar3
            w2buf_m = fm2[:, 0:8192]
            hbuf_m = fm2[:, 8192:16384]
            w1buf_m = vring[:, 0:2048]

            sil_cum = []
            tot = 0
            for (t0, t1) in GROUPS:
                tot += (t1 - t0)
                sil_cum.append(tot)
            sc_sil_base = SC_SIL.n

            def moe_evac(v, gi, e, fq, tt, p_ap, first):
                if first and gi >= 1:
                    v.wait_ge(SC_SIL.h, sc_sil_base + sil_cum[gi - 1])
                t0 = GROUPS[gi][0]
                accs = acc[:, (tt - t0) * 1024:(tt - t0 + 1) * 1024]
                cmcol = comb[:, tt * 8 + e:tt * 8 + e + 1]
                if first:
                    v.tensor_scalar(accs, p_ap, cmcol, None, op0=ALU.mult)
                else:
                    v.scalar_tensor_tensor(accs, p_ap, cmcol, accs, op0=ALU.mult, op1=ALU.add)

            def moe_b2_mm(t_, tt, par):
                ins = None
                for ns in range(2):
                    ins = t_.matmul(p_o[par][:, ns * 512:(ns + 1) * 512],
                                    combT[:, tt * 128:(tt + 1) * 128],
                                    b2moeb[:, ns * 512:(ns + 1) * 512],
                                    start=False, stop=True)
                return ins

            ve_o_base_holder = [0]

            def moe_tail_sc(s, gi):
                t0, t1 = GROUPS[gi]
                s.wait_ge(VE_O.h, ve_o_base_holder[0] + 8 * 4 * sil_cum[gi])
                for tt in range(t0, t1):
                    ins = s.activation(out=g_moe[:, tt * D:(tt + 1) * D],
                                       in_=acc[:, (tt - t0) * 1024:(tt - t0 + 1) * 1024],
                                       func=AF.Silu)
                    SC_SIL.inc(ins)

            ve_o_base_holder[0] = VE_O.n
            moe_base = emit_mlp(GROUPS, 8, moe_w1_in, moe_w2_in, xn0f, hbuf_m, w2buf_m, w1buf_m,
                                AF.Gelu,
                                lambda e, fq, mt: b1moe[:, e * 32 + fq * 8 + mt:e * 32 + fq * 8 + mt + 1],
                                moe_evac, extra_mm2=moe_b2_mm,
                                gate_start=[(LG, LG.n), (VE_CT, VE_CT.n)],
                                tail_sc=moe_tail_sc)

            final_waits.append(SC_SIL)

        if stage >= 3:
            phase_join()
            es_psM.close()
            es_psB = ExitStack()
            p_s = es_psB.enter_context(nc.psum_tensor("p_s", [128, 1024]))
            p_tb = es_psB.enter_context(nc.psum_tensor("p_tb", [128, 1024], BF16))
            p_y = es_psB.enter_context(nc.psum_tensor("p_y", [128, 512]))
            p_x1 = es_psB.enter_context(nc.psum_tensor("p_x1", [128, 1024]))

            x1f = fm2
            ybuf = ar3[:, 0:D]
            ln1g_sb = ar3[:, D:2 * D]
            ln1b_sb = ar3[:, 2 * D:3 * D]
            vtmp = ar3[:, 3 * D:4 * D]
            p_sb = attn_sb[:, 0:640]
            pt_sb = attn_sb[:, 640:1280]

            def win(qt):
                j0 = max(0, qt - 2)
                j1 = min(NT - 1, qt + 2)
                return j0, j1

            @blk.sync
            def _(sync):
                dmaN(sync, LD, ln1g_sb, ln1g_in)
                dmaN(sync, LD, ln1b_sb, ln1b_in)
            ld_b = LD.n

            @blk.vector
            def _(v):
                v.wait_ge(LD.h, ld_b)
                v.memset(vring[:, 0:RING * H * 65], 1.0)
                v.tensor_copy(ident32[:], ident[:])
                v.drain()
                def build_vtile(j):
                    slot = j % RING
                    v.tensor_scalar(vtmp, x_sb[:, j * D:(j + 1) * D],
                                    stat0[:, 2 * j:2 * j + 1], rstd0[:, j:j + 1],
                                    op0=ALU.subtract, op1=ALU.mult)
                    v.drain()
                    ins = v.tensor_copy(
                        vring[:, slot * H * 65:(slot + 1) * H * 65].rearrange(
                            "p (h e) -> p h e", e=65)[:, :, 0:64],
                        vtmp.rearrange("p (h d) -> p h d", d=64))
                    VE_RB.inc(ins)
                    v.drain()
                for j in range(3):
                    build_vtile(j)
                for qt in range(NT):
                    j0, j1 = win(qt)
                    nw = j1 - j0 + 1
                    v.tensor_scalar(ybuf, x_sb[:, qt * D:(qt + 1) * D],
                                    stat0[:, 2 * qt:2 * qt + 1], rstd0[:, qt:qt + 1],
                                    op0=ALU.subtract, op1=ALU.mult)
                    VE_Y.inc(v.drain())
                    j = qt + 3
                    if j < NT:
                        if qt >= 3:
                            v.wait_ge(PE_A.h, qt * H)
                        build_vtile(j)
                    for h in range(H):
                        i = qt * H + h
                        v.wait_ge(PE_T.h, i + 1)
                        for jw in range(nw):
                            j_ = j0 + jw
                            srcp = p_tb[:, jw * 128:(jw + 1) * 128]
                            dstp = pt_sb[:, jw * 128:(jw + 1) * 128]
                            if j_ == qt - 2:
                                v.tensor_mul(dstp, srcp, masks[:, 0:128])
                            elif j_ == qt + 2:
                                v.tensor_mul(dstp, srcp, masks[:, 128:256])
                            else:
                                v.tensor_copy(dstp, srcp)
                        VE_P.inc(v.drain())
                        v.wait_ge(PE_A.h, i + 1)
                        iv = small[:, 0:1]
                        v.reciprocal(iv, p_y[:, 64:65])
                        v.drain()
                        ins = v.scalar_tensor_tensor(ybuf[:, h * 64:(h + 1) * 64], p_y[:, 0:64],
                                                     iv, ybuf[:, h * 64:(h + 1) * 64],
                                                     op0=ALU.mult, op1=ALU.add)
                        VE_Y.inc(ins)
                        v.drain()
                    v.bn_stats(bnsc[:, 0:6], ybuf[:, 0:512])
                    v.bn_stats(bnsc[:, 6:12], ybuf[:, 512:1024])
                    v.drain()
                    v.bn_aggr(small[:, 1:3], bnsc[:].rearrange("p (g s) -> p g s", s=12))
                    VE_L1.inc(v.drain())
                    v.wait_ge(SC_L1.h, qt + 1)
                    v.reciprocal(small[:, 4:5], small[:, 3:4])
                    v.drain()
                    v.tensor_scalar(ybuf, ybuf, small[:, 1:2], small[:, 4:5],
                                    op0=ALU.subtract, op1=ALU.mult)
                    v.drain()
                    v.tensor_mul(ybuf, ybuf, ln1g_sb)
                    v.drain()
                    v.tensor_add(ybuf, ybuf, ln1b_sb)
                    VE_X1.inc(v.drain())
                    v.wait_ge(PE_X1.h, qt + 1)
                    dstf = x1f[:].rearrange("p (c s) -> p c s", s=S)[:, :, qt * 128:(qt + 1) * 128]
                    ins = v.tensor_copy(dstf, p_x1[:].rearrange("p (c j) -> p c j", j=128))
                    VE_XE.inc(ins)
                    v.drain()

            @blk.scalar
            def _(s):
                for qt in range(NT):
                    j0, j1 = win(qt)
                    nw = j1 - j0 + 1
                    for h in range(H):
                        i = qt * H + h
                        s.wait_ge(PE_S.h, i + 1)
                        if i >= 1:
                            s.wait_ge(PE_T.h, i)
                        ins = s.activation(out=p_sb[:, 0:nw * 128], in_=p_s[:, 0:nw * 128],
                                           func=AF.Exp, bias=negc[:, qt * H + h:qt * H + h + 1],
                                           scale=1.0)
                        SC_E.inc(ins)
                    s.wait_ge(VE_L1.h, qt + 1)
                    ins = s.activation(out=small[:, 3:4], in_=small[:, 2:3],
                                       func=AF.Sqrt, bias=cst[:, 0:1], scale=1.0)
                    SC_L1.inc(ins)

            @blk.tensor
            def _(t_):
                for qt in range(NT):
                    j0, j1 = win(qt)
                    nw = j1 - j0 + 1
                    hc_of = lambda h: (h // 2) * S
                    for h in range(H):
                        i = qt * H + h
                        bp = (h % 2) * 64
                        if i >= 1:
                            t_.wait_ge(SC_E.h, i)
                        q_ap = xn0f[bp:bp + 64, hc_of(h) + qt * 128: hc_of(h) + (qt + 1) * 128]
                        ins = None
                        ncols = nw * 128
                        for soff in range(0, ncols, 512):
                            slen = min(512, ncols - soff)
                            k_ap = xn0f[bp:bp + 64,
                                        hc_of(h) + j0 * 128 + soff: hc_of(h) + j0 * 128 + soff + slen]
                            ins = t_.matmul(p_s[:, soff:soff + slen], q_ap, k_ap,
                                            start=True, stop=True)
                        PE_S.inc(ins)
                        t_.wait_ge(SC_E.h, i + 1)
                        if i >= 1:
                            t_.wait_ge(VE_P.h, i)
                        ins = None
                        for jw in range(nw):
                            ins = t_.transpose(p_tb[:, jw * 128:(jw + 1) * 128],
                                               p_sb[:, jw * 128:(jw + 1) * 128], ident[:])
                        PE_T.inc(ins)
                        t_.wait_ge(VE_P.h, i + 1)
                        t_.wait_ge(VE_RB.h, min(qt + 3, NT))
                        t_.wait_ge(VE_Y.h, (qt + 1) + i)
                        ins = None
                        for jw in range(nw):
                            j_ = j0 + jw
                            slot = j_ % RING
                            ins = t_.matmul(p_y[:, 0:65],
                                            pt_sb[:, jw * 128:(jw + 1) * 128],
                                            vring[:, slot * H * 65 + h * 65: slot * H * 65 + (h + 1) * 65],
                                            start=(jw == 0), stop=(jw == nw - 1))
                        PE_A.inc(ins)
                    t_.wait_ge(VE_X1.h, qt + 1)
                    if qt >= 1:
                        t_.wait_ge(VE_XE.h, qt)
                    ins = None
                    for c in range(NC):
                        ins = t_.transpose(p_x1[:, c * 128:(c + 1) * 128],
                                           ybuf[:, c * 128:(c + 1) * 128], ident32[:])
                    PE_X1.inc(ins)

            final_waits += [VE_XE]

        if stage >= 4:
            phase_join()
            es_psB.close()
            es_psC = ExitStack()
            p_h0c = es_psC.enter_context(nc.psum_tensor("pc_h0", [128, 1024]))
            p_h = [p_h0c, p_h0c]
            p_o = [es_psC.enter_context(nc.psum_tensor("pc_o0", [128, 1024])),
                   es_psC.enter_context(nc.psum_tensor("pc_o1", [128, 1024]))]
            p_xb = es_psC.enter_context(nc.psum_tensor("p_xb", [128, 1024], BF16))

            acc = ar3
            w2buf_c = ar1[:, 0:8192]
            hbuf_c = ar1[:, 8192:16384]
            w1buf_c = vring[:, 0:2048]
            ln2g_sb = vring[:, 2048:3072]
            ln2b_sb = vring[:, 3072:4096]

            @blk.sync
            def _(sync):
                dmaN(sync, LD, ln2g_sb, ln2g_in)
                dmaN(sync, LD, ln2b_sb, ln2b_in)
                dmaN(sync, LD, vring[:, 4096:5120], ffn_b2_in)
            ld_c = LD.n
            b2f_sb = vring[:, 4096:5120]

            ep_cum = []
            tot = 0
            for (t0, t1) in GROUPS:
                tot += (t1 - t0)
                ep_cum.append(tot)
            ve_ep_base = VE_EP.n
            pe_xi_base = PE_XI.n
            ve_o_base_c = VE_O.n
            ve_l1_base = VE_L1.n

            def ffn_evac(v, gi, e, fq, tt, p_ap, first):
                if first and gi >= 1 and tt == GROUPS[gi][0]:
                    v.wait_ge(VE_EP.h, ve_ep_base + ep_cum[gi - 1])
                t0 = GROUPS[gi][0]
                accs = acc[:, (tt - t0) * 1024:(tt - t0 + 1) * 1024]
                if first:
                    v.tensor_copy(accs, p_ap)
                else:
                    v.tensor_add(accs, accs, p_ap)

            ffn_veo_cum = []
            tot = 0
            for (t0, t1) in GROUPS:
                tot += 4 * (t1 - t0)
                ffn_veo_cum.append(tot)

            def ffn_tail_pe(t_, gi):
                t0, t1 = GROUPS[gi]
                for tt in range(t0, t1):
                    k = (sum(b - a for a, b in GROUPS[:gi]) + (tt - t0))
                    t_.wait_ge(VE_O.h, ve_o_base_c + ffn_veo_cum[gi])
                    if k >= 1:
                        t_.wait_ge(VE_XI.h, k)
                    ins = None
                    for c in range(NC):
                        ins = t_.transpose(p_xb[:, c * 128:(c + 1) * 128],
                                           x1f[:, c * S + tt * 128: c * S + (tt + 1) * 128],
                                           ident[:])
                    PE_XI.inc(ins)

            def ffn_tail_sc(s, gi):
                t0, t1 = GROUPS[gi]
                for tt in range(t0, t1):
                    k = (sum(b - a for a, b in GROUPS[:gi]) + (tt - t0))
                    s.wait_ge(VE_L1.h, ve_l1_base + k + 1)
                    ins = s.activation(out=small[:, 8:9], in_=small[:, 7:8],
                                       func=AF.Sqrt, bias=cst[:, 0:1], scale=1.0)
                    SC_L2.inc(ins)

            def ffn_tail_ve(v, gi):
                t0, t1 = GROUPS[gi]
                for tt in range(t0, t1):
                    k = (sum(b - a for a, b in GROUPS[:gi]) + (tt - t0))
                    accs = acc[:, (tt - t0) * 1024:(tt - t0 + 1) * 1024]
                    v.wait_ge(PE_XI.h, pe_xi_base + k + 1)
                    v.tensor_add(accs, accs, p_xb[:])
                    VE_XI.inc(v.drain())
                    v.tensor_add(accs, accs, b2f_sb)
                    v.drain()
                    v.bn_stats(bnsc[:, 0:6], accs[:, 0:512])
                    v.bn_stats(bnsc[:, 6:12], accs[:, 512:1024])
                    v.drain()
                    v.bn_aggr(small[:, 6:8], bnsc[:].rearrange("p (g s) -> p g s", s=12))
                    VE_L1.inc(v.drain())
                    v.wait_ge(SC_L2.h, k + 1)
                    v.reciprocal(small[:, 9:10], small[:, 8:9])
                    v.drain()
                    v.tensor_scalar(accs, accs, small[:, 6:7], small[:, 9:10],
                                    op0=ALU.subtract, op1=ALU.mult)
                    v.drain()
                    v.tensor_mul(accs, accs, ln2g_sb)
                    v.drain()
                    v.tensor_add(accs, accs, ln2b_sb)
                    v.drain()
                    v.tensor_mul(accs, accs, ar2[:, tt * D:(tt + 1) * D])
                    v.drain()
                    ins = v.tensor_add(x_sb[:, tt * D:(tt + 1) * D],
                                       x_sb[:, tt * D:(tt + 1) * D], accs)
                    VE_EP.inc(ins)
                    v.drain()

            ffn_base = emit_mlp(GROUPS, 1, ffn_w1_in, ffn_w2_in, x1f, hbuf_c, w2buf_c, w1buf_c,
                                AF.Relu, lambda e, fq, mt: b1ffn[:, fq * 8 + mt:fq * 8 + mt + 1],
                                ffn_evac, gate_start=[(LD, ld_c)],
                                tail_pe=ffn_tail_pe, tail_sc=ffn_tail_sc, tail_ve=ffn_tail_ve,
                                h_serial=True)

            final_waits += [VE_EP]

        if stage >= 5:
            phase_join()
            es_psC.close()
            es_psD = ExitStack()
            p_d = [es_psD.enter_context(nc.psum_tensor("p_d0", [128, 1024])),
                   es_psD.enter_context(nc.psum_tensor("p_d1", [128, 1024]))]
            p_dt = [es_psD.enter_context(nc.psum_tensor("p_dt0", [128, 1024], BF16)),
                    es_psD.enter_context(nc.psum_tensor("p_dt1", [128, 1024], BF16))]
            p_am = es_psD.enter_context(nc.psum_tensor("p_am", [128, 512]))

            VE_WF = sem("VE_WF"); VE_K = sem("VE_K"); SC_AT = sem("SC_AT"); VE_AT = sem("VE_AT")
            SDP = [sem("SDa"), sem("SDb")]
            sd_counts_g = [[], []]
            nc2_clear = [VE_WF, VE_K, SC_AT, VE_AT]

            xn1nF = fm2
            wbD = ar1[:, 0:8192]
            atwb = vring[:, 0:256]
            xn1nt = [vring[:, 256:1280], vring[:, 1280:2304]]
            ktile_b = [vring[:, 2304:3328], vring[:, 3328:4352]]
            onesrow = vring[0:1, 4352:4480]
            amrow = vring[0:1, 4480:4496]
            keysc = ar3[:, 0:1024]
            ksq = ar3[:, 1024:2048]
            knw_sb = ar3[:, 2048:3072]
            qnw_sb = ar3[:, 3072:4096]
            at32 = ar3[:, 4096:4128]
            atb_sb = b1ffn
            snw_c = lambda c: small[:, 32 + c:33 + c]
            rnw_c = lambda c: small[:, 40 + c:41 + c]

            @blk.sync
            def _(sync):
                dmaN(sync, LDD, atb_sb[:], atb_in)
                dmaN(sync, LDD, small[:, 32:40], snw_in)
                dmaN(sync, LDD, small[:, 40:48], rnw_in)
                dmaN(sync, LDD, knw_sb, knw_in)
                dmaN(sync, LDD, qnw_sb, qnw_in)
            ldd_n = LDD.n

            @blk.gpsimd
            def _(gp):
                wkc = wk_in.rearrange("(c p) n -> p c n", p=128)
                dmaN(gp, LDG, wbD.rearrange("p (c n) -> p c n", n=1024), wkc)
            ldg_wk = LDG.n

            @blk.vector
            def _(v):
                for t in range(NT):
                    xt = x_sb[:, t * D:(t + 1) * D]
                    v.bn_stats(bnsc[:, 0:6], xt[:, 0:512])
                    v.bn_stats(bnsc[:, 6:12], xt[:, 512:1024])
                    v.drain()
                    v.bn_aggr(stat1[:, 2 * t:2 * t + 2], bnsc[:].rearrange("p (g s) -> p g s", s=12))
                    v.drain()
                VE_D.inc(v.drain())

            @blk.scalar
            def _(s):
                s.wait_ge(VE_D.h, VE_D.n)
                ins = s.activation(out=sqrtv[:].rearrange("p (t o) -> p t o", o=1),
                                   in_=stat1[:].rearrange("p (t s) -> p t s", s=2)[:, :, 1:2],
                                   func=AF.Sqrt, bias=cst[:, 0:1], scale=1.0)
                SC_D.inc(ins)

            @blk.vector
            def _(v):
                v.wait_ge(SC_D.h, SC_D.n)
                v.reciprocal(rstd1[:], sqrtv[:])
                v.drain()
                varS = stat1[:].rearrange("p (t s) -> p t s", s=2)[:, :, 1:2]
                v.tensor_tensor(small[:, 48:64].rearrange("p (t o) -> p t o", o=1), varS,
                                rstd1[:].rearrange("p (t o) -> p t o", o=1), op=ALU.mult)
                v.drain()
                v.tensor_mul(small[:, 48:64], small[:, 48:64], rstd1[:])
                VE_D.inc(v.drain())

            @blk.scalar
            def _(s):
                s.wait_ge(VE_D.h, VE_D.n)
                ins = s.activation(out=small[:, 0:16], in_=small[:, 48:64],
                                   func=AF.Sqrt, bias=cst[:, 1:2], scale=1.0)
                SC_D.inc(ins)

            @blk.vector
            def _(v):
                v.wait_ge(SC_D.h, SC_D.n)
                v.wait_ge(LDD.h, ldd_n)
                v.reciprocal(small[:, 16:32], small[:, 0:16])
                v.drain()
                v.tensor_mul(srr[:], small[:, 16:32], rstd1[:])
                v.drain()
                v.wait_ge(LDG.h, ldg_wk)
                for c in range(NC):
                    v.tensor_scalar(wbD[:, c * 1024:(c + 1) * 1024], wbD[:, c * 1024:(c + 1) * 1024],
                                    snw_c(c), None, op0=ALU.mult)
                VE_WF.inc(v.drain())
                for t in range(NT):
                    if t >= 2:
                        v.wait_ge(PE_D.h, t - 1)
                    v.tensor_scalar(xn1nt[t % 2], x_sb[:, t * D:(t + 1) * D],
                                    stat1[:, 2 * t:2 * t + 1], srr[:, t:t + 1],
                                    op0=ALU.subtract, op1=ALU.mult)
                    VE_D.inc(v.drain())
                    if t >= 1:
                        v.wait_ge(PE_D.h, t)
                        dstf = xn1nF[:].rearrange("p (c s) -> p c s", s=S)[:, :, (t - 1) * 128:t * 128]
                        VE_DE.inc(v.tensor_copy(dstf, p_dt[(t - 1) % 2][:].rearrange("p (c j) -> p c j", j=128)))
                v.wait_ge(PE_D.h, NT)
                dstf = xn1nF[:].rearrange("p (c s) -> p c s", s=S)[:, :, (NT - 1) * 128:NT * 128]
                VE_DE.inc(v.tensor_copy(dstf, p_dt[(NT - 1) % 2][:].rearrange("p (c j) -> p c j", j=128)))

            ve_d_base = VE_D.n - NT

            @blk.tensor
            def _(t_):
                for t in range(NT):
                    t_.wait_ge(VE_D.h, ve_d_base + t + 1)
                    if t >= 2:
                        t_.wait_ge(VE_DE.h, t - 1)
                    ins = None
                    for c in range(NC):
                        ins = t_.transpose(p_dt[t % 2][:, c * 128:(c + 1) * 128],
                                           xn1nt[t % 2][:, c * 128:(c + 1) * 128], ident[:])
                    PE_D.inc(ins)

            ve_de_all = VE_DE.n

            def emit_proj(kind):
                pe_base = PE_D.n
                sc_base = SC_D.n
                ve_base = VE_K.n
                sd_counts = sd_counts_g
                for t in range(NT):
                    @blk.tensor
                    def _(t_, t=t, pe_base=pe_base):
                        if t == 0:
                            t_.wait_ge(VE_DE.h, ve_de_all)
                            t_.wait_ge(VE_WF.h, VE_WF.n)
                        if t >= 2:
                            t_.wait_ge(VE_K.h, ve_base + (t - 1 if kind == 1 else 2 * t - 3))
                        ins = None
                        for ns in range(2):
                            for c in range(NC):
                                ins = t_.matmul(p_d[t % 2][:, ns * 512:(ns + 1) * 512],
                                                xn1nF[:, c * S + t * 128:c * S + (t + 1) * 128],
                                                wbD[:, c * 1024 + ns * 512:c * 1024 + (ns + 1) * 512],
                                                start=(c == 0), stop=(c == NC - 1))
                        PE_D.inc(ins)

                    @blk.vector
                    def _(v, t=t, pe_base=pe_base, ve_base=ve_base, sc_base=sc_base):
                        v.wait_ge(PE_D.h, pe_base + t + 1)
                        if kind != 1 and t >= 1:
                            v.wait_ge(SC_D.h, sc_base + t)
                        if kind == 1:
                            ins = v.tensor_copy(ar2[:, t * 1024:(t + 1) * 1024], p_d[t % 2][:])
                            VE_K.inc(ins)
                            v.drain()
                            return
                        v.tensor_copy(keysc, p_d[t % 2][:])
                        v.drain()
                        v.tensor_mul(ksq, keysc, keysc)
                        v.drain()
                        v.tensor_reduce(small[:, 0:16].rearrange("p (h o) -> p h o", o=1),
                                        ksq.rearrange("p (h d) -> p h d", d=64),
                                        axis=AX.X, op=ALU.add)
                        VE_K.inc(v.drain())

                    if kind != 1:
                        @blk.scalar
                        def _(s, t=t, ve_base=ve_base, sc_base=sc_base):
                            s.wait_ge(VE_K.h, ve_base + 2 * t + 1)
                            ins = s.activation(out=small[:, 16:32], in_=small[:, 0:16],
                                               func=AF.Sqrt, bias=cst[:, 1:2], scale=cst[:, 4:5])
                            SC_D.inc(ins)

                        @blk.vector
                        def _(v, t=t, sc_base=sc_base):
                            v.wait_ge(SC_D.h, sc_base + t + 1)
                            v.reciprocal(small[:, 0:16], small[:, 16:32])
                            v.drain()
                            nwsb = knw_sb if kind == 0 else qnw_sb
                            v.tensor_mul(keysc, keysc, nwsb)
                            v.drain()
                            if sd_counts[t % 2]:
                                v.wait_ge(SDP[t % 2].h, sd_counts[t % 2][-1])
                            ins = v.tensor_tensor(
                                ktile_b[t % 2].rearrange("p (h d) -> p h d", d=64), 
                                keysc.rearrange("p (h d) -> p h d", d=64),
                                small[:, 0:16].rearrange("p (h o) -> p h o", o=1).to_broadcast([128, 16, 64]),
                                op=ALU.mult)
                            VE_K.inc(ins)
                            v.drain()

                        @blk.sync
                        def _(sync, t=t, ve_base=ve_base):
                            sync.wait_ge(VE_K.h, ve_base + 2 * (t + 1))
                            scr = scr_k if kind == 0 else scr_q
                            dst = scr.rearrange("h (t p) d -> t p h d", p=128)[t]
                            dmaN(sync, SDP[t % 2], dst,
                                 ktile_b[t % 2].rearrange("p (h d) -> p h d", d=64))
                            sd_counts[t % 2].append(SDP[t % 2].n)

            emit_proj(0)

            @blk.gpsimd
            def _(gp):
                gp.wait_ge(PE_D.h, PE_D.n)
                wvc = wv_in.rearrange("(c p) n -> p c n", p=128)
                dmaN(gp, LDG, wbD.rearrange("p (c n) -> p c n", n=1024), wvc)

            @blk.vector
            def _(v):
                v.wait_ge(LDG.h, LDG.n)
                for c in range(NC):
                    v.tensor_scalar(wbD[:, c * 1024:(c + 1) * 1024], wbD[:, c * 1024:(c + 1) * 1024],
                                    snw_c(c), None, op0=ALU.mult)
                VE_WF.inc(v.drain())

            @blk.tensor
            def _(t_):
                t_.wait_ge(VE_WF.h, VE_WF.n)
            emit_proj(1)

            @blk.gpsimd
            def _(gp):
                gp.wait_ge(PE_D.h, PE_D.n)
                wqc = wq_in.rearrange("(c p) n -> p c n", p=128)
                dmaN(gp, LDG, wbD.rearrange("p (c n) -> p c n", n=1024), wqc)

            @blk.vector
            def _(v):
                v.wait_ge(LDG.h, LDG.n)
                for c in range(NC):
                    v.tensor_scalar(wbD[:, c * 1024:(c + 1) * 1024], wbD[:, c * 1024:(c + 1) * 1024],
                                    rnw_c(c), None, op0=ALU.mult)
                VE_WF.inc(v.drain())

            @blk.tensor
            def _(t_):
                t_.wait_ge(VE_WF.h, VE_WF.n)
            emit_proj(2)

            @blk.gpsimd
            def _(gp):
                gp.wait_ge(PE_D.h, PE_D.n)
                atc = atw_in.rearrange("(c p) n -> p c n", p=128)
                dmaN(gp, LDG, atwb.rearrange("p (c n) -> p c n", n=32), atc)
                mw0c = mw0_in.rearrange("p n -> p n")
                dmaN(gp, LDG, mw0b[:], mw0c)
                dmaN(gp, LDG, mw1b[:].rearrange("p (k n) -> p k n", n=64),
                     mw1_in.rearrange("(k p) n -> p k n", p=128))

            @blk.vector
            def _(v):
                v.wait_ge(LDG.h, LDG.n)
                for c in range(NC):
                    v.tensor_scalar(atwb[:, c * 32:(c + 1) * 32], atwb[:, c * 32:(c + 1) * 32],
                                    snw_c(c), None, op0=ALU.mult)
                VE_WF.inc(v.drain())

            pe_at_base = PE_D.n
            sc_at_base = SC_AT.n

            @blk.tensor
            def _(t_):
                t_.wait_ge(VE_WF.h, VE_WF.n)
                for t in range(NT):
                    if t >= 1:
                        t_.wait_ge(VE_AT.h, t)
                    ins = None
                    for c in range(NC):
                        ins = t_.matmul(p_am[:, 0:32],
                                        xn1nF[:, c * S + t * 128:c * S + (t + 1) * 128],
                                        atwb[:, c * 32:(c + 1) * 32],
                                        start=(c == 0), stop=(c == NC - 1))
                    PE_D.inc(ins)

            @blk.vector
            def _(v):
                for t in range(NT):
                    v.wait_ge(PE_D.h, pe_at_base + t + 1)
                    if t >= 1:
                        v.wait_ge(SC_AT.h, sc_at_base + t)
                    v.tensor_add(at32, p_am[:, 0:32], atb_sb[:])
                    VE_AT.inc(v.drain())

            @blk.scalar
            def _(s):
                for t in range(NT):
                    s.wait_ge(VE_AT.h, t + 1)
                    ins = s.activation(out=atheta[:, t * 32:(t + 1) * 32], in_=at32,
                                       func=AF.Sigmoid)
                    SC_AT.inc(ins)


            @blk.vector
            def _(v):
                v.wait_ge(SC_AT.h, SC_AT.n)
                ins = v.tensor_scalar(
                    atheta[:].rearrange("p (t g) -> p t g", g=32)[:, :, 16:32],
                    atheta[:].rearrange("p (t g) -> p t g", g=32)[:, :, 16:32],
                    0.02, None, op0=ALU.mult)
                VE_AT.inc(ins)
                v.drain()
                v.memset(onesrow, 1.0)
                v.drain()
                VE_AT.inc(v.drain())

            @blk.tensor
            def _(t_):
                t_.wait_ge(VE_AT.h, VE_AT.n)
                ins = None
                for t in range(NT):
                    ins = t_.matmul(p_am[0:16, 32:33], atheta[:, t * 32:t * 32 + 16],
                                    onesb[:, 0:1], start=(t == 0), stop=(t == NT - 1))
                PE_AM.inc(ins)

            @blk.vector
            def _(v):
                v.wait_ge(PE_AM.h, PE_AM.n)
                v.tensor_scalar(small[0:16, 0:1], p_am[0:16, 32:33], -1.0 / 2048.0, 1.0,
                                op0=ALU.mult, op1=ALU.add)
                v.drain()
                ins = v.tensor_copy(vring[0:16, 4496:4497], small[0:16, 0:1])
                VE_AM.inc(ins)
                v.drain()

            @blk.tensor
            def _(t_):
                t_.wait_ge(VE_AM.h, VE_AM.n)
                ins = t_.transpose(p_dt[0][0:1, 0:16], vring[0:16, 4496:4497], ident[0:16, 0:16])
                PE_AM.inc(ins)

            @blk.vector
            def _(v):
                v.wait_ge(PE_AM.h, PE_AM.n)
                ins = v.tensor_copy(amrow, p_dt[0][0:1, 0:16])
                VE_AM.inc(ins)
                v.drain()

            @blk.tensor
            def _(t_):
                t_.wait_ge(VE_AM.h, VE_AM.n)
                ins = t_.matmul(p_am[:, 48:64], onesrow, amrow, start=True, stop=True)
                PE_AM.inc(ins)
                ins = t_.transpose(p_dt[0][0:64, 128:256], mw1b[:, 0:64], ident[:])
                ins = t_.transpose(p_dt[0][0:64, 256:384], mw1b[:, 64:128], ident[:])
                PE_AM.inc(ins)

            @blk.vector
            def _(v):
                v.wait_ge(PE_AM.h, PE_AM.n)
                v.tensor_copy(amrep[:], p_am[:, 48:64])
                v.tensor_copy(mw1T[:, 0:128], p_dt[0][0:64, 128:256])
                ins = v.tensor_copy(mw1T[:, 128:256], p_dt[0][0:64, 256:384])
                VE_AM.inc(ins)
                v.drain()

            final_waits += [VE_AM, SDP[0], SDP[1]]

        if stage >= 6:
            phase_join()
            es_psD.close()
            es_psE = ExitStack()
            p_a = [es_psE.enter_context(nc.psum_tensor("p_a0", [128, 512])),
                   es_psE.enter_context(nc.psum_tensor("p_a1", [128, 512]))]
            p_b = [es_psE.enter_context(nc.psum_tensor("p_b0", [128, 512])),
                   es_psE.enter_context(nc.psum_tensor("p_b1", [128, 512]))]
            p_tre = [es_psE.enter_context(nc.psum_tensor("p_tre0", [128, 1024], BF16)),
                     es_psE.enter_context(nc.psum_tensor("p_tre1", [128, 1024], BF16))]
            p_g0 = es_psE.enter_context(nc.psum_tensor("p_g0", [128, 512]))
            p_g1 = es_psE.enter_context(nc.psum_tensor("p_g1", [128, 512]))

            LKH = sem("LKH2"); LQH = sem("LQH2")
            VE_E1 = sem("VE_E1"); PE_E1 = sem("PE_E1"); SC_E1 = sem("SC_E1")
            GP_E1 = sem("GP_E1"); VE_S1b = sem("VE_S1b")

            s1tm = fm2
            kh = ar1[:, 0:1024]
            qh = ar1[:, 1024:2048]
            khf = ar1[:, 4096:6144]
            h1f = ar1[:, 6144:10240]
            pre0tm = ar1[:, 10240:14336]
            dhf = ar1[:, 14336:16384]
            d_htm = vring[:, 0:1024]
            w0a = vring[0:64, 1024:1280]
            w0n = vring[0:64, 1280:1536]
            w1a = vring[:, 1536:1664]
            w1n = vring[:, 1664:1792]
            d_h0t = [vring[:, 1792:2048], vring[:, 2048:2304]]
            ggt = [vring[:, 2304:2560], vring[:, 2560:2816]]
            e1t = [vring[:, 2816:3072], vring[:, 3072:3328]]
            cfb = vring[0:1, 3328:3329]
            pre1 = ar3[:, 0:1024]
            sq0t = [ar3[:, 1024:1280], ar3[:, 1280:1536]]
            numt = [ar3[:, 1536:1792], ar3[:, 1792:2048]]
            dA = ar3[:, 2048:3072]
            dB = ar3[:, 3072:4096]
            dC = ar3[:, 4096:5120]
            dD = ar3[:, 5120:6144]

            for h in range(H):
                lkh_n0 = LKH.n
                lqh_n0 = LQH.n

                @blk.sync
                def _(sync, h=h):
                    if h == 0:
                        sync.wait_ge(SDP[0].h, SDP[0].n)
                        sync.wait_ge(SDP[1].h, SDP[1].n)
                    if h >= 1:
                        sync.wait_ge(VE_S1b.h, h)
                    dmaN(sync, LKH, kh.rearrange("p (t d) -> p t d", d=64),
                         scr_k[h].rearrange("(t p) d -> p t d", p=128))
                    dmaN(sync, LQH, qh.rearrange("p (t d) -> p t d", d=64),
                         scr_q[h].rearrange("(t p) d -> p t d", p=128))

                pe_t0 = PE_E1.n
                ve_khf0 = VE_E1.n

                @blk.tensor
                def _(t_, h=h, pe_t0=pe_t0, lkh_n=LKH.n, ve_khf0=ve_khf0):
                    t_.wait_ge(LKH.h, lkh_n)
                    if h >= 1:
                        t_.wait_ge(VE_S1b.h, h)
                    for t in range(NT):
                        if t >= 2:
                            t_.wait_ge(VE_E1.h, ve_khf0 + t - 1)
                        ins = t_.transpose(p_tre[t % 2][0:64, 0:128], kh[:, t * 64:(t + 1) * 64],
                                           ident[:])
                        PE_E1.inc(ins)

                @blk.vector
                def _(v, h=h, pe_t0=pe_t0):
                    for t in range(NT):
                        v.wait_ge(PE_E1.h, pe_t0 + t + 1)
                        ins = v.tensor_copy(khf[0:64, t * 128:(t + 1) * 128], p_tre[t % 2][0:64, 0:128])
                        VE_E1.inc(ins)
                        v.drain()

                pe_a0 = PE_E1.n
                ve_khf_all = VE_E1.n

                sc_h1f0 = SC_E1.n

                @blk.tensor
                def _(t_, pe_a0=pe_a0, ve_khf_all=ve_khf_all, sc_h1f0=sc_h1f0):
                    t_.wait_ge(VE_E1.h, ve_khf_all)
                    k = 0
                    for mt in range(2):
                        for ns in range(4):
                            if k >= 2:
                                t_.wait_ge(SC_E1.h, sc_h1f0 + k - 1)
                            ins = t_.matmul(p_a[k % 2][:, 0:512],
                                            mw0b[:, mt * 128:(mt + 1) * 128],
                                            khf[0:64, ns * 512:(ns + 1) * 512],
                                            start=True, stop=True)
                            PE_E1.inc(ins)
                            k += 1

                @blk.scalar
                def _(s, pe_a0=pe_a0):
                    k = 0
                    for mt in range(2):
                        for ns in range(4):
                            s.wait_ge(PE_E1.h, pe_a0 + k + 1)
                            ins = s.activation(out=h1f[:, mt * 2048 + ns * 512: mt * 2048 + (ns + 1) * 512],
                                               in_=p_a[k % 2][:, 0:512], func=AF.Gelu)
                            SC_E1.inc(ins)
                            k += 1

                pe_b0 = PE_E1.n
                sc_h1f_all = SC_E1.n

                ve_p0tm0 = VE_E1.n

                @blk.tensor
                def _(t_, pe_b0=pe_b0, ve_p0tm0=ve_p0tm0):
                    t_.wait_ge(VE_E1.h, ve_p0tm0)
                    for t in range(NT):
                        if t >= 2:
                            t_.wait_ge(VE_E1.h, ve_p0tm0 + t - 1)
                        ins = t_.matmul(p_b[t % 2][:, 0:256], khf[0:64, t * 128:(t + 1) * 128],
                                        mw0b[:], start=True, stop=True)
                        PE_E1.inc(ins)

                @blk.vector
                def _(v, pe_b0=pe_b0):
                    for t in range(NT):
                        v.wait_ge(PE_E1.h, pe_b0 + t + 1)
                        ins = v.tensor_copy(pre0tm[:, t * 256:(t + 1) * 256], p_b[t % 2][:, 0:256])
                        VE_E1.inc(ins)
                        v.drain()

                pe_c0 = PE_E1.n
                ve_p0tm_all = VE_E1.n

                ve_pre10 = VE_E1.n

                @blk.tensor
                def _(t_, pe_c0=pe_c0, sc_h1f_all=sc_h1f_all, ve_pre10=ve_pre10):
                    t_.wait_ge(SC_E1.h, sc_h1f_all)
                    t_.wait_ge(VE_E1.h, ve_pre10)
                    for t in range(NT):
                        if t >= 2:
                            t_.wait_ge(VE_E1.h, ve_pre10 + t - 1)
                        ins = None
                        for kc in range(2):
                            ins = t_.matmul(p_b[t % 2][:, 0:64],
                                            h1f[:, kc * 2048 + t * 128: kc * 2048 + (t + 1) * 128],
                                            mw1b[:, kc * 64:(kc + 1) * 64],
                                            start=(kc == 0), stop=(kc == 1))
                        PE_E1.inc(ins)

                @blk.vector
                def _(v, pe_c0=pe_c0):
                    for t in range(NT):
                        v.wait_ge(PE_E1.h, pe_c0 + t + 1)
                        ins = v.tensor_copy(pre1[:, t * 64:(t + 1) * 64], p_b[t % 2][:, 0:64])
                        VE_E1.inc(ins)
                        v.drain()

                ve_pre1_all = VE_E1.n

                @blk.scalar
                def _(s, h=h, ve_pre1_all=ve_pre1_all):
                    s.wait_ge(VE_E1.h, ve_pre1_all)
                    ins = s.activation(out=h1f[:], in_=pre0tm[:], func=AF.Gelu)
                    SC_E1.inc(ins)
                h1tm = h1f

                sc_h1tm = SC_E1.n

                @blk.vector
                def _(v, h=h, sc_h1tm=sc_h1tm, lqh_n=LQH.n):
                    v.wait_ge(SC_E1.h, sc_h1tm)
                    v.wait_ge(LQH.h, lqh_n)
                    pre1v = pre1.rearrange("p (t d) -> p t d", d=64)
                    v.tensor_mul(dA, pre1[:], pre1[:])
                    v.drain()
                    v.tensor_reduce(small[:, 0:16].rearrange("p (t o) -> p t o", o=1),
                                    dA.rearrange("p (t d) -> p t d", d=64), axis=AX.X, op=ALU.add)
                    VE_E1.inc(v.drain())

                @blk.scalar
                def _(s):
                    s.wait_ge(VE_E1.h, VE_E1.n)
                    ins = s.activation(out=small[:, 16:32], in_=small[:, 0:16],
                                       func=AF.Sqrt, bias=cst[:, 2:3], scale=cst[:, 4:5])
                    SC_E1.inc(ins)

                @blk.vector
                def _(v, h=h):
                    v.wait_ge(SC_E1.h, SC_E1.n)
                    v.reciprocal(small[:, 0:16], small[:, 16:32])
                    v.drain()
                    irb = small[:, 0:16].rearrange("p (t o) -> p t o", o=1).to_broadcast([128, NT, 64])
                    v3 = lambda x: x.rearrange("p (t d) -> p t d", d=64)
                    v.tensor_tensor(v3(dA), v3(pre1[:]), irb, op=ALU.mult)
                    v.drain()
                    v.tensor_add(dB, dA, kh)
                    v.drain()
                    vals_h = ar2[:].rearrange("p (t n) -> p t n", n=1024)[:, :, h * 64:(h + 1) * 64]
                    v.tensor_tensor(v3(dC), v3(dB), vals_h, op=ALU.subtract)
                    v.drain()
                    thb = atheta[:].rearrange("p (t g) -> p t g", g=32)[:, :, 16 + h:17 + h].to_broadcast([128, NT, 64])
                    v.tensor_tensor(v3(dB), v3(dC), thb, op=ALU.mult)
                    v.drain()
                    v.tensor_mul(dC, dB, dA)
                    v.drain()
                    v.tensor_reduce(small[:, 16:32].rearrange("p (t o) -> p t o", o=1),
                                    v3(dC), axis=AX.X, op=ALU.add)
                    v.drain()
                    v.tensor_scalar(small[:, 16:32], small[:, 16:32], 1.0 / 64.0, None, op0=ALU.mult)
                    v.drain()
                    dyb = small[:, 16:32].rearrange("p (t o) -> p t o", o=1).to_broadcast([128, NT, 64])
                    v.tensor_tensor(v3(dC), v3(dA), dyb, op=ALU.mult)
                    v.drain()
                    v.tensor_sub(dC, dB, dC)
                    v.drain()
                    irb2 = small[:, 0:16].rearrange("p (t o) -> p t o", o=1).to_broadcast([128, NT, 64])
                    ins = v.tensor_tensor(v3(d_htm), v3(dC), irb2, op=ALU.mult)
                    VE_E1.inc(ins)
                    v.drain()

                pe_d0 = PE_E1.n
                ve_dh = VE_E1.n

                ve_dhf0 = VE_E1.n

                @blk.tensor
                def _(t_, pe_d0=pe_d0, ve_dh=ve_dh, ve_dhf0=ve_dhf0):
                    t_.wait_ge(VE_E1.h, ve_dh)
                    for t in range(NT):
                        if t >= 2:
                            t_.wait_ge(VE_E1.h, ve_dhf0 + t - 1)
                        ins = t_.transpose(p_tre[t % 2][0:64, 0:128], d_htm[:, t * 64:(t + 1) * 64],
                                           ident[:])
                        PE_E1.inc(ins)

                @blk.vector
                def _(v, pe_d0=pe_d0):
                    for t in range(NT):
                        v.wait_ge(PE_E1.h, pe_d0 + t + 1)
                        ins = v.tensor_copy(dhf[0:64, t * 128:(t + 1) * 128], p_tre[t % 2][0:64, 0:128])
                        VE_E1.inc(ins)
                        v.drain()

                pe_e0 = PE_E1.n
                ve_dhf_all = VE_E1.n
                sc_e0 = SC_E1.n
                gp_e0 = GP_E1.n
                ve_f0 = VE_E1.n

                ve_dh00 = VE_E1.n

                @blk.tensor
                def _(t_, pe_e0=pe_e0, ve_dhf_all=ve_dhf_all, ve_dh00=ve_dh00):
                    t_.wait_ge(VE_E1.h, ve_dhf_all)
                    for t in range(NT + 1):
                        if t < NT:
                            if t >= 2:
                                t_.wait_ge(VE_E1.h, ve_dh00 + 2 * t - 2)
                            ins = t_.matmul(p_b[t % 2][:, 0:256], dhf[0:64, t * 128:(t + 1) * 128],
                                            mw1T[:], start=True, stop=True)
                            PE_E1.inc(ins)
                        if t >= 1:
                            tt = t - 1
                            t_.wait_ge(VE_E1.h, ve_dh00 + 2 * tt + 2)
                            ins = t_.matmul(p_g0[0:64, 0:256], kh[:, tt * 64:(tt + 1) * 64],
                                            d_h0t[tt % 2], start=(tt == 0), stop=(tt == NT - 1))
                            PE_E1.inc(ins)

                rects = [ar3[:, 3072:3328], ar3[:, 3328:3584]]
                rect = [ar3[:, 3584:3840], ar3[:, 3840:4096]]

                @blk.scalar
                def _(s, sc_e0=sc_e0, gp_e0=gp_e0, ve_gg0=ve_dh00, ve_dh=ve_dh):
                    s.wait_ge(VE_E1.h, ve_dh)
                    for t in range(NT):
                        if t >= 2:
                            s.wait_ge(GP_E1.h, gp_e0 + 4 * (t - 1))
                            s.wait_ge(VE_E1.h, ve_gg0 + 2 * t - 3)
                        ins = s.activation(out=sq0t[t % 2], in_=pre0tm[:, t * 256:(t + 1) * 256],
                                           func=AF.Square)
                        SC_E1.inc(ins)
                        s.drain()
                        ins = s.activation(out=e1t[t % 2], in_=sq0t[t % 2], func=AF.Exp,
                                           bias=0.0, scale=cst[:, 6:7])
                        SC_E1.inc(ins)
                        s.drain()
                        ins = s.activation(out=rects[t % 2], in_=sq0t[t % 2], func=AF.Identity,
                                           bias=cst[:, 3:4])
                        SC_E1.inc(ins)
                        s.drain()
                        ins = s.activation(out=e1t[t % 2], in_=e1t[t % 2], func=AF.Identity,
                                           bias=0.0, scale=cst[:, 7:8])
                        SC_E1.inc(ins)
                        s.drain()

                ve_gg0 = VE_E1.n

                @blk.gpsimd
                def _(gp, h=h, sc_e0=sc_e0, ve_gg0=ve_gg0):
                    for t in range(NT):
                        gp.wait_ge(SC_E1.h, sc_e0 + 4 * (t + 1))
                        gp.wait_ge(VE_E1.h, ve_gg0 + 2 * t + 1)
                        if t >= 2:
                            gp.wait_ge(VE_E1.h, ve_gg0 + 2 * t - 2)
                        gp.tensor_tensor(numt[t % 2], h1tm[:, t * 256:(t + 1) * 256],
                                         pre0tm[:, t * 256:(t + 1) * 256], op=ALU.mult)
                        gp.drain()
                        gp.tensor_tensor(numt[t % 2], numt[t % 2], rect[t % 2], op=ALU.mult)
                        gp.drain()
                        gp.tensor_tensor(ggt[t % 2], pre0tm[:, t * 256:(t + 1) * 256], e1t[t % 2],
                                         op=ALU.mult)
                        gp.drain()
                        ins = gp.tensor_tensor(ggt[t % 2], ggt[t % 2], numt[t % 2], op=ALU.add)
                        GP_E1.inc(ins, 4)
                        gp.drain()

                @blk.vector
                def _(v, pe_e0=pe_e0, gp_e0=gp_e0, sc_e0=sc_e0):
                    for t in range(NT):
                        v.wait_ge(SC_E1.h, sc_e0 + 4 * t + 3)
                        if t >= 2:
                            v.wait_ge(GP_E1.h, gp_e0 + 4 * (t - 1))
                        ins = v.reciprocal(rect[t % 2], rects[t % 2])
                        VE_E1.inc(ins)
                        v.wait_ge(PE_E1.h, pe_e0 + max(1, 2 * t))
                        v.wait_ge(GP_E1.h, gp_e0 + 4 * (t + 1))
                        ins = v.tensor_tensor(d_h0t[t % 2], p_b[t % 2][:, 0:256], ggt[t % 2],
                                              op=ALU.mult)
                        VE_E1.inc(ins)
                        v.drain()

                ve_dh0_all = VE_E1.n

                @blk.tensor
                def _(t_, h=h):
                    for mh in range(2):
                        for t in range(NT):
                            ins = t_.matmul(p_g1[:, mh * 64:(mh + 1) * 64],
                                            h1tm[:, t * 256 + mh * 128: t * 256 + (mh + 1) * 128],
                                            d_htm[:, t * 64:(t + 1) * 64],
                                            start=(t == 0), stop=(t == NT - 1))
                        PE_E1.inc(ins)

                pe_gall = PE_E1.n

                @blk.scalar
                def _(s, pe_gall=pe_gall):
                    s.wait_ge(PE_E1.h, pe_gall)
                    ins = s.activation(out=dA[0:64, 0:256], in_=p_g0[0:64, 0:256], func=AF.Square)
                    SC_E1.inc(ins)
                    s.drain()
                    ins = s.activation(out=dA[:, 256:384], in_=p_g1[:, 0:128], func=AF.Square)
                    SC_E1.inc(ins)
                    s.drain()
                sc_clip = SC_E1.n

                @blk.vector
                def _(v, sc_clip=sc_clip):
                    v.wait_ge(SC_E1.h, sc_clip)
                    v.tensor_reduce(small[0:64, 32:33], dA[0:64, 0:256], axis=AX.X, op=ALU.add)
                    v.tensor_reduce(small[:, 33:34], dA[:, 256:384], axis=AX.X, op=ALU.add)
                    VE_E1.inc(v.drain())

                @blk.tensor
                def _(t_):
                    t_.wait_ge(VE_E1.h, VE_E1.n)
                    ins = t_.matmul(p_a[0][0:1, 0:1], small[0:64, 32:33], cst[0:64, 5:6],
                                    start=True, stop=False)
                    ins = t_.matmul(p_a[0][0:1, 0:1], small[:, 33:34], cst[:, 5:6],
                                    start=False, stop=True)
                    PE_E1.inc(ins)

                @blk.scalar
                def _(s):
                    s.wait_ge(PE_E1.h, PE_E1.n)
                    ins = s.activation(out=small[0:1, 34:35], in_=p_a[0][0:1, 0:1], func=AF.Sqrt)
                    SC_E1.inc(ins)

                @blk.vector
                def _(v):
                    v.wait_ge(SC_E1.h, SC_E1.n)
                    v.tensor_scalar(small[0:1, 34:35], small[0:1, 34:35], 1e-6, None, op0=ALU.add)
                    v.drain()
                    v.reciprocal(small[0:1, 35:36], small[0:1, 34:35])
                    v.drain()
                    v.tensor_scalar(small[0:1, 35:36], small[0:1, 35:36], 10.0, None, op0=ALU.mult)
                    v.drain()
                    v.tensor_scalar(small[0:1, 35:36], small[0:1, 35:36], 1.0, None, op0=ALU.min)
                    v.drain()
                    v.tensor_scalar(small[0:1, 35:36], small[0:1, 35:36], -1.0, None, op0=ALU.mult)
                    v.drain()
                    ins = v.tensor_copy(cfb, small[0:1, 35:36])
                    VE_E1.inc(ins)
                    v.drain()

                @blk.tensor
                def _(t_):
                    t_.wait_ge(VE_E1.h, VE_E1.n)
                    ins = t_.matmul(p_a[0][:, 8:9], onesrow, cfb, start=True, stop=True)
                    PE_E1.inc(ins)

                @blk.vector
                def _(v, h=h):
                    v.wait_ge(PE_E1.h, PE_E1.n)
                    v.tensor_copy(small[:, 36:37], p_a[0][:, 8:9])
                    v.drain()
                    v.tensor_scalar(w0a, mw0b[:], amrep[0:64, h:h + 1], None, op0=ALU.mult)
                    v.tensor_scalar(w1a, mw1b[:], amrep[:, h:h + 1], None, op0=ALU.mult)
                    v.drain()
                    v.scalar_tensor_tensor(w0n, p_g0[0:64, 0:256], small[0:64, 36:37], w0a,
                                           op0=ALU.mult, op1=ALU.add)
                    ins = v.scalar_tensor_tensor(w1n, p_g1[:, 0:128], small[:, 36:37], w1a,
                                                 op0=ALU.mult, op1=ALU.add)
                    VE_E1.inc(ins)
                    v.drain()

                pe_q0 = PE_E1.n
                ve_w = VE_E1.n

                ve_qhf0 = VE_E1.n

                @blk.tensor
                def _(t_, pe_q0=pe_q0, ve_w=ve_w, lqh_n=LQH.n, ve_qhf0=ve_qhf0):
                    t_.wait_ge(LQH.h, lqh_n)
                    t_.wait_ge(VE_E1.h, ve_w)
                    for t in range(NT):
                        if t >= 2:
                            t_.wait_ge(VE_E1.h, ve_qhf0 + t - 1)
                        ins = t_.transpose(p_tre[t % 2][0:64, 0:128], qh[:, t * 64:(t + 1) * 64],
                                           ident[:])
                        PE_E1.inc(ins)

                @blk.vector
                def _(v, pe_q0=pe_q0):
                    for t in range(NT):
                        v.wait_ge(PE_E1.h, pe_q0 + t + 1)
                        ins = v.tensor_copy(khf[0:64, t * 128:(t + 1) * 128], p_tre[t % 2][0:64, 0:128])
                        VE_E1.inc(ins)
                        v.drain()
                qhf = khf

                pe_m0 = PE_E1.n
                ve_qhf_all = VE_E1.n

                sc_a0f0 = SC_E1.n

                @blk.tensor
                def _(t_, pe_m0=pe_m0, ve_qhf_all=ve_qhf_all, sc_a0f0=sc_a0f0):
                    t_.wait_ge(VE_E1.h, ve_qhf_all)
                    k = 0
                    for mt in range(2):
                        for ns in range(4):
                            if k >= 2:
                                t_.wait_ge(SC_E1.h, sc_a0f0 + k - 1)
                            ins = t_.matmul(p_a[k % 2][:, 0:512],
                                            w0n[:, mt * 128:(mt + 1) * 128],
                                            qhf[0:64, ns * 512:(ns + 1) * 512],
                                            start=True, stop=True)
                            PE_E1.inc(ins)
                            k += 1

                a0f = pre0tm

                @blk.scalar
                def _(s, pe_m0=pe_m0):
                    k = 0
                    for mt in range(2):
                        for ns in range(4):
                            s.wait_ge(PE_E1.h, pe_m0 + k + 1)
                            ins = s.activation(out=a0f[:, mt * 2048 + ns * 512: mt * 2048 + (ns + 1) * 512],
                                               in_=p_a[k % 2][:, 0:512], func=AF.Gelu)
                            SC_E1.inc(ins)
                            k += 1

                pe_m1 = PE_E1.n
                sc_a0f_all = SC_E1.n
                m1tm = pre1

                ve_m10 = VE_E1.n

                @blk.tensor
                def _(t_, pe_m1=pe_m1, sc_a0f_all=sc_a0f_all, ve_m10=ve_m10):
                    t_.wait_ge(SC_E1.h, sc_a0f_all)
                    t_.wait_ge(VE_E1.h, ve_m10)
                    for t in range(NT):
                        if t >= 2:
                            t_.wait_ge(VE_E1.h, ve_m10 + t - 1)
                        ins = None
                        for kc in range(2):
                            ins = t_.matmul(p_b[t % 2][:, 0:64],
                                            a0f[:, kc * 2048 + t * 128: kc * 2048 + (t + 1) * 128],
                                            w1n[:, kc * 64:(kc + 1) * 64],
                                            start=(kc == 0), stop=(kc == 1))
                        PE_E1.inc(ins)

                @blk.vector
                def _(v, pe_m1=pe_m1):
                    for t in range(NT):
                        v.wait_ge(PE_E1.h, pe_m1 + t + 1)
                        ins = v.tensor_copy(m1tm[:, t * 64:(t + 1) * 64], p_b[t % 2][:, 0:64])
                        VE_E1.inc(ins)
                        v.drain()

                @blk.vector
                def _(v, h=h):
                    v.wait_ge(VE_E1.h, VE_E1.n)
                    v3 = lambda x: x.rearrange("p (t d) -> p t d", d=64)
                    v.tensor_mul(dA, m1tm[:], m1tm[:])
                    v.drain()
                    v.tensor_reduce(small[:, 0:16].rearrange("p (t o) -> p t o", o=1),
                                    v3(dA), axis=AX.X, op=ALU.add)
                    VE_E1.inc(v.drain())

                @blk.scalar
                def _(s):
                    s.wait_ge(VE_E1.h, VE_E1.n)
                    ins = s.activation(out=small[:, 16:32], in_=small[:, 0:16],
                                       func=AF.Sqrt, bias=cst[:, 2:3], scale=cst[:, 4:5])
                    SC_E1.inc(ins)

                @blk.vector
                def _(v, h=h):
                    v.wait_ge(SC_E1.h, SC_E1.n)
                    v.reciprocal(small[:, 0:16], small[:, 16:32])
                    v.drain()
                    v3 = lambda x: x.rearrange("p (t d) -> p t d", d=64)
                    irb = small[:, 0:16].rearrange("p (t o) -> p t o", o=1).to_broadcast([128, NT, 64])
                    v.tensor_tensor(v3(dA), v3(m1tm[:]), irb, op=ALU.mult)
                    v.drain()
                    s1h = s1tm[:].rearrange("p (t n) -> p t n", n=1024)[:, :, h * 64:(h + 1) * 64]
                    ins = v.tensor_tensor(s1h, v3(dA), qh.rearrange("p (t d) -> p t d", d=64),
                                          op=ALU.add)
                    VE_S1b.inc(ins)
                    v.drain()

            final_waits += [VE_S1b]

        if stage >= 7:
            phase_join()
            es_psE.close()
            es_psF = ExitStack()
            p_tf = es_psF.enter_context(nc.psum_tensor("p_tf", [128, 1024], BF16))
            p_q = es_psF.enter_context(nc.psum_tensor("p_q", [128, 1024]))

            LF2 = sem("LF2"); LF2g = sem("LF2g")
            VE_F1 = sem("VE_F1"); PE_F1 = sem("PE_F1"); SC_F1 = sem("SC_F1")
            STP = [sem("STa"), sem("STb")]
            st_counts = [[], []]

            wqtb = ar1[:, 0:8192]
            xn1t_f = ar3[:, 0:1024]
            t1f = ar3[:, 1024:2048]
            xof = [ar3[:, 2048:3072], ar3[:, 3072:4096]]
            wscb = ar3[:, 4096:5120]
            wqbb = ar3[:, 5120:6144]
            xqt = vring[:, 0:1024]
            xqf = vring[:, 1024:2048]
            g1t = vring[:, 2048:3072]
            s1g = vring[:, 3072:4096]

            @blk.sync
            def _(sync):
                dmaN(sync, LF2, wscb, wsc_in)
                dmaN(sync, LF2, wqbb, wqb_in)

            @blk.gpsimd
            def _(gp):
                dmaN(gp, LF2g, wqtb.rearrange("p (c n) -> p c n", n=1024),
                     wqt_in.rearrange("(c p) n -> p c n", p=128))
            lf_n = LF2.n
            lf_ng = LF2g.n

            for t in range(NT):
                @blk.vector
                def _(v, t=t):
                    if t == 0:
                        v.wait_ge(LF2.h, lf_n)
                        v.wait_ge(LF2g.h, lf_ng)
                    if t >= 2 and st_counts[t % 2]:
                        v.wait_ge(STP[t % 2].h, st_counts[t % 2][-1])
                    mu = stat1[:, 2 * t:2 * t + 1]
                    rs = rstd1[:, t:t + 1]
                    v.tensor_scalar(xn1t_f, x_sb[:, t * D:(t + 1) * D], mu, rs,
                                    op0=ALU.subtract, op1=ALU.mult)
                    v.drain()
                    v.tensor_reduce(small[:, 0:1], xn1t_f, axis=AX.X, op=ALU.max,
                                    apply_absolute_value=True)
                    v.drain()
                    v.tensor_scalar(small[:, 1:2], small[:, 0:1], 1e-6, 1.0 / 127.0,
                                    op0=ALU.max, op1=ALU.mult)
                    v.drain()
                    v.reciprocal(small[:, 2:3], small[:, 1:2])
                    v.drain()
                    v.tensor_scalar(xn1t_f, xn1t_f, small[:, 2:3], None, op0=ALU.mult)
                    v.drain()
                    v.tensor_scalar(xn1t_f, xn1t_f, 12582912.0, 12582912.0,
                                    op0=ALU.add, op1=ALU.subtract)
                    v.drain()
                    v.tensor_scalar(xn1t_f, xn1t_f, 127.0, -127.0, op0=ALU.min, op1=ALU.max)
                    v.drain()
                    ins = v.tensor_copy(xqt, xn1t_f)
                    VE_F1.inc(ins)
                    v.drain()

                @blk.tensor
                def _(t_, t=t):
                    t_.wait_ge(VE_F1.h, VE_F1.n)
                    ins = None
                    for c in range(NC):
                        ins = t_.transpose(p_tf[:, c * 128:(c + 1) * 128],
                                           xqt[:, c * 128:(c + 1) * 128], ident[:])
                    PE_F1.inc(ins)

                @blk.vector
                def _(v, t=t):
                    v.wait_ge(PE_F1.h, PE_F1.n)
                    ins = v.tensor_copy(xqf, p_tf[:])
                    VE_F1.inc(ins)
                    v.drain()

                @blk.tensor
                def _(t_, t=t):
                    t_.wait_ge(VE_F1.h, VE_F1.n)
                    ins = None
                    for ns in range(2):
                        for c in range(NC):
                            ins = t_.matmul(p_q[:, ns * 512:(ns + 1) * 512],
                                            xqf[:, c * 128:(c + 1) * 128],
                                            wqtb[:, c * 1024 + ns * 512: c * 1024 + (ns + 1) * 512],
                                            start=(c == 0), stop=(c == NC - 1))
                    PE_F1.inc(ins)

                @blk.vector
                def _(v, t=t):
                    v.wait_ge(PE_F1.h, PE_F1.n)
                    v.tensor_scalar(t1f, p_q[:], small[:, 1:2], None, op0=ALU.mult)
                    v.drain()
                    v.tensor_mul(t1f, t1f, wscb)
                    v.drain()
                    ins = v.tensor_add(t1f, t1f, wqbb)
                    VE_F1.inc(ins)
                    v.drain()

                @blk.scalar
                def _(s, t=t):
                    s.wait_ge(VE_F1.h, VE_F1.n)
                    ins = s.activation(out=g1t, in_=t1f, func=AF.Silu)
                    SC_F1.inc(ins)

                @blk.vector
                def _(v, t=t):
                    v.wait_ge(SC_F1.h, SC_F1.n)
                    v.tensor_mul(s1g, s1tm[:, t * D:(t + 1) * D], g1t)
                    v.drain()
                    ins = v.tensor_add(xof[t % 2], x_sb[:, t * D:(t + 1) * D], s1g)
                    VE_F1.inc(ins)
                    v.drain()

                @blk.sync
                def _(sync, t=t):
                    sync.wait_ge(VE_F1.h, VE_F1.n)
                    dst = out_hbm.rearrange("(t p) d -> t p d", p=128)[t]
                    dmaN(sync, STP[t % 2], dst, xof[t % 2])
                    st_counts[t % 2].append(STP[t % 2].n)

            final_waits += [STP[0], STP[1]]

        @blk.sync
        def _(sync):
            for s_ in final_waits:
                sync.wait_ge(s_.h, s_.n)

            def store(dst, src_ap):
                dmaN(sync, ST, dst, src_ap)
            if t_xn0f is not None:
                store(t_xn0f, xn0f[:])
            if t_negc is not None:
                store(t_negc, negc[:])
            if t_comb is not None:
                store(t_comb, comb[:])
            if t_g is not None and stage >= 2:
                store(t_g, ar2[:, 0:NT * D])
            if t_x1f is not None and stage >= 3:
                store(t_x1f, fm2[:])
            if t_vals is not None and stage >= 5:
                store(t_vals, ar2[:])
            if t_s1 is not None and stage >= 6:
                store(t_s1, fm2[:])
            if stage < 7:
                store(out_hbm.rearrange("(t p) d -> p t d", p=128),
                      x_sb[:].rearrange("p (t d) -> p t d", d=D))
            sync.wait_ge(ST.h, ST.n)

    return nc, tap_outs



def _host_inputs(core, inp):
    bf = ml_dtypes.bfloat16
    f32 = np.float32
    A = lambda x: np.ascontiguousarray(np.asarray(x, f32))
    d = {}
    d["x"] = A(inp["L0_attn_src"][core])
    d["ident"] = np.eye(128, dtype=bf)
    jj = np.arange(128)[:, None]; ii = np.arange(128)[None, :]
    d["masks"] = np.concatenate([(jj >= ii).astype(bf), (jj <= ii).astype(bf)], axis=1)
    d["gate_w"] = A(inp["L0_ffn_gate_w"])
    bcast = lambda vec: np.ascontiguousarray(np.broadcast_to(np.asarray(vec, f32)[None, :], (128, 1024)))
    bcastb = lambda vec: np.ascontiguousarray(np.broadcast_to(
        np.asarray(vec, f32).astype(bf)[None, :], (128, 1024)))
    d["ln1g"] = bcast(inp["L0_attn_ln1_g"]); d["ln1b"] = bcast(inp["L0_attn_ln1_b"])
    d["ln2g"] = bcastb(inp["L0_attn_ln2_g"]); d["ln2b"] = bcastb(inp["L0_attn_ln2_b"])
    d["ffn_w1"] = A(inp["L0_attn_W1"]); d["ffn_w2"] = A(inp["L0_attn_W2"])
    d["ffn_b1"] = np.ascontiguousarray(np.asarray(inp["L0_attn_b1"], f32).reshape(32, 128).T)
    d["ffn_b2"] = bcastb(inp["L0_attn_b2"])
    d["moe_w1"] = A(inp["L0_ffn_w1"]); d["moe_w2"] = A(inp["L0_ffn_w2"])
    d["moe_b1"] = np.ascontiguousarray(np.asarray(inp["L0_ffn_b1"], f32).reshape(8, 32, 128).transpose(0, 2, 1))
    d["moe_b2"] = A(inp["L0_ffn_b2"])
    d["wk"] = A(inp["L1_attn_W_K"]); d["wv"] = A(inp["L1_attn_W_V"]); d["wq"] = A(inp["L1_attn_W_Q"])
    fmcols = lambda vec: np.ascontiguousarray(np.asarray(vec, f32).reshape(8, 128).T)
    d["snw"] = fmcols(inp["L1_attn_store_norm_w"])
    d["rnw"] = fmcols(inp["L1_attn_retrieve_norm_w"])
    d["knw"] = np.ascontiguousarray(np.broadcast_to(
        np.tile(np.asarray(inp["L1_attn_k_norm_w"], f32), 16)[None, :], (128, 1024)))
    d["qnw"] = np.ascontiguousarray(np.broadcast_to(
        np.tile(np.asarray(inp["L1_attn_q_norm_w"], f32), 16)[None, :], (128, 1024)))
    d["mw0"] = A(inp["L1_attn_mem_W0"]); d["mw1"] = A(inp["L1_attn_mem_W1"])
    d["atw"] = np.ascontiguousarray(np.concatenate(
        [np.asarray(inp["L1_attn_alpha_w"], f32), np.asarray(inp["L1_attn_theta_w"], f32)], axis=1))
    atb = np.concatenate([np.asarray(inp["L1_attn_alpha_b"], f32),
                          np.asarray(inp["L1_attn_theta_b"], f32)])
    d["atb"] = np.ascontiguousarray(np.broadcast_to(atb[None, :], (128, 32)))
    d["wqt"] = np.ascontiguousarray(np.asarray(inp["L1_ffn_weight_q"]).T)
    d["wsc"] = np.ascontiguousarray(np.broadcast_to(
        np.asarray(inp["L1_ffn_weight_scale"], f32)[None, :], (128, 1024)))
    d["wqb"] = np.ascontiguousarray(np.broadcast_to(
        np.asarray(inp["L1_ffn_bias"], f32)[None, :], (128, 1024)))
    return d


def kernel(**inputs):
    nc, _ = build_program()
    in_maps = [_host_inputs(c, inputs) for c in range(8)]
    res = run_bass_kernel_spmd(nc, in_maps, core_ids=list(range(8)))
    return np.stack([r["out"] for r in res.results]).astype(np.float32)


# version 07
# speedup vs baseline: 1.4860x; 1.1299x over previous
from contextlib import ExitStack

import numpy as np
import ml_dtypes

import concourse.bass as bass
import concourse.mybir as mybir
from concourse.bass_utils import run_bass_kernel_spmd

F32 = mybir.dt.float32
BF16 = mybir.dt.bfloat16
I8 = mybir.dt.int8
AF = mybir.ActivationFunctionType
ALU = mybir.AluOpType
AX = mybir.AxisListType

S = 2048
D = 1024
NT = 16
NC = 8
H = 16
HD = 64
LN_EPS = 1e-5
GROUPS = [(0, 6), (6, 11), (11, 16)]
GROUPS_M = [(0, 8), (8, 16)]
RING = 6


class Sem:
    def __init__(self, nc, es, name):
        self.h = es.enter_context(nc.semaphore(name))
        self.n = 0

    def inc(self, instr, k=1):
        instr.then_inc(self.h, k)
        self.n += k
        return self.n


def build_program(taps=(), stage=99):
    taps = set(taps)
    nc = bass.Bass("TRN2", target_bir_lowering=False, debug=False)
    es = ExitStack()
    sems = []
    pre_sems = {}

    def sem(name):
        if name in pre_sems:
            return pre_sems.pop(name)
        s = Sem(nc, es, name)
        sems.append(s)
        return s

    sb = lambda name, shape, dt=F32: es.enter_context(nc.sbuf_tensor(name, shape, dt))

    def _icount():
        n = 0
        for _ in nc.all_instructions():
            n += 1
        return n

    def dmaN(eng, s, out, in_):
        before = _icount()
        ins = eng.dma_start(out=out, in_=in_)
        ins.then_inc(s.h, 16)
        s.n += 16 * (_icount() - before)
        return ins
    dr = lambda name, shape, dt=F32, kind="ExternalInput": nc.dram_tensor(name, shape, dt, kind=kind).ap()
    tap_outs = {}

    def tap(name, shape, dt=F32):
        if name in taps:
            tap_outs[name] = dr("tap_" + name, shape, dt, kind="ExternalOutput")
            return tap_outs[name]
        return None

    x_in = dr("x", [S, D])
    out_hbm = dr("out", [S, D], kind="ExternalOutput")
    ident_in = dr("ident", [128, 128], BF16)
    masks_in = dr("masks", [128, 256], BF16)
    gate_w_in = dr("gate_w", [D, 8])
    ln1g_in = dr("ln1g", [128, D]); ln1b_in = dr("ln1b", [128, D])
    ln2g_in = dr("ln2g", [128, D], BF16); ln2b_in = dr("ln2b", [128, D], BF16)
    ffn_w1_in = dr("ffn_w1", [D, 4096]); ffn_w2_in = dr("ffn_w2", [4096, D])
    ffn_b1_in = dr("ffn_b1", [128, 32]); ffn_b2_in = dr("ffn_b2", [128, D], BF16)
    moe_w1_in = dr("moe_w1", [8, D, 4096]); moe_w2_in = dr("moe_w2", [8, 4096, D])
    moe_b1_in = dr("moe_b1", [8, 128, 32]); moe_b2_in = dr("moe_b2", [8, D])
    wk_in = dr("wk", [D, D]); wv_in = dr("wv", [D, D]); wq_in = dr("wq", [D, D])
    snw_in = dr("snw", [128, 8]); rnw_in = dr("rnw", [128, 8])
    knw_in = dr("knw", [128, D]); qnw_in = dr("qnw", [128, D])
    mw0_in = dr("mw0", [64, 256]); mw1_in = dr("mw1", [256, 64])
    atw_in = dr("atw", [D, 32]); atb_in = dr("atb", [128, 32])
    wqt_in = dr("wqt", [D, D], I8)
    wsc_in = dr("wsc", [128, D]); wqb_in = dr("wqb", [128, D])
    scr_k = dr("scr_k", [H, S, HD], BF16, kind="Internal")
    scr_q = dr("scr_q", [H, S, HD], BF16, kind="Internal")

    with es:
        x_sb = sb("x_sb", [128, NT * D])
        ar1 = sb("ar1", [128, 16384], BF16)
        fm2 = sb("fm2", [128, 16384], BF16)
        ar2 = sb("ar2", [128, 16384], BF16)
        ar3 = sb("ar3", [128, 6144])
        vring = sb("vring", [128, RING * H * 65], BF16)
        attn_sb = sb("attn_sb", [128, 1280], BF16)
        ident = sb("ident_sb", [128, 128], BF16)
        masks = sb("masks_sb", [128, 256], BF16)
        cst = sb("cst", [128, 8])
        stat0 = sb("stat0", [128, NT * 2])
        stat1 = sb("stat1", [128, NT * 2])
        rstd0 = sb("rstd0", [128, NT])
        rstd1 = sb("rstd1", [128, NT])
        srr = sb("srr", [128, NT])
        sqrtv = sb("sqrtv", [128, NT])
        comb = sb("comb", [128, NT * 8])
        negc = sb("negc", [128, NT * H])
        gwf32 = sb("gwf32", [128, NC * 8])
        b1moe = sb("b1moe", [128, 8 * 32])
        b1ffn = sb("b1ffn", [128, 32])
        atheta = sb("atheta", [128, NT * 32], BF16)
        amrep = sb("amrep", [128, 16])
        mw0b = sb("mw0b", [64, 256], BF16)
        mw1b = sb("mw1b", [128, 128], BF16)
        mw1T = sb("mw1T", [64, 256], BF16)
        onesb = sb("onesb", [128, 2], BF16)
        ident32 = sb("ident32_sb", [128, 128])
        small = sb("small", [128, 64])
        bnsc = sb("bnsc", [128, 12])

        t_xn0f = tap("xn0f", [128, NC * S], BF16)
        t_negc = tap("negc", [128, NT * H])
        t_comb = tap("comb", [128, NT * 8])
        t_g = tap("g", [128, NT * D], BF16)
        t_x1f = tap("x1f", [128, 16384], BF16)
        t_xl0 = tap("xl0", [128, NT * D])
        t_vals = tap("vals", [128, 16384], BF16)
        t_xn1f = tap("xn1f", [128, 16384], BF16)
        t_s1 = tap("s1", [128, 16384], BF16)

        LD = sem("LD"); LG = sem("LG")
        VE_PRE = sem("VE_PRE"); SC_SQ = sem("SC_SQ"); VE_RS = sem("VE_RS")
        VE_TILE = sem("VE_TILE"); PE_TR = sem("PE_TR"); VE_EVAC = sem("VE_EVAC")
        PE_GATE = sem("PE_GATE"); VE_LG = sem("VE_LG"); SC_SIG = sem("SC_SIG"); VE_COMB = sem("VE_COMB")
        VE_CBB = sem("VE_CBB"); PE_CT = sem("PE_CT"); VE_CT = sem("VE_CT")
        LW1a = sem("LW1a"); LW1b = sem("LW1b"); LW2 = sem("LW2"); FW2 = sem("FW2")
        PE_H = sem("PE_H"); SC_H = sem("SC_H"); PE_O = sem("PE_O"); VE_O = sem("VE_O")
        SC_SIL = sem("SC_SIL"); PE_XI = sem("PE_XI"); VE_XI = sem("VE_XI")
        PE_S = sem("PE_S"); SC_E = sem("SC_E"); PE_T = sem("PE_T"); VE_P = sem("VE_P")
        PE_A = sem("PE_A"); VE_Y = sem("VE_Y"); VE_L1 = sem("VE_L1"); SC_L1 = sem("SC_L1")
        VE_X1 = sem("VE_X1"); PE_X1 = sem("PE_X1"); VE_XE = sem("VE_XE"); VE_RB = sem("VE_RB")
        VE_EP = sem("VE_EP"); SC_L2 = sem("SC_L2")
        LDD = sem("LDD"); LDG = sem("LDG"); VE_D = sem("VE_D"); SC_D = sem("SC_D"); PE_D = sem("PE_D")
        VE_DE = sem("VE_DE"); SD = sem("SD"); PE_AM = sem("PE_AM"); VE_AM = sem("VE_AM")
        LKH = sem("LKH"); VE_E = sem("VE_E"); PE_E = sem("PE_E"); SC_E2 = sem("SC_E2")
        GP_E = sem("GP_E"); VE_S1 = sem("VE_S1")
        LF = sem("LF"); VE_F = sem("VE_F"); PE_F = sem("PE_F"); SC_F = sem("SC_F")
        ST = sem("ST")

        for _n in ["VE_WF", "VE_K", "SC_AT", "VE_AT", "SDa", "SDb", "LKH2", "LQH2",
                   "VE_E1", "PE_E1", "SC_E1", "GP_E1", "VE_S1b", "LF2", "LF2g",
                   "VE_F1", "PE_F1", "SC_F1", "STa", "STb"]:
            pre_sems[_n] = Sem(nc, es, _n)
            sems.append(pre_sems[_n])
        for s_ in sems:
            nc.sync.sem_clear(s_.h)
        nc.all_engine_barrier()

        PH_pre = nc.sync
        blk = es.enter_context(nc.Block())
        PH = sem("PH")
        nc.sync.sem_clear(PH.h)
        nc.all_engine_barrier()
        ph_state = [0]

        def phase_join():
            ph_state[0] += 5
            tgt = ph_state[0]
            for ename in ("vector", "scalar", "tensor", "gpsimd", "sync"):
                def mk(e, tgt=tgt):
                    e.drain().then_inc(PH.h, 1)
                    e.wait_ge(PH.h, tgt)
                getattr(blk, ename)(mk)

        @blk.vector
        def _(v):
            for _ in range(48):
                v.memset(ar1[:], 0.0)
                v.drain()

        xn0f = ar1
        xn0t = fm2[:, 0:D]
        xn0t_b = fm2[:, D:2 * D]
        xn0t32 = ar3[:, 0:D]
        sqt = ar3[:, D:2 * D]
        lgs = ar3[:, 2048:2176]
        mlg = ar3[:, 2176:2304]
        gst = ar3[:, 2304:2368]
        x32fm = [ar3[:, 2432:3456], ar3[:, 3456:4480]]
        combb = fm2[:, 2 * D:2 * D + 128]
        combT = vring[0:8, 2048:4096]
        b2moeb = vring[0:8, 4096:5120]

        es_psA = ExitStack()
        p_tr = [es_psA.enter_context(nc.psum_tensor("p_tr0", [128, 1024], BF16)),
                es_psA.enter_context(nc.psum_tensor("p_tr1", [128, 1024], BF16))]
        p_gate = es_psA.enter_context(nc.psum_tensor("p_gate", [128, 512]))
        p_t32 = [es_psA.enter_context(nc.psum_tensor("p_t32a", [128, 1024])),
                 es_psA.enter_context(nc.psum_tensor("p_t32b", [128, 1024]))]

        @blk.sync
        def _(sync):
            x3 = x_in.rearrange("(t p) d -> p t d", p=128)
            dmaN(sync, LD, x_sb[:].rearrange("p (t d) -> p t d", d=D), x3)
            dmaN(sync, LD, ident[:], ident_in)
            dmaN(sync, LD, masks[:], masks_in)
            dmaN(sync, LD, b1moe[:].rearrange("p (e c) -> p e c", c=32),
                 moe_b1_in.rearrange("e p c -> p e c"))
            dmaN(sync, LD, b1ffn[:], ffn_b1_in)

        @blk.gpsimd
        def _(gp):
            gw = gate_w_in.rearrange("(c p) e -> p c e", p=128)
            dmaN(gp, LG, gwf32[:].rearrange("p (c e) -> p c e", e=8), gw)
            dmaN(gp, LG, b2moeb.rearrange("e (o n) -> e o n", o=1),
                 moe_b2_in.rearrange("e (o n) -> e o n", o=1))

        @blk.vector
        def _(v):
            v.memset(cst[:, 0:1], LN_EPS)
            v.memset(cst[:, 1:2], 1e-6)
            v.memset(cst[:, 2:3], 1e-8)
            v.memset(cst[:, 3:4], 1e-34)
            v.memset(cst[:, 4:5], 1.0 / 64.0)
            v.memset(cst[:, 5:6], 1.0)
            v.memset(cst[:, 6:7], -0.5)
            v.memset(cst[:, 7:8], 0.3989422804014327)
            v.memset(onesb[:], 1.0)
            v.wait_ge(LD.h, LD.n)
            v.tensor_copy(ident32[:], ident[:])
            v.drain()
            for t in range(NT):
                xt = x_sb[:, t * D:(t + 1) * D]
                v.bn_stats(bnsc[:, 0:6], xt[:, 0:512])
                v.bn_stats(bnsc[:, 6:12], xt[:, 512:1024])
                v.drain()
                v.bn_aggr(stat0[:, 2 * t:2 * t + 2], bnsc[:].rearrange("p (g s) -> p g s", s=12))
                v.drain()
            VE_PRE.inc(v.drain())

        @blk.scalar
        def _(s):
            s.wait_ge(VE_PRE.h, VE_PRE.n)
            ins = s.activation(out=sqrtv[:].rearrange("p (t o) -> p t o", o=1),
                               in_=stat0[:].rearrange("p (t s) -> p t s", s=2)[:, :, 1:2],
                               func=AF.Sqrt, bias=cst[:, 0:1], scale=1.0)
            SC_SQ.inc(ins)

        @blk.vector
        def _(v):
            v.wait_ge(SC_SQ.h, SC_SQ.n)
            VE_RS.inc(v.reciprocal(rstd0[:], sqrtv[:]))

        def comb_pt1(v, t):
            v.wait_ge(PE_GATE.h, t + 1)
            lg = lgs[:, 0:8]; ml = mlg[:, t * 8:(t + 1) * 8]
            m1 = gst[:, 4 * t:4 * t + 1]; m2 = gst[:, 4 * t + 1:4 * t + 2]
            w2c = gst[:, 4 * t + 2:4 * t + 3]
            cm = comb[:, t * 8:(t + 1) * 8]
            v.tensor_copy(lg, p_gate[:, 0:8])
            v.drain()
            ins = v.reduce_max(m1, lg, axis=AX.X)
            VE_LG.inc(ins)
            v.drain()
            v.tensor_scalar(cm, lg, m1, None, op0=ALU.is_ge)
            v.drain()
            v.scalar_tensor_tensor(ml, cm, -1e30, lg, op0=ALU.mult, op1=ALU.add)
            v.drain()
            v.reduce_max(m2, ml, axis=AX.X)
            v.drain()
            ins = v.tensor_sub(w2c, m2, m1)
            VE_LG.inc(ins)
            v.drain()

        @blk.vector
        def _(v):
            v.wait_ge(VE_RS.h, VE_RS.n)
            for t in range(NT):
                if t >= 1:
                    v.wait_ge(PE_TR.h, t)
                xt = x_sb[:, t * D:(t + 1) * D]
                mu = stat0[:, 2 * t:2 * t + 1]
                rs = rstd0[:, t:t + 1]
                dst = xn0t if t % 2 == 0 else xn0t_b
                v.tensor_scalar(dst, xt, mu, rs, op0=ALU.subtract, op1=ALU.mult)
                v.tensor_scalar(xn0t32, xt, mu, rs, op0=ALU.subtract, op1=ALU.mult)
                v.drain()
                v.tensor_mul(sqt, xn0t32, xn0t32)
                v.drain()
                ins = v.tensor_reduce(
                    negc[:, t * H:(t + 1) * H].rearrange("p (h o) -> p h o", o=1),
                    sqt.rearrange("p (h d) -> p h d", d=64),
                    axis=AX.X, op=ALU.add, negate=True)
                VE_TILE.inc(ins)
                if t >= 1:
                    v.wait_ge(PE_TR.h, t)
                    dstf = xn0f[:].rearrange("p (c s) -> p c s", s=S)[:, :, (t - 1) * 128:t * 128]
                    v.tensor_copy(dstf, p_tr[(t - 1) % 2][:].rearrange("p (c j) -> p c j", j=128))
                    if t >= 2:
                        v.wait_ge(PE_GATE.h, t - 1)
                    VE_EVAC.inc(v.tensor_copy(x32fm[(t - 1) % 2], p_t32[(t - 1) % 2][:]))
                if t >= 2:
                    comb_pt1(v, t - 2)
            v.wait_ge(PE_TR.h, NT)
            dstf = xn0f[:].rearrange("p (c s) -> p c s", s=S)[:, :, (NT - 1) * 128:NT * 128]
            v.tensor_copy(dstf, p_tr[(NT - 1) % 2][:].rearrange("p (c j) -> p c j", j=128))
            v.wait_ge(PE_GATE.h, NT - 1)
            VE_EVAC.inc(v.tensor_copy(x32fm[(NT - 1) % 2], p_t32[(NT - 1) % 2][:]))
            comb_pt1(v, NT - 2)
            comb_pt1(v, NT - 1)
            v.wait_ge(VE_TILE.h, NT)
            v.drain()
            ins = v.tensor_scalar(negc[:], negc[:], -10.0, None, op0=ALU.add)
            VE_TILE.inc(ins)
            v.drain()

        @blk.tensor
        def _(t_):
            t_.wait_ge(LG.h, LG.n)

            def gate_mm(t):
                t_.wait_ge(VE_EVAC.h, t + 1)
                if t >= 1:
                    t_.wait_ge(VE_LG.h, 2 * t - 1)
                ins = None
                for c in range(NC):
                    ins = t_.matmul(p_gate[:, 0:8], x32fm[t % 2][:, c * 128:(c + 1) * 128],
                                    gwf32[:, c * 8:(c + 1) * 8], start=(c == 0), stop=(c == NC - 1))
                PE_GATE.inc(ins)

            for t in range(NT):
                t_.wait_ge(VE_TILE.h, t + 1)
                if t >= 2:
                    t_.wait_ge(VE_EVAC.h, t - 1)
                src = xn0t if t % 2 == 0 else xn0t_b
                dstp = p_tr[t % 2]
                ins = None
                for c in range(NC):
                    ins = t_.transpose(dstp[:, c * 128:(c + 1) * 128], src[:, c * 128:(c + 1) * 128], ident[:])
                for c in range(NC):
                    ins = t_.transpose(p_t32[t % 2][:, c * 128:(c + 1) * 128],
                                       xn0t32[:, c * 128:(c + 1) * 128], ident32[:])
                PE_TR.inc(ins)
                if t >= 1:
                    gate_mm(t - 1)
            gate_mm(NT - 1)


        @blk.scalar
        def _(s):
            for t in range(NT):
                s.wait_ge(VE_LG.h, 2 * (t + 1))
                ins = s.activation(out=gst[:, 4 * t + 2:4 * t + 3],
                                   in_=gst[:, 4 * t + 2:4 * t + 3], func=AF.Sigmoid)
                SC_SIG.inc(ins)

        @blk.vector
        def _(v):
            for t in range(NT):
                v.wait_ge(SC_SIG.h, t + 1)
                m2 = gst[:, 4 * t + 1:4 * t + 2]; w2c = gst[:, 4 * t + 2:4 * t + 3]
                w1c = gst[:, 4 * t + 3:4 * t + 4]
                ml = mlg[:, t * 8:(t + 1) * 8]
                cm = comb[:, t * 8:(t + 1) * 8]
                v.tensor_scalar(w1c, w2c, -1.0, 1.0, op0=ALU.mult, op1=ALU.add)
                v.tensor_scalar(ml, ml, m2, None, op0=ALU.is_ge)
                v.drain()
                v.tensor_scalar(cm, cm, w1c, None, op0=ALU.mult)
                v.drain()
                ins = v.scalar_tensor_tensor(cm, ml, w2c, cm, op0=ALU.mult, op1=ALU.add)
                VE_COMB.inc(ins)
                v.drain()

        @blk.vector
        def _(v):
            v.wait_ge(VE_COMB.h, VE_COMB.n)
            VE_CBB.inc(v.tensor_copy(combb, comb[:]))
            v.drain()

        @blk.tensor
        def _(t_):
            t_.wait_ge(VE_CBB.h, VE_CBB.n)
            ins = None
            for t in range(NT):
                dstp = p_tr[t // 8][0:8, (t % 8) * 128:(t % 8 + 1) * 128]
                ins = t_.transpose(dstp, combb[:, t * 8:(t + 1) * 8], ident[:])
            PE_CT.inc(ins)

        @blk.vector
        def _(v):
            v.wait_ge(PE_CT.h, PE_CT.n)
            v.tensor_copy(combT[:, 0:1024], p_tr[0][0:8, :])
            VE_CT.inc(v.tensor_copy(combT[:, 1024:2048], p_tr[1][0:8, :]))
            v.drain()

        final_waits = [VE_COMB, VE_CT, VE_TILE]

        def emit_mlp(groups, n_experts, w1_dram, w2_dram, rhs_fm, hbuf, w2buf, w1buf,
                     act_fn, bias_col_fn, evac_fn, extra_mm2=None, gate_start=None,
                     tail_pe=None, tail_sc=None, tail_ve=None, h_serial=False):
            LW1P = [LW1a, LW1b]
            base = dict(LW1a=LW1a.n, LW1b=LW1b.n, LW2=LW2.n, FW2=FW2.n, PE_H=PE_H.n,
                        SC_H=SC_H.n, PE_O=PE_O.n, VE_O=VE_O.n)
            n_fq = 4
            n_mt = 8

            @blk.gpsimd
            def _(gp):
                mt_i = 0
                fq_i = 0
                for gi, (t0, t1) in enumerate(groups):
                    for e in range(n_experts):
                        for fq in range(n_fq):
                            if fq_i >= 1:
                                gp.wait_ge(FW2.h, base["FW2"] + fq_i)
                            w2s = w2_dram[e] if n_experts > 1 else w2_dram
                            src = w2s[fq * 1024:(fq + 1) * 1024, :].rearrange("(c p) n -> p c n", p=128)
                            dmaN(gp, LW2, w2buf.rearrange("p (c n) -> p c n", n=1024), src)
                            fq_i += 1
                            for mt in range(n_mt):
                                if mt_i >= 2:
                                    gp.wait_ge(PE_H.h, base["PE_H"] + mt_i - 1)
                                w1s = w1_dram[e] if n_experts > 1 else w1_dram
                                m0 = fq * 1024 + mt * 128
                                srcw = w1s[:, m0:m0 + 128].rearrange("(c p) m -> p c m", p=128)
                                par = mt_i % 2
                                dmaN(gp, LW1P[par],
                                     w1buf[:, par * 1024:(par + 1) * 1024].rearrange(
                                         "p (c m) -> p c m", m=128), srcw)
                                mt_i += 1

            @blk.tensor
            def _(t_):
                if gate_start is not None:
                    for s_, n_ in gate_start:
                        t_.wait_ge(s_.h, n_)
                mt_i = 0
                fq_i = 0
                o_i = 0
                for gi, (t0, t1) in enumerate(groups):
                    ntok = (t1 - t0) * 128
                    nspans = [(i * 512, min(512, ntok - i * 512)) for i in range((ntok + 511) // 512)]
                    if tail_pe is not None and gi >= 1:
                        tail_pe(t_, gi - 1)
                    for e in range(n_experts):
                        for fq in range(n_fq):
                            for mt in range(n_mt):
                                par_ = mt_i % 2
                                t_.wait_ge(LW1P[par_].h,
                                           base["LW1a" if par_ == 0 else "LW1b"] + 16 * (mt_i // 2 + 1))
                                if h_serial:
                                    if mt_i >= 1:
                                        t_.wait_ge(SC_H.h, base["SC_H"] + mt_i)
                                elif mt_i >= 2:
                                    t_.wait_ge(SC_H.h, base["SC_H"] + mt_i - 1)
                                par = 0 if h_serial else mt_i % 2
                                w1slice = w1buf[:, (mt_i % 2) * 1024:(mt_i % 2 + 1) * 1024]
                                ins = None
                                for (soff, slen) in nspans:
                                    for kc in range(NC):
                                        ins = t_.matmul(
                                            p_h[par][:, soff:soff + slen],
                                            w1slice[:, kc * 128:(kc + 1) * 128],
                                            rhs_fm[:, kc * S + t0 * 128 + soff:
                                                   kc * S + t0 * 128 + soff + slen],
                                            start=(kc == 0), stop=(kc == NC - 1))
                                PE_H.inc(ins)
                                mt_i += 1
                            t_.wait_ge(LW2.h, base["LW2"] + 16 * (fq_i + 1))
                            t_.wait_ge(SC_H.h, base["SC_H"] + mt_i)
                            for tt in range(t0, t1):
                                if o_i >= 2:
                                    t_.wait_ge(VE_O.h, base["VE_O"] + o_i - 1)
                                par = o_i % 2
                                lt = (tt - t0) * 128
                                ins = None
                                has_extra = (extra_mm2 is not None and e == n_experts - 1
                                             and fq == n_fq - 1)
                                for ns in range(2):
                                    for fc in range(8):
                                        stop = (fc == 7 and not has_extra)
                                        ins = t_.matmul(
                                            p_o[par][:, ns * 512:(ns + 1) * 512],
                                            hbuf[:, fc * 1024 + lt: fc * 1024 + lt + 128],
                                            w2buf[:, fc * 1024 + ns * 512: fc * 1024 + (ns + 1) * 512],
                                            start=(fc == 0), stop=stop)
                                if has_extra:
                                    ins = extra_mm2(t_, tt, par)
                                PE_O.inc(ins)
                                o_i += 1
                            FW2.inc(t_.drain())
                            fq_i += 1
                if tail_pe is not None:
                    tail_pe(t_, len(groups) - 1)

            @blk.scalar
            def _(s):
                mt_i = 0
                for gi, (t0, t1) in enumerate(groups):
                    ntok = (t1 - t0) * 128
                    if tail_sc is not None and gi >= 1:
                        tail_sc(s, gi - 1)
                    for e in range(n_experts):
                        for fq in range(n_fq):
                            for mt in range(n_mt):
                                s.wait_ge(PE_H.h, base["PE_H"] + mt_i + 1)
                                par = 0 if h_serial else mt_i % 2
                                ins = s.activation(out=hbuf[:, mt * 1024:mt * 1024 + ntok],
                                                   in_=p_h[par][:, 0:ntok],
                                                   func=act_fn, bias=bias_col_fn(e, fq, mt), scale=1.0)
                                SC_H.inc(ins)
                                mt_i += 1
                if tail_sc is not None:
                    tail_sc(s, len(groups) - 1)

            @blk.vector
            def _(v):
                o_i = 0
                for gi, (t0, t1) in enumerate(groups):
                    if tail_ve is not None and gi >= 1:
                        tail_ve(v, gi - 1)
                    for e in range(n_experts):
                        for fq in range(n_fq):
                            for tt in range(t0, t1):
                                v.wait_ge(PE_O.h, base["PE_O"] + o_i + 1)
                                par = o_i % 2
                                evac_fn(v, gi, e, fq, tt, p_o[par][:], (e == 0 and fq == 0))
                                VE_O.inc(v.drain())
                                o_i += 1
                if tail_ve is not None:
                    tail_ve(v, len(groups) - 1)
            return base

        if stage >= 2:
            phase_join()
            es_psA.close()
            es_psM = ExitStack()
            p_h = [es_psM.enter_context(nc.psum_tensor("p_h0", [128, 1024])),
                   es_psM.enter_context(nc.psum_tensor("p_h1", [128, 1024]))]
            p_o = [es_psM.enter_context(nc.psum_tensor("p_o0", [128, 1024])),
                   es_psM.enter_context(nc.psum_tensor("p_o1", [128, 1024]))]

            g_moe = ar2
            acc = ar3
            ar2f = ar2[:].bitcast(F32)

            def acc_t(k):
                return (acc[:, k * 1024:(k + 1) * 1024] if k < 6
                        else ar2f[:, 4096 + (k - 6) * 1024: 4096 + (k - 5) * 1024])
            w2buf_m = fm2[:, 0:8192]
            hbuf_m = fm2[:, 8192:16384]
            w1buf_m = vring[:, 0:2048]

            sil_cum = []
            tot = 0
            for (t0, t1) in GROUPS_M:
                tot += (t1 - t0)
                sil_cum.append(tot)
            sc_sil_base = SC_SIL.n

            def moe_evac(v, gi, e, fq, tt, p_ap, first):
                if first and gi >= 1:
                    v.wait_ge(SC_SIL.h, sc_sil_base + sil_cum[gi - 1])
                t0 = GROUPS_M[gi][0]
                accs = acc_t(tt - t0)
                cmcol = comb[:, tt * 8 + e:tt * 8 + e + 1]
                if first:
                    v.tensor_scalar(accs, p_ap, cmcol, None, op0=ALU.mult)
                else:
                    v.scalar_tensor_tensor(accs, p_ap, cmcol, accs, op0=ALU.mult, op1=ALU.add)

            def moe_b2_mm(t_, tt, par):
                ins = None
                for ns in range(2):
                    ins = t_.matmul(p_o[par][:, ns * 512:(ns + 1) * 512],
                                    combT[:, tt * 128:(tt + 1) * 128],
                                    b2moeb[:, ns * 512:(ns + 1) * 512],
                                    start=False, stop=True)
                return ins

            ve_o_base_holder = [0]

            def moe_tail_sc(s, gi):
                t0, t1 = GROUPS_M[gi]
                s.wait_ge(VE_O.h, ve_o_base_holder[0] + 8 * 4 * sil_cum[gi])
                order = ([tt for tt in range(t0, t1) if tt - t0 >= 6]
                         + [tt for tt in range(t0, t1) if tt - t0 < 6]) if gi >= 1                     else list(range(t0, t1))
                for tt in order:
                    ins = s.activation(out=g_moe[:, tt * D:(tt + 1) * D],
                                       in_=acc_t(tt - t0),
                                       func=AF.Silu)
                    SC_SIL.inc(ins)
                    if gi >= 1 and tt - t0 >= 6:
                        s.drain()

            ve_o_base_holder[0] = VE_O.n
            moe_base = emit_mlp(GROUPS_M, 8, moe_w1_in, moe_w2_in, xn0f, hbuf_m, w2buf_m, w1buf_m,
                                AF.Gelu,
                                lambda e, fq, mt: b1moe[:, e * 32 + fq * 8 + mt:e * 32 + fq * 8 + mt + 1],
                                moe_evac, extra_mm2=moe_b2_mm,
                                gate_start=[(LG, LG.n), (VE_CT, VE_CT.n)],
                                tail_sc=moe_tail_sc)

            final_waits.append(SC_SIL)

        if stage >= 3:
            phase_join()
            es_psM.close()
            es_psB = ExitStack()
            p_s = es_psB.enter_context(nc.psum_tensor("p_s", [128, 1024]))
            p_tb = es_psB.enter_context(nc.psum_tensor("p_tb", [128, 1024], BF16))
            p_y = es_psB.enter_context(nc.psum_tensor("p_y", [128, 512]))
            p_x1 = es_psB.enter_context(nc.psum_tensor("p_x1", [128, 1024]))

            x1f = fm2
            ybuf = ar3[:, 0:D]
            ln1g_sb = ar3[:, D:2 * D]
            ln1b_sb = ar3[:, 2 * D:3 * D]
            vtmp = ar3[:, 3 * D:4 * D]
            p_sb = attn_sb[:, 0:640]
            pt_sb = attn_sb[:, 640:1280]

            def win(qt):
                j0 = max(0, qt - 2)
                j1 = min(NT - 1, qt + 2)
                return j0, j1

            @blk.sync
            def _(sync):
                dmaN(sync, LD, ln1g_sb, ln1g_in)
                dmaN(sync, LD, ln1b_sb, ln1b_in)
            ld_b = LD.n

            @blk.vector
            def _(v):
                v.wait_ge(LD.h, ld_b)
                v.memset(vring[:, 0:RING * H * 65], 1.0)
                v.tensor_copy(ident32[:], ident[:])
                v.drain()
                def build_vtile(j):
                    slot = j % RING
                    v.tensor_scalar(vtmp, x_sb[:, j * D:(j + 1) * D],
                                    stat0[:, 2 * j:2 * j + 1], rstd0[:, j:j + 1],
                                    op0=ALU.subtract, op1=ALU.mult)
                    v.drain()
                    ins = v.tensor_copy(
                        vring[:, slot * H * 65:(slot + 1) * H * 65].rearrange(
                            "p (h e) -> p h e", e=65)[:, :, 0:64],
                        vtmp.rearrange("p (h d) -> p h d", d=64))
                    VE_RB.inc(ins)
                    v.drain()
                for j in range(3):
                    build_vtile(j)
                for qt in range(NT):
                    j0, j1 = win(qt)
                    nw = j1 - j0 + 1
                    v.tensor_scalar(ybuf, x_sb[:, qt * D:(qt + 1) * D],
                                    stat0[:, 2 * qt:2 * qt + 1], rstd0[:, qt:qt + 1],
                                    op0=ALU.subtract, op1=ALU.mult)
                    VE_Y.inc(v.drain())
                    j = qt + 3
                    if j < NT:
                        if qt >= 3:
                            v.wait_ge(PE_A.h, qt * H)
                        build_vtile(j)
                    for h in range(H):
                        i = qt * H + h
                        v.wait_ge(PE_T.h, i + 1)
                        for jw in range(nw):
                            j_ = j0 + jw
                            srcp = p_tb[:, jw * 128:(jw + 1) * 128]
                            dstp = pt_sb[:, jw * 128:(jw + 1) * 128]
                            if j_ == qt - 2:
                                v.tensor_mul(dstp, srcp, masks[:, 0:128])
                            elif j_ == qt + 2:
                                v.tensor_mul(dstp, srcp, masks[:, 128:256])
                            else:
                                v.tensor_copy(dstp, srcp)
                        VE_P.inc(v.drain())
                        v.wait_ge(PE_A.h, i + 1)
                        iv = small[:, 0:1]
                        v.reciprocal(iv, p_y[:, 64:65])
                        v.drain()
                        ins = v.scalar_tensor_tensor(ybuf[:, h * 64:(h + 1) * 64], p_y[:, 0:64],
                                                     iv, ybuf[:, h * 64:(h + 1) * 64],
                                                     op0=ALU.mult, op1=ALU.add)
                        VE_Y.inc(ins)
                        v.drain()
                    v.bn_stats(bnsc[:, 0:6], ybuf[:, 0:512])
                    v.bn_stats(bnsc[:, 6:12], ybuf[:, 512:1024])
                    v.drain()
                    v.bn_aggr(small[:, 1:3], bnsc[:].rearrange("p (g s) -> p g s", s=12))
                    VE_L1.inc(v.drain())
                    v.wait_ge(SC_L1.h, qt + 1)
                    v.reciprocal(small[:, 4:5], small[:, 3:4])
                    v.drain()
                    v.tensor_scalar(ybuf, ybuf, small[:, 1:2], small[:, 4:5],
                                    op0=ALU.subtract, op1=ALU.mult)
                    v.drain()
                    v.tensor_mul(ybuf, ybuf, ln1g_sb)
                    v.drain()
                    v.tensor_add(ybuf, ybuf, ln1b_sb)
                    VE_X1.inc(v.drain())
                    v.wait_ge(PE_X1.h, qt + 1)
                    dstf = x1f[:].rearrange("p (c s) -> p c s", s=S)[:, :, qt * 128:(qt + 1) * 128]
                    ins = v.tensor_copy(dstf, p_x1[:].rearrange("p (c j) -> p c j", j=128))
                    VE_XE.inc(ins)
                    v.drain()

            @blk.scalar
            def _(s):
                for qt in range(NT):
                    j0, j1 = win(qt)
                    nw = j1 - j0 + 1
                    for h in range(H):
                        i = qt * H + h
                        s.wait_ge(PE_S.h, i + 1)
                        if i >= 1:
                            s.wait_ge(PE_T.h, i)
                        ins = s.activation(out=p_sb[:, 0:nw * 128], in_=p_s[:, 0:nw * 128],
                                           func=AF.Exp, bias=negc[:, qt * H + h:qt * H + h + 1],
                                           scale=1.0)
                        SC_E.inc(ins)
                    s.wait_ge(VE_L1.h, qt + 1)
                    ins = s.activation(out=small[:, 3:4], in_=small[:, 2:3],
                                       func=AF.Sqrt, bias=cst[:, 0:1], scale=1.0)
                    SC_L1.inc(ins)

            @blk.tensor
            def _(t_):
                for qt in range(NT):
                    j0, j1 = win(qt)
                    nw = j1 - j0 + 1
                    hc_of = lambda h: (h // 2) * S
                    for h in range(H):
                        i = qt * H + h
                        bp = (h % 2) * 64
                        if i >= 1:
                            t_.wait_ge(SC_E.h, i)
                        q_ap = xn0f[bp:bp + 64, hc_of(h) + qt * 128: hc_of(h) + (qt + 1) * 128]
                        ins = None
                        ncols = nw * 128
                        for soff in range(0, ncols, 512):
                            slen = min(512, ncols - soff)
                            k_ap = xn0f[bp:bp + 64,
                                        hc_of(h) + j0 * 128 + soff: hc_of(h) + j0 * 128 + soff + slen]
                            ins = t_.matmul(p_s[:, soff:soff + slen], q_ap, k_ap,
                                            start=True, stop=True)
                        PE_S.inc(ins)
                        t_.wait_ge(SC_E.h, i + 1)
                        if i >= 1:
                            t_.wait_ge(VE_P.h, i)
                        ins = None
                        for jw in range(nw):
                            ins = t_.transpose(p_tb[:, jw * 128:(jw + 1) * 128],
                                               p_sb[:, jw * 128:(jw + 1) * 128], ident[:])
                        PE_T.inc(ins)
                        t_.wait_ge(VE_P.h, i + 1)
                        t_.wait_ge(VE_RB.h, min(qt + 3, NT))
                        t_.wait_ge(VE_Y.h, (qt + 1) + i)
                        ins = None
                        for jw in range(nw):
                            j_ = j0 + jw
                            slot = j_ % RING
                            ins = t_.matmul(p_y[:, 0:65],
                                            pt_sb[:, jw * 128:(jw + 1) * 128],
                                            vring[:, slot * H * 65 + h * 65: slot * H * 65 + (h + 1) * 65],
                                            start=(jw == 0), stop=(jw == nw - 1))
                        PE_A.inc(ins)
                    t_.wait_ge(VE_X1.h, qt + 1)
                    if qt >= 1:
                        t_.wait_ge(VE_XE.h, qt)
                    ins = None
                    for c in range(NC):
                        ins = t_.transpose(p_x1[:, c * 128:(c + 1) * 128],
                                           ybuf[:, c * 128:(c + 1) * 128], ident32[:])
                    PE_X1.inc(ins)

            final_waits += [VE_XE]

        if stage >= 4:
            phase_join()
            es_psB.close()
            es_psC = ExitStack()
            p_h0c = es_psC.enter_context(nc.psum_tensor("pc_h0", [128, 1024]))
            p_h = [p_h0c, p_h0c]
            p_o = [es_psC.enter_context(nc.psum_tensor("pc_o0", [128, 1024])),
                   es_psC.enter_context(nc.psum_tensor("pc_o1", [128, 1024]))]
            p_xb = es_psC.enter_context(nc.psum_tensor("p_xb", [128, 1024], BF16))

            acc = ar3
            w2buf_c = ar1[:, 0:8192]
            hbuf_c = ar1[:, 8192:16384]
            w1buf_c = vring[:, 0:2048]
            ln2g_sb = vring[:, 2048:3072]
            ln2b_sb = vring[:, 3072:4096]

            @blk.sync
            def _(sync):
                dmaN(sync, LD, ln2g_sb, ln2g_in)
                dmaN(sync, LD, ln2b_sb, ln2b_in)
                dmaN(sync, LD, vring[:, 4096:5120], ffn_b2_in)
            ld_c = LD.n
            b2f_sb = vring[:, 4096:5120]

            ep_cum = []
            tot = 0
            for (t0, t1) in GROUPS:
                tot += (t1 - t0)
                ep_cum.append(tot)
            ve_ep_base = VE_EP.n
            pe_xi_base = PE_XI.n
            ve_o_base_c = VE_O.n
            ve_l1_base = VE_L1.n

            def ffn_evac(v, gi, e, fq, tt, p_ap, first):
                if first and gi >= 1 and tt == GROUPS[gi][0]:
                    v.wait_ge(VE_EP.h, ve_ep_base + ep_cum[gi - 1])
                t0 = GROUPS[gi][0]
                accs = acc_t(tt - t0)
                if first:
                    v.tensor_copy(accs, p_ap)
                else:
                    v.tensor_add(accs, accs, p_ap)

            ffn_veo_cum = []
            tot = 0
            for (t0, t1) in GROUPS:
                tot += 4 * (t1 - t0)
                ffn_veo_cum.append(tot)

            def ffn_tail_pe(t_, gi):
                t0, t1 = GROUPS[gi]
                for tt in range(t0, t1):
                    k = (sum(b - a for a, b in GROUPS[:gi]) + (tt - t0))
                    t_.wait_ge(VE_O.h, ve_o_base_c + ffn_veo_cum[gi])
                    if k >= 1:
                        t_.wait_ge(VE_XI.h, k)
                    ins = None
                    for c in range(NC):
                        ins = t_.transpose(p_xb[:, c * 128:(c + 1) * 128],
                                           x1f[:, c * S + tt * 128: c * S + (tt + 1) * 128],
                                           ident[:])
                    PE_XI.inc(ins)

            def ffn_tail_sc(s, gi):
                t0, t1 = GROUPS[gi]
                for tt in range(t0, t1):
                    k = (sum(b - a for a, b in GROUPS[:gi]) + (tt - t0))
                    s.wait_ge(VE_L1.h, ve_l1_base + k + 1)
                    ins = s.activation(out=small[:, 8:9], in_=small[:, 7:8],
                                       func=AF.Sqrt, bias=cst[:, 0:1], scale=1.0)
                    SC_L2.inc(ins)

            def ffn_tail_ve(v, gi):
                t0, t1 = GROUPS[gi]
                for tt in range(t0, t1):
                    k = (sum(b - a for a, b in GROUPS[:gi]) + (tt - t0))
                    accs = acc_t(tt - t0)
                    v.wait_ge(PE_XI.h, pe_xi_base + k + 1)
                    v.tensor_add(accs, accs, p_xb[:])
                    VE_XI.inc(v.drain())
                    v.tensor_add(accs, accs, b2f_sb)
                    v.drain()
                    v.bn_stats(bnsc[:, 0:6], accs[:, 0:512])
                    v.bn_stats(bnsc[:, 6:12], accs[:, 512:1024])
                    v.drain()
                    v.bn_aggr(small[:, 6:8], bnsc[:].rearrange("p (g s) -> p g s", s=12))
                    VE_L1.inc(v.drain())
                    v.wait_ge(SC_L2.h, k + 1)
                    v.reciprocal(small[:, 9:10], small[:, 8:9])
                    v.drain()
                    v.tensor_scalar(accs, accs, small[:, 6:7], small[:, 9:10],
                                    op0=ALU.subtract, op1=ALU.mult)
                    v.drain()
                    v.tensor_mul(accs, accs, ln2g_sb)
                    v.drain()
                    v.tensor_add(accs, accs, ln2b_sb)
                    v.drain()
                    v.tensor_mul(accs, accs, ar2[:, tt * D:(tt + 1) * D])
                    v.drain()
                    ins = v.tensor_add(x_sb[:, tt * D:(tt + 1) * D],
                                       x_sb[:, tt * D:(tt + 1) * D], accs)
                    VE_EP.inc(ins)
                    v.drain()

            ffn_base = emit_mlp(GROUPS, 1, ffn_w1_in, ffn_w2_in, x1f, hbuf_c, w2buf_c, w1buf_c,
                                AF.Relu, lambda e, fq, mt: b1ffn[:, fq * 8 + mt:fq * 8 + mt + 1],
                                ffn_evac, gate_start=[(LD, ld_c)],
                                tail_pe=ffn_tail_pe, tail_sc=ffn_tail_sc, tail_ve=ffn_tail_ve,
                                h_serial=True)

            final_waits += [VE_EP]

        if stage >= 5:
            phase_join()
            es_psC.close()
            es_psD = ExitStack()
            p_d = [es_psD.enter_context(nc.psum_tensor("p_d0", [128, 1024])),
                   es_psD.enter_context(nc.psum_tensor("p_d1", [128, 1024]))]
            p_dt = [es_psD.enter_context(nc.psum_tensor("p_dt0", [128, 1024], BF16)),
                    es_psD.enter_context(nc.psum_tensor("p_dt1", [128, 1024], BF16))]
            p_am = es_psD.enter_context(nc.psum_tensor("p_am", [128, 512]))

            VE_WF = sem("VE_WF"); VE_K = sem("VE_K"); SC_AT = sem("SC_AT"); VE_AT = sem("VE_AT")
            SDP = [sem("SDa"), sem("SDb")]
            sd_counts_g = [[], []]
            nc2_clear = [VE_WF, VE_K, SC_AT, VE_AT]

            xn1nF = fm2
            wbD = ar1[:, 0:8192]
            atwb = vring[:, 0:256]
            xn1nt = [vring[:, 256:1280], vring[:, 1280:2304]]
            ktile_b = [vring[:, 2304:3328], vring[:, 3328:4352]]
            onesrow = vring[0:1, 4352:4480]
            amrow = vring[0:1, 4480:4496]
            keysc = ar3[:, 0:1024]
            ksq = ar3[:, 1024:2048]
            knw_sb = ar3[:, 2048:3072]
            qnw_sb = ar3[:, 3072:4096]
            at32 = ar3[:, 4096:4128]
            atb_sb = b1ffn
            snw_c = lambda c: small[:, 32 + c:33 + c]
            rnw_c = lambda c: small[:, 40 + c:41 + c]

            @blk.sync
            def _(sync):
                dmaN(sync, LDD, atb_sb[:], atb_in)
                dmaN(sync, LDD, small[:, 32:40], snw_in)
                dmaN(sync, LDD, small[:, 40:48], rnw_in)
                dmaN(sync, LDD, knw_sb, knw_in)
                dmaN(sync, LDD, qnw_sb, qnw_in)
            ldd_n = LDD.n

            @blk.gpsimd
            def _(gp):
                wkc = wk_in.rearrange("(c p) n -> p c n", p=128)
                dmaN(gp, LDG, wbD.rearrange("p (c n) -> p c n", n=1024), wkc)
            ldg_wk = LDG.n

            @blk.vector
            def _(v):
                for t in range(NT):
                    xt = x_sb[:, t * D:(t + 1) * D]
                    v.bn_stats(bnsc[:, 0:6], xt[:, 0:512])
                    v.bn_stats(bnsc[:, 6:12], xt[:, 512:1024])
                    v.drain()
                    v.bn_aggr(stat1[:, 2 * t:2 * t + 2], bnsc[:].rearrange("p (g s) -> p g s", s=12))
                    v.drain()
                VE_D.inc(v.drain())

            @blk.scalar
            def _(s):
                s.wait_ge(VE_D.h, VE_D.n)
                ins = s.activation(out=sqrtv[:].rearrange("p (t o) -> p t o", o=1),
                                   in_=stat1[:].rearrange("p (t s) -> p t s", s=2)[:, :, 1:2],
                                   func=AF.Sqrt, bias=cst[:, 0:1], scale=1.0)
                SC_D.inc(ins)

            @blk.vector
            def _(v):
                v.wait_ge(SC_D.h, SC_D.n)
                v.reciprocal(rstd1[:], sqrtv[:])
                v.drain()
                varS = stat1[:].rearrange("p (t s) -> p t s", s=2)[:, :, 1:2]
                v.tensor_tensor(small[:, 48:64].rearrange("p (t o) -> p t o", o=1), varS,
                                rstd1[:].rearrange("p (t o) -> p t o", o=1), op=ALU.mult)
                v.drain()
                v.tensor_mul(small[:, 48:64], small[:, 48:64], rstd1[:])
                VE_D.inc(v.drain())

            @blk.scalar
            def _(s):
                s.wait_ge(VE_D.h, VE_D.n)
                ins = s.activation(out=small[:, 0:16], in_=small[:, 48:64],
                                   func=AF.Sqrt, bias=cst[:, 1:2], scale=1.0)
                SC_D.inc(ins)

            @blk.vector
            def _(v):
                v.wait_ge(SC_D.h, SC_D.n)
                v.wait_ge(LDD.h, ldd_n)
                v.reciprocal(small[:, 16:32], small[:, 0:16])
                v.drain()
                v.tensor_mul(srr[:], small[:, 16:32], rstd1[:])
                v.drain()
                v.wait_ge(LDG.h, ldg_wk)
                for c in range(NC):
                    v.tensor_scalar(wbD[:, c * 1024:(c + 1) * 1024], wbD[:, c * 1024:(c + 1) * 1024],
                                    snw_c(c), None, op0=ALU.mult)
                VE_WF.inc(v.drain())
                for t in range(NT):
                    if t >= 2:
                        v.wait_ge(PE_D.h, t - 1)
                    v.tensor_scalar(xn1nt[t % 2], x_sb[:, t * D:(t + 1) * D],
                                    stat1[:, 2 * t:2 * t + 1], srr[:, t:t + 1],
                                    op0=ALU.subtract, op1=ALU.mult)
                    VE_D.inc(v.drain())
                    if t >= 1:
                        v.wait_ge(PE_D.h, t)
                        dstf = xn1nF[:].rearrange("p (c s) -> p c s", s=S)[:, :, (t - 1) * 128:t * 128]
                        VE_DE.inc(v.tensor_copy(dstf, p_dt[(t - 1) % 2][:].rearrange("p (c j) -> p c j", j=128)))
                v.wait_ge(PE_D.h, NT)
                dstf = xn1nF[:].rearrange("p (c s) -> p c s", s=S)[:, :, (NT - 1) * 128:NT * 128]
                VE_DE.inc(v.tensor_copy(dstf, p_dt[(NT - 1) % 2][:].rearrange("p (c j) -> p c j", j=128)))

            ve_d_base = VE_D.n - NT

            @blk.tensor
            def _(t_):
                for t in range(NT):
                    t_.wait_ge(VE_D.h, ve_d_base + t + 1)
                    if t >= 2:
                        t_.wait_ge(VE_DE.h, t - 1)
                    ins = None
                    for c in range(NC):
                        ins = t_.transpose(p_dt[t % 2][:, c * 128:(c + 1) * 128],
                                           xn1nt[t % 2][:, c * 128:(c + 1) * 128], ident[:])
                    PE_D.inc(ins)

            ve_de_all = VE_DE.n

            def emit_proj(kind):
                pe_base = PE_D.n
                sc_base = SC_D.n
                ve_base = VE_K.n
                sd_counts = sd_counts_g
                for t in range(NT):
                    @blk.tensor
                    def _(t_, t=t, pe_base=pe_base):
                        if t == 0:
                            t_.wait_ge(VE_DE.h, ve_de_all)
                            t_.wait_ge(VE_WF.h, VE_WF.n)
                        if t >= 2:
                            t_.wait_ge(VE_K.h, ve_base + (t - 1 if kind == 1 else 2 * t - 3))
                        ins = None
                        for ns in range(2):
                            for c in range(NC):
                                ins = t_.matmul(p_d[t % 2][:, ns * 512:(ns + 1) * 512],
                                                xn1nF[:, c * S + t * 128:c * S + (t + 1) * 128],
                                                wbD[:, c * 1024 + ns * 512:c * 1024 + (ns + 1) * 512],
                                                start=(c == 0), stop=(c == NC - 1))
                        PE_D.inc(ins)

                    @blk.vector
                    def _(v, t=t, pe_base=pe_base, ve_base=ve_base, sc_base=sc_base):
                        v.wait_ge(PE_D.h, pe_base + t + 1)
                        if kind != 1 and t >= 1:
                            v.wait_ge(SC_D.h, sc_base + t)
                        if kind == 1:
                            ins = v.tensor_copy(ar2[:, t * 1024:(t + 1) * 1024], p_d[t % 2][:])
                            VE_K.inc(ins)
                            v.drain()
                            return
                        v.tensor_copy(keysc, p_d[t % 2][:])
                        v.drain()
                        v.tensor_mul(ksq, keysc, keysc)
                        v.drain()
                        v.tensor_reduce(small[:, 0:16].rearrange("p (h o) -> p h o", o=1),
                                        ksq.rearrange("p (h d) -> p h d", d=64),
                                        axis=AX.X, op=ALU.add)
                        VE_K.inc(v.drain())

                    if kind != 1:
                        @blk.scalar
                        def _(s, t=t, ve_base=ve_base, sc_base=sc_base):
                            s.wait_ge(VE_K.h, ve_base + 2 * t + 1)
                            ins = s.activation(out=small[:, 16:32], in_=small[:, 0:16],
                                               func=AF.Sqrt, bias=cst[:, 1:2], scale=cst[:, 4:5])
                            SC_D.inc(ins)

                        @blk.vector
                        def _(v, t=t, sc_base=sc_base):
                            v.wait_ge(SC_D.h, sc_base + t + 1)
                            v.reciprocal(small[:, 0:16], small[:, 16:32])
                            v.drain()
                            nwsb = knw_sb if kind == 0 else qnw_sb
                            v.tensor_mul(keysc, keysc, nwsb)
                            v.drain()
                            if sd_counts[t % 2]:
                                v.wait_ge(SDP[t % 2].h, sd_counts[t % 2][-1])
                            ins = v.tensor_tensor(
                                ktile_b[t % 2].rearrange("p (h d) -> p h d", d=64), 
                                keysc.rearrange("p (h d) -> p h d", d=64),
                                small[:, 0:16].rearrange("p (h o) -> p h o", o=1).to_broadcast([128, 16, 64]),
                                op=ALU.mult)
                            VE_K.inc(ins)
                            v.drain()

                        @blk.sync
                        def _(sync, t=t, ve_base=ve_base):
                            sync.wait_ge(VE_K.h, ve_base + 2 * (t + 1))
                            scr = scr_k if kind == 0 else scr_q
                            dst = scr.rearrange("h (t p) d -> t p h d", p=128)[t]
                            dmaN(sync, SDP[t % 2], dst,
                                 ktile_b[t % 2].rearrange("p (h d) -> p h d", d=64))
                            sd_counts[t % 2].append(SDP[t % 2].n)

            emit_proj(0)

            @blk.gpsimd
            def _(gp):
                gp.wait_ge(PE_D.h, PE_D.n)
                wvc = wv_in.rearrange("(c p) n -> p c n", p=128)
                dmaN(gp, LDG, wbD.rearrange("p (c n) -> p c n", n=1024), wvc)

            @blk.vector
            def _(v):
                v.wait_ge(LDG.h, LDG.n)
                for c in range(NC):
                    v.tensor_scalar(wbD[:, c * 1024:(c + 1) * 1024], wbD[:, c * 1024:(c + 1) * 1024],
                                    snw_c(c), None, op0=ALU.mult)
                VE_WF.inc(v.drain())

            @blk.tensor
            def _(t_):
                t_.wait_ge(VE_WF.h, VE_WF.n)
            emit_proj(1)

            @blk.gpsimd
            def _(gp):
                gp.wait_ge(PE_D.h, PE_D.n)
                wqc = wq_in.rearrange("(c p) n -> p c n", p=128)
                dmaN(gp, LDG, wbD.rearrange("p (c n) -> p c n", n=1024), wqc)

            @blk.vector
            def _(v):
                v.wait_ge(LDG.h, LDG.n)
                for c in range(NC):
                    v.tensor_scalar(wbD[:, c * 1024:(c + 1) * 1024], wbD[:, c * 1024:(c + 1) * 1024],
                                    rnw_c(c), None, op0=ALU.mult)
                VE_WF.inc(v.drain())

            @blk.tensor
            def _(t_):
                t_.wait_ge(VE_WF.h, VE_WF.n)
            emit_proj(2)

            @blk.gpsimd
            def _(gp):
                gp.wait_ge(PE_D.h, PE_D.n)
                atc = atw_in.rearrange("(c p) n -> p c n", p=128)
                dmaN(gp, LDG, atwb.rearrange("p (c n) -> p c n", n=32), atc)
                mw0c = mw0_in.rearrange("p n -> p n")
                dmaN(gp, LDG, mw0b[:], mw0c)
                dmaN(gp, LDG, mw1b[:].rearrange("p (k n) -> p k n", n=64),
                     mw1_in.rearrange("(k p) n -> p k n", p=128))

            @blk.vector
            def _(v):
                v.wait_ge(LDG.h, LDG.n)
                for c in range(NC):
                    v.tensor_scalar(atwb[:, c * 32:(c + 1) * 32], atwb[:, c * 32:(c + 1) * 32],
                                    snw_c(c), None, op0=ALU.mult)
                VE_WF.inc(v.drain())

            pe_at_base = PE_D.n
            sc_at_base = SC_AT.n

            @blk.tensor
            def _(t_):
                t_.wait_ge(VE_WF.h, VE_WF.n)
                for t in range(NT):
                    if t >= 1:
                        t_.wait_ge(VE_AT.h, t)
                    ins = None
                    for c in range(NC):
                        ins = t_.matmul(p_am[:, 0:32],
                                        xn1nF[:, c * S + t * 128:c * S + (t + 1) * 128],
                                        atwb[:, c * 32:(c + 1) * 32],
                                        start=(c == 0), stop=(c == NC - 1))
                    PE_D.inc(ins)

            @blk.vector
            def _(v):
                for t in range(NT):
                    v.wait_ge(PE_D.h, pe_at_base + t + 1)
                    if t >= 1:
                        v.wait_ge(SC_AT.h, sc_at_base + t)
                    v.tensor_add(at32, p_am[:, 0:32], atb_sb[:])
                    VE_AT.inc(v.drain())

            @blk.scalar
            def _(s):
                for t in range(NT):
                    s.wait_ge(VE_AT.h, t + 1)
                    ins = s.activation(out=atheta[:, t * 32:(t + 1) * 32], in_=at32,
                                       func=AF.Sigmoid)
                    SC_AT.inc(ins)


            @blk.vector
            def _(v):
                v.wait_ge(SC_AT.h, SC_AT.n)
                ins = v.tensor_scalar(
                    atheta[:].rearrange("p (t g) -> p t g", g=32)[:, :, 16:32],
                    atheta[:].rearrange("p (t g) -> p t g", g=32)[:, :, 16:32],
                    0.02, None, op0=ALU.mult)
                VE_AT.inc(ins)
                v.drain()
                v.memset(onesrow, 1.0)
                v.drain()
                VE_AT.inc(v.drain())

            @blk.tensor
            def _(t_):
                t_.wait_ge(VE_AT.h, VE_AT.n)
                ins = None
                for t in range(NT):
                    ins = t_.matmul(p_am[0:16, 32:33], atheta[:, t * 32:t * 32 + 16],
                                    onesb[:, 0:1], start=(t == 0), stop=(t == NT - 1))
                PE_AM.inc(ins)

            @blk.vector
            def _(v):
                v.wait_ge(PE_AM.h, PE_AM.n)
                v.tensor_scalar(small[0:16, 0:1], p_am[0:16, 32:33], -1.0 / 2048.0, 1.0,
                                op0=ALU.mult, op1=ALU.add)
                v.drain()
                ins = v.tensor_copy(vring[0:16, 4496:4497], small[0:16, 0:1])
                VE_AM.inc(ins)
                v.drain()

            @blk.tensor
            def _(t_):
                t_.wait_ge(VE_AM.h, VE_AM.n)
                ins = t_.transpose(p_dt[0][0:1, 0:16], vring[0:16, 4496:4497], ident[0:16, 0:16])
                PE_AM.inc(ins)

            @blk.vector
            def _(v):
                v.wait_ge(PE_AM.h, PE_AM.n)
                ins = v.tensor_copy(amrow, p_dt[0][0:1, 0:16])
                VE_AM.inc(ins)
                v.drain()

            @blk.tensor
            def _(t_):
                t_.wait_ge(VE_AM.h, VE_AM.n)
                ins = t_.matmul(p_am[:, 48:64], onesrow, amrow, start=True, stop=True)
                PE_AM.inc(ins)
                ins = t_.transpose(p_dt[0][0:64, 128:256], mw1b[:, 0:64], ident[:])
                ins = t_.transpose(p_dt[0][0:64, 256:384], mw1b[:, 64:128], ident[:])
                PE_AM.inc(ins)

            @blk.vector
            def _(v):
                v.wait_ge(PE_AM.h, PE_AM.n)
                v.tensor_copy(amrep[:], p_am[:, 48:64])
                v.tensor_copy(mw1T[:, 0:128], p_dt[0][0:64, 128:256])
                ins = v.tensor_copy(mw1T[:, 128:256], p_dt[0][0:64, 256:384])
                VE_AM.inc(ins)
                v.drain()

            final_waits += [VE_AM, SDP[0], SDP[1]]

        if stage >= 6:
            phase_join()
            es_psD.close()
            es_psE = ExitStack()
            p_a = [es_psE.enter_context(nc.psum_tensor("p_a0", [128, 512])),
                   es_psE.enter_context(nc.psum_tensor("p_a1", [128, 512]))]
            p_b = [es_psE.enter_context(nc.psum_tensor("p_b0", [128, 512])),
                   es_psE.enter_context(nc.psum_tensor("p_b1", [128, 512]))]
            p_tre = [es_psE.enter_context(nc.psum_tensor("p_tre0", [128, 1024], BF16)),
                     es_psE.enter_context(nc.psum_tensor("p_tre1", [128, 1024], BF16))]
            p_g0 = es_psE.enter_context(nc.psum_tensor("p_g0", [128, 512]))
            p_g1 = es_psE.enter_context(nc.psum_tensor("p_g1", [128, 512]))

            LKH = sem("LKH2"); LQH = sem("LQH2")
            VE_E1 = sem("VE_E1"); PE_E1 = sem("PE_E1"); SC_E1 = sem("SC_E1")
            GP_E1 = sem("GP_E1"); VE_S1b = sem("VE_S1b")

            s1tm = fm2
            kh = ar1[:, 0:1024]
            qh = ar1[:, 1024:2048]
            khf = ar1[:, 4096:6144]
            h1f = ar1[:, 6144:10240]
            pre0tm = ar1[:, 10240:14336]
            dhf = ar1[:, 14336:16384]
            d_htm = vring[:, 0:1024]
            w0a = vring[0:64, 1024:1280]
            w0n = vring[0:64, 1280:1536]
            w1a = vring[:, 1536:1664]
            w1n = vring[:, 1664:1792]
            d_h0t = [vring[:, 1792:2048], vring[:, 2048:2304]]
            ggt = [vring[:, 2304:2560], vring[:, 2560:2816]]
            e1t = [vring[:, 2816:3072], vring[:, 3072:3328]]
            cfb = vring[0:1, 3328:3329]
            pre1 = ar3[:, 0:1024]
            sq0t = [ar3[:, 1024:1280], ar3[:, 1280:1536]]
            numt = [ar3[:, 1536:1792], ar3[:, 1792:2048]]
            dA = ar3[:, 2048:3072]
            dB = ar3[:, 3072:4096]
            dC = ar3[:, 4096:5120]
            dD = ar3[:, 5120:6144]

            for h in range(H):
                lkh_n0 = LKH.n
                lqh_n0 = LQH.n

                @blk.sync
                def _(sync, h=h):
                    if h == 0:
                        sync.wait_ge(SDP[0].h, SDP[0].n)
                        sync.wait_ge(SDP[1].h, SDP[1].n)
                    if h >= 1:
                        sync.wait_ge(VE_S1b.h, h)
                    dmaN(sync, LKH, kh.rearrange("p (t d) -> p t d", d=64),
                         scr_k[h].rearrange("(t p) d -> p t d", p=128))
                    dmaN(sync, LQH, qh.rearrange("p (t d) -> p t d", d=64),
                         scr_q[h].rearrange("(t p) d -> p t d", p=128))

                pe_t0 = PE_E1.n
                ve_khf0 = VE_E1.n

                @blk.tensor
                def _(t_, h=h, pe_t0=pe_t0, lkh_n=LKH.n, ve_khf0=ve_khf0):
                    t_.wait_ge(LKH.h, lkh_n)
                    if h >= 1:
                        t_.wait_ge(VE_S1b.h, h)
                    for t in range(NT):
                        if t >= 2:
                            t_.wait_ge(VE_E1.h, ve_khf0 + t - 1)
                        ins = t_.transpose(p_tre[t % 2][0:64, 0:128], kh[:, t * 64:(t + 1) * 64],
                                           ident[:])
                        PE_E1.inc(ins)

                @blk.vector
                def _(v, h=h, pe_t0=pe_t0):
                    for t in range(NT):
                        v.wait_ge(PE_E1.h, pe_t0 + t + 1)
                        ins = v.tensor_copy(khf[0:64, t * 128:(t + 1) * 128], p_tre[t % 2][0:64, 0:128])
                        VE_E1.inc(ins)
                        v.drain()

                pe_a0 = PE_E1.n
                ve_khf_all = VE_E1.n

                sc_h1f0 = SC_E1.n

                @blk.tensor
                def _(t_, pe_a0=pe_a0, ve_khf_all=ve_khf_all, sc_h1f0=sc_h1f0):
                    t_.wait_ge(VE_E1.h, ve_khf_all)
                    k = 0
                    for mt in range(2):
                        for ns in range(4):
                            if k >= 2:
                                t_.wait_ge(SC_E1.h, sc_h1f0 + k - 1)
                            ins = t_.matmul(p_a[k % 2][:, 0:512],
                                            mw0b[:, mt * 128:(mt + 1) * 128],
                                            khf[0:64, ns * 512:(ns + 1) * 512],
                                            start=True, stop=True)
                            PE_E1.inc(ins)
                            k += 1

                @blk.scalar
                def _(s, pe_a0=pe_a0):
                    k = 0
                    for mt in range(2):
                        for ns in range(4):
                            s.wait_ge(PE_E1.h, pe_a0 + k + 1)
                            ins = s.activation(out=h1f[:, mt * 2048 + ns * 512: mt * 2048 + (ns + 1) * 512],
                                               in_=p_a[k % 2][:, 0:512], func=AF.Gelu)
                            SC_E1.inc(ins)
                            k += 1

                pe_b0 = PE_E1.n
                sc_h1f_all = SC_E1.n

                ve_p0tm0 = VE_E1.n

                @blk.tensor
                def _(t_, pe_b0=pe_b0, ve_p0tm0=ve_p0tm0):
                    t_.wait_ge(VE_E1.h, ve_p0tm0)
                    for t in range(NT):
                        if t >= 2:
                            t_.wait_ge(VE_E1.h, ve_p0tm0 + t - 1)
                        ins = t_.matmul(p_b[t % 2][:, 0:256], khf[0:64, t * 128:(t + 1) * 128],
                                        mw0b[:], start=True, stop=True)
                        PE_E1.inc(ins)

                @blk.vector
                def _(v, pe_b0=pe_b0):
                    for t in range(NT):
                        v.wait_ge(PE_E1.h, pe_b0 + t + 1)
                        ins = v.tensor_copy(pre0tm[:, t * 256:(t + 1) * 256], p_b[t % 2][:, 0:256])
                        VE_E1.inc(ins)
                        v.drain()

                pe_c0 = PE_E1.n
                ve_p0tm_all = VE_E1.n

                ve_pre10 = VE_E1.n

                @blk.tensor
                def _(t_, pe_c0=pe_c0, sc_h1f_all=sc_h1f_all, ve_pre10=ve_pre10):
                    t_.wait_ge(SC_E1.h, sc_h1f_all)
                    t_.wait_ge(VE_E1.h, ve_pre10)
                    for t in range(NT):
                        if t >= 2:
                            t_.wait_ge(VE_E1.h, ve_pre10 + t - 1)
                        ins = None
                        for kc in range(2):
                            ins = t_.matmul(p_b[t % 2][:, 0:64],
                                            h1f[:, kc * 2048 + t * 128: kc * 2048 + (t + 1) * 128],
                                            mw1b[:, kc * 64:(kc + 1) * 64],
                                            start=(kc == 0), stop=(kc == 1))
                        PE_E1.inc(ins)

                @blk.vector
                def _(v, pe_c0=pe_c0):
                    for t in range(NT):
                        v.wait_ge(PE_E1.h, pe_c0 + t + 1)
                        ins = v.tensor_copy(pre1[:, t * 64:(t + 1) * 64], p_b[t % 2][:, 0:64])
                        VE_E1.inc(ins)
                        v.drain()

                ve_pre1_all = VE_E1.n

                @blk.scalar
                def _(s, h=h, ve_pre1_all=ve_pre1_all):
                    s.wait_ge(VE_E1.h, ve_pre1_all)
                    ins = s.activation(out=h1f[:], in_=pre0tm[:], func=AF.Gelu)
                    SC_E1.inc(ins)
                h1tm = h1f

                sc_h1tm = SC_E1.n

                @blk.vector
                def _(v, h=h, sc_h1tm=sc_h1tm, lqh_n=LQH.n):
                    v.wait_ge(SC_E1.h, sc_h1tm)
                    v.wait_ge(LQH.h, lqh_n)
                    pre1v = pre1.rearrange("p (t d) -> p t d", d=64)
                    v.tensor_mul(dA, pre1[:], pre1[:])
                    v.drain()
                    v.tensor_reduce(small[:, 0:16].rearrange("p (t o) -> p t o", o=1),
                                    dA.rearrange("p (t d) -> p t d", d=64), axis=AX.X, op=ALU.add)
                    VE_E1.inc(v.drain())

                @blk.scalar
                def _(s):
                    s.wait_ge(VE_E1.h, VE_E1.n)
                    ins = s.activation(out=small[:, 16:32], in_=small[:, 0:16],
                                       func=AF.Sqrt, bias=cst[:, 2:3], scale=cst[:, 4:5])
                    SC_E1.inc(ins)

                @blk.vector
                def _(v, h=h):
                    v.wait_ge(SC_E1.h, SC_E1.n)
                    v.reciprocal(small[:, 0:16], small[:, 16:32])
                    v.drain()
                    irb = small[:, 0:16].rearrange("p (t o) -> p t o", o=1).to_broadcast([128, NT, 64])
                    v3 = lambda x: x.rearrange("p (t d) -> p t d", d=64)
                    v.tensor_tensor(v3(dA), v3(pre1[:]), irb, op=ALU.mult)
                    v.drain()
                    v.tensor_add(dB, dA, kh)
                    v.drain()
                    vals_h = ar2[:].rearrange("p (t n) -> p t n", n=1024)[:, :, h * 64:(h + 1) * 64]
                    v.tensor_tensor(v3(dC), v3(dB), vals_h, op=ALU.subtract)
                    v.drain()
                    thb = atheta[:].rearrange("p (t g) -> p t g", g=32)[:, :, 16 + h:17 + h].to_broadcast([128, NT, 64])
                    v.tensor_tensor(v3(dB), v3(dC), thb, op=ALU.mult)
                    v.drain()
                    v.tensor_mul(dC, dB, dA)
                    v.drain()
                    v.tensor_reduce(small[:, 16:32].rearrange("p (t o) -> p t o", o=1),
                                    v3(dC), axis=AX.X, op=ALU.add)
                    v.drain()
                    v.tensor_scalar(small[:, 16:32], small[:, 16:32], 1.0 / 64.0, None, op0=ALU.mult)
                    v.drain()
                    dyb = small[:, 16:32].rearrange("p (t o) -> p t o", o=1).to_broadcast([128, NT, 64])
                    v.tensor_tensor(v3(dC), v3(dA), dyb, op=ALU.mult)
                    v.drain()
                    v.tensor_sub(dC, dB, dC)
                    v.drain()
                    irb2 = small[:, 0:16].rearrange("p (t o) -> p t o", o=1).to_broadcast([128, NT, 64])
                    ins = v.tensor_tensor(v3(d_htm), v3(dC), irb2, op=ALU.mult)
                    VE_E1.inc(ins)
                    v.drain()

                pe_d0 = PE_E1.n
                ve_dh = VE_E1.n

                ve_dhf0 = VE_E1.n

                @blk.tensor
                def _(t_, pe_d0=pe_d0, ve_dh=ve_dh, ve_dhf0=ve_dhf0):
                    t_.wait_ge(VE_E1.h, ve_dh)
                    for t in range(NT):
                        if t >= 2:
                            t_.wait_ge(VE_E1.h, ve_dhf0 + t - 1)
                        ins = t_.transpose(p_tre[t % 2][0:64, 0:128], d_htm[:, t * 64:(t + 1) * 64],
                                           ident[:])
                        PE_E1.inc(ins)

                @blk.vector
                def _(v, pe_d0=pe_d0):
                    for t in range(NT):
                        v.wait_ge(PE_E1.h, pe_d0 + t + 1)
                        ins = v.tensor_copy(dhf[0:64, t * 128:(t + 1) * 128], p_tre[t % 2][0:64, 0:128])
                        VE_E1.inc(ins)
                        v.drain()

                pe_e0 = PE_E1.n
                ve_dhf_all = VE_E1.n
                sc_e0 = SC_E1.n
                gp_e0 = GP_E1.n
                ve_f0 = VE_E1.n

                ve_dh00 = VE_E1.n

                @blk.tensor
                def _(t_, pe_e0=pe_e0, ve_dhf_all=ve_dhf_all, ve_dh00=ve_dh00):
                    t_.wait_ge(VE_E1.h, ve_dhf_all)
                    for t in range(NT + 1):
                        if t < NT:
                            if t >= 2:
                                t_.wait_ge(VE_E1.h, ve_dh00 + 2 * t - 2)
                            ins = t_.matmul(p_b[t % 2][:, 0:256], dhf[0:64, t * 128:(t + 1) * 128],
                                            mw1T[:], start=True, stop=True)
                            PE_E1.inc(ins)
                        if t >= 1:
                            tt = t - 1
                            t_.wait_ge(VE_E1.h, ve_dh00 + 2 * tt + 2)
                            ins = t_.matmul(p_g0[0:64, 0:256], kh[:, tt * 64:(tt + 1) * 64],
                                            d_h0t[tt % 2], start=(tt == 0), stop=(tt == NT - 1))
                            PE_E1.inc(ins)

                rects = [ar3[:, 3072:3328], ar3[:, 3328:3584]]
                rect = [ar3[:, 3584:3840], ar3[:, 3840:4096]]

                @blk.scalar
                def _(s, sc_e0=sc_e0, gp_e0=gp_e0, ve_gg0=ve_dh00, ve_dh=ve_dh):
                    s.wait_ge(VE_E1.h, ve_dh)
                    for t in range(NT):
                        if t >= 2:
                            s.wait_ge(GP_E1.h, gp_e0 + 4 * (t - 1))
                            s.wait_ge(VE_E1.h, ve_gg0 + 2 * t - 3)
                        ins = s.activation(out=sq0t[t % 2], in_=pre0tm[:, t * 256:(t + 1) * 256],
                                           func=AF.Square)
                        SC_E1.inc(ins)
                        s.drain()
                        ins = s.activation(out=e1t[t % 2], in_=sq0t[t % 2], func=AF.Exp,
                                           bias=0.0, scale=cst[:, 6:7])
                        SC_E1.inc(ins)
                        s.drain()
                        ins = s.activation(out=rects[t % 2], in_=sq0t[t % 2], func=AF.Identity,
                                           bias=cst[:, 3:4])
                        SC_E1.inc(ins)
                        s.drain()
                        ins = s.activation(out=e1t[t % 2], in_=e1t[t % 2], func=AF.Identity,
                                           bias=0.0, scale=cst[:, 7:8])
                        SC_E1.inc(ins)
                        s.drain()

                ve_gg0 = VE_E1.n

                @blk.gpsimd
                def _(gp, h=h, sc_e0=sc_e0, ve_gg0=ve_gg0):
                    for t in range(NT):
                        gp.wait_ge(SC_E1.h, sc_e0 + 4 * (t + 1))
                        gp.wait_ge(VE_E1.h, ve_gg0 + 2 * t + 1)
                        if t >= 2:
                            gp.wait_ge(VE_E1.h, ve_gg0 + 2 * t - 2)
                        gp.tensor_tensor(numt[t % 2], h1tm[:, t * 256:(t + 1) * 256],
                                         pre0tm[:, t * 256:(t + 1) * 256], op=ALU.mult)
                        gp.drain()
                        gp.tensor_tensor(numt[t % 2], numt[t % 2], rect[t % 2], op=ALU.mult)
                        gp.drain()
                        gp.tensor_tensor(ggt[t % 2], pre0tm[:, t * 256:(t + 1) * 256], e1t[t % 2],
                                         op=ALU.mult)
                        gp.drain()
                        ins = gp.tensor_tensor(ggt[t % 2], ggt[t % 2], numt[t % 2], op=ALU.add)
                        GP_E1.inc(ins, 4)
                        gp.drain()

                @blk.vector
                def _(v, pe_e0=pe_e0, gp_e0=gp_e0, sc_e0=sc_e0):
                    for t in range(NT):
                        v.wait_ge(SC_E1.h, sc_e0 + 4 * t + 3)
                        if t >= 2:
                            v.wait_ge(GP_E1.h, gp_e0 + 4 * (t - 1))
                        ins = v.reciprocal(rect[t % 2], rects[t % 2])
                        VE_E1.inc(ins)
                        v.wait_ge(PE_E1.h, pe_e0 + max(1, 2 * t))
                        v.wait_ge(GP_E1.h, gp_e0 + 4 * (t + 1))
                        ins = v.tensor_tensor(d_h0t[t % 2], p_b[t % 2][:, 0:256], ggt[t % 2],
                                              op=ALU.mult)
                        VE_E1.inc(ins)
                        v.drain()

                ve_dh0_all = VE_E1.n

                @blk.tensor
                def _(t_, h=h):
                    for mh in range(2):
                        for t in range(NT):
                            ins = t_.matmul(p_g1[:, mh * 64:(mh + 1) * 64],
                                            h1tm[:, t * 256 + mh * 128: t * 256 + (mh + 1) * 128],
                                            d_htm[:, t * 64:(t + 1) * 64],
                                            start=(t == 0), stop=(t == NT - 1))
                        PE_E1.inc(ins)

                pe_gall = PE_E1.n

                @blk.scalar
                def _(s, pe_gall=pe_gall):
                    s.wait_ge(PE_E1.h, pe_gall)
                    ins = s.activation(out=dA[0:64, 0:256], in_=p_g0[0:64, 0:256], func=AF.Square)
                    SC_E1.inc(ins)
                    s.drain()
                    ins = s.activation(out=dA[:, 256:384], in_=p_g1[:, 0:128], func=AF.Square)
                    SC_E1.inc(ins)
                    s.drain()
                sc_clip = SC_E1.n

                @blk.vector
                def _(v, sc_clip=sc_clip):
                    v.wait_ge(SC_E1.h, sc_clip)
                    v.tensor_reduce(small[0:64, 32:33], dA[0:64, 0:256], axis=AX.X, op=ALU.add)
                    v.tensor_reduce(small[:, 33:34], dA[:, 256:384], axis=AX.X, op=ALU.add)
                    VE_E1.inc(v.drain())

                @blk.tensor
                def _(t_):
                    t_.wait_ge(VE_E1.h, VE_E1.n)
                    ins = t_.matmul(p_a[0][0:1, 0:1], small[0:64, 32:33], cst[0:64, 5:6],
                                    start=True, stop=False)
                    ins = t_.matmul(p_a[0][0:1, 0:1], small[:, 33:34], cst[:, 5:6],
                                    start=False, stop=True)
                    PE_E1.inc(ins)

                @blk.scalar
                def _(s):
                    s.wait_ge(PE_E1.h, PE_E1.n)
                    ins = s.activation(out=small[0:1, 34:35], in_=p_a[0][0:1, 0:1], func=AF.Sqrt)
                    SC_E1.inc(ins)

                @blk.vector
                def _(v):
                    v.wait_ge(SC_E1.h, SC_E1.n)
                    v.tensor_scalar(small[0:1, 34:35], small[0:1, 34:35], 1e-6, None, op0=ALU.add)
                    v.drain()
                    v.reciprocal(small[0:1, 35:36], small[0:1, 34:35])
                    v.drain()
                    v.tensor_scalar(small[0:1, 35:36], small[0:1, 35:36], 10.0, None, op0=ALU.mult)
                    v.drain()
                    v.tensor_scalar(small[0:1, 35:36], small[0:1, 35:36], 1.0, None, op0=ALU.min)
                    v.drain()
                    v.tensor_scalar(small[0:1, 35:36], small[0:1, 35:36], -1.0, None, op0=ALU.mult)
                    v.drain()
                    ins = v.tensor_copy(cfb, small[0:1, 35:36])
                    VE_E1.inc(ins)
                    v.drain()

                @blk.tensor
                def _(t_):
                    t_.wait_ge(VE_E1.h, VE_E1.n)
                    ins = t_.matmul(p_a[0][:, 8:9], onesrow, cfb, start=True, stop=True)
                    PE_E1.inc(ins)

                @blk.vector
                def _(v, h=h):
                    v.wait_ge(PE_E1.h, PE_E1.n)
                    v.tensor_copy(small[:, 36:37], p_a[0][:, 8:9])
                    v.drain()
                    v.tensor_scalar(w0a, mw0b[:], amrep[0:64, h:h + 1], None, op0=ALU.mult)
                    v.tensor_scalar(w1a, mw1b[:], amrep[:, h:h + 1], None, op0=ALU.mult)
                    v.drain()
                    v.scalar_tensor_tensor(w0n, p_g0[0:64, 0:256], small[0:64, 36:37], w0a,
                                           op0=ALU.mult, op1=ALU.add)
                    ins = v.scalar_tensor_tensor(w1n, p_g1[:, 0:128], small[:, 36:37], w1a,
                                                 op0=ALU.mult, op1=ALU.add)
                    VE_E1.inc(ins)
                    v.drain()

                pe_q0 = PE_E1.n
                ve_w = VE_E1.n

                ve_qhf0 = VE_E1.n

                @blk.tensor
                def _(t_, pe_q0=pe_q0, ve_w=ve_w, lqh_n=LQH.n, ve_qhf0=ve_qhf0):
                    t_.wait_ge(LQH.h, lqh_n)
                    t_.wait_ge(VE_E1.h, ve_w)
                    for t in range(NT):
                        if t >= 2:
                            t_.wait_ge(VE_E1.h, ve_qhf0 + t - 1)
                        ins = t_.transpose(p_tre[t % 2][0:64, 0:128], qh[:, t * 64:(t + 1) * 64],
                                           ident[:])
                        PE_E1.inc(ins)

                @blk.vector
                def _(v, pe_q0=pe_q0):
                    for t in range(NT):
                        v.wait_ge(PE_E1.h, pe_q0 + t + 1)
                        ins = v.tensor_copy(khf[0:64, t * 128:(t + 1) * 128], p_tre[t % 2][0:64, 0:128])
                        VE_E1.inc(ins)
                        v.drain()
                qhf = khf

                pe_m0 = PE_E1.n
                ve_qhf_all = VE_E1.n

                sc_a0f0 = SC_E1.n

                @blk.tensor
                def _(t_, pe_m0=pe_m0, ve_qhf_all=ve_qhf_all, sc_a0f0=sc_a0f0):
                    t_.wait_ge(VE_E1.h, ve_qhf_all)
                    k = 0
                    for mt in range(2):
                        for ns in range(4):
                            if k >= 2:
                                t_.wait_ge(SC_E1.h, sc_a0f0 + k - 1)
                            ins = t_.matmul(p_a[k % 2][:, 0:512],
                                            w0n[:, mt * 128:(mt + 1) * 128],
                                            qhf[0:64, ns * 512:(ns + 1) * 512],
                                            start=True, stop=True)
                            PE_E1.inc(ins)
                            k += 1

                a0f = pre0tm

                @blk.scalar
                def _(s, pe_m0=pe_m0):
                    k = 0
                    for mt in range(2):
                        for ns in range(4):
                            s.wait_ge(PE_E1.h, pe_m0 + k + 1)
                            ins = s.activation(out=a0f[:, mt * 2048 + ns * 512: mt * 2048 + (ns + 1) * 512],
                                               in_=p_a[k % 2][:, 0:512], func=AF.Gelu)
                            SC_E1.inc(ins)
                            k += 1

                pe_m1 = PE_E1.n
                sc_a0f_all = SC_E1.n
                m1tm = pre1

                ve_m10 = VE_E1.n

                @blk.tensor
                def _(t_, pe_m1=pe_m1, sc_a0f_all=sc_a0f_all, ve_m10=ve_m10):
                    t_.wait_ge(SC_E1.h, sc_a0f_all)
                    t_.wait_ge(VE_E1.h, ve_m10)
                    for t in range(NT):
                        if t >= 2:
                            t_.wait_ge(VE_E1.h, ve_m10 + t - 1)
                        ins = None
                        for kc in range(2):
                            ins = t_.matmul(p_b[t % 2][:, 0:64],
                                            a0f[:, kc * 2048 + t * 128: kc * 2048 + (t + 1) * 128],
                                            w1n[:, kc * 64:(kc + 1) * 64],
                                            start=(kc == 0), stop=(kc == 1))
                        PE_E1.inc(ins)

                @blk.vector
                def _(v, pe_m1=pe_m1):
                    for t in range(NT):
                        v.wait_ge(PE_E1.h, pe_m1 + t + 1)
                        ins = v.tensor_copy(m1tm[:, t * 64:(t + 1) * 64], p_b[t % 2][:, 0:64])
                        VE_E1.inc(ins)
                        v.drain()

                @blk.vector
                def _(v, h=h):
                    v.wait_ge(VE_E1.h, VE_E1.n)
                    v3 = lambda x: x.rearrange("p (t d) -> p t d", d=64)
                    v.tensor_mul(dA, m1tm[:], m1tm[:])
                    v.drain()
                    v.tensor_reduce(small[:, 0:16].rearrange("p (t o) -> p t o", o=1),
                                    v3(dA), axis=AX.X, op=ALU.add)
                    VE_E1.inc(v.drain())

                @blk.scalar
                def _(s):
                    s.wait_ge(VE_E1.h, VE_E1.n)
                    ins = s.activation(out=small[:, 16:32], in_=small[:, 0:16],
                                       func=AF.Sqrt, bias=cst[:, 2:3], scale=cst[:, 4:5])
                    SC_E1.inc(ins)

                @blk.vector
                def _(v, h=h):
                    v.wait_ge(SC_E1.h, SC_E1.n)
                    v.reciprocal(small[:, 0:16], small[:, 16:32])
                    v.drain()
                    v3 = lambda x: x.rearrange("p (t d) -> p t d", d=64)
                    irb = small[:, 0:16].rearrange("p (t o) -> p t o", o=1).to_broadcast([128, NT, 64])
                    v.tensor_tensor(v3(dA), v3(m1tm[:]), irb, op=ALU.mult)
                    v.drain()
                    s1h = s1tm[:].rearrange("p (t n) -> p t n", n=1024)[:, :, h * 64:(h + 1) * 64]
                    ins = v.tensor_tensor(s1h, v3(dA), qh.rearrange("p (t d) -> p t d", d=64),
                                          op=ALU.add)
                    VE_S1b.inc(ins)
                    v.drain()

            final_waits += [VE_S1b]

        if stage >= 7:
            phase_join()
            es_psE.close()
            es_psF = ExitStack()
            p_tf = es_psF.enter_context(nc.psum_tensor("p_tf", [128, 1024], BF16))
            p_q = es_psF.enter_context(nc.psum_tensor("p_q", [128, 1024]))

            LF2 = sem("LF2"); LF2g = sem("LF2g")
            VE_F1 = sem("VE_F1"); PE_F1 = sem("PE_F1"); SC_F1 = sem("SC_F1")
            STP = [sem("STa"), sem("STb")]
            st_counts = [[], []]

            wqtb = ar1[:, 0:8192]
            xn1t_f = ar3[:, 0:1024]
            t1f = ar3[:, 1024:2048]
            xof = [ar3[:, 2048:3072], ar3[:, 3072:4096]]
            wscb = ar3[:, 4096:5120]
            wqbb = ar3[:, 5120:6144]
            xqt = vring[:, 0:1024]
            xqf = vring[:, 1024:2048]
            g1t = vring[:, 2048:3072]
            s1g = vring[:, 3072:4096]

            @blk.sync
            def _(sync):
                dmaN(sync, LF2, wscb, wsc_in)
                dmaN(sync, LF2, wqbb, wqb_in)

            @blk.gpsimd
            def _(gp):
                dmaN(gp, LF2g, wqtb.rearrange("p (c n) -> p c n", n=1024),
                     wqt_in.rearrange("(c p) n -> p c n", p=128))
            lf_n = LF2.n
            lf_ng = LF2g.n

            for t in range(NT):
                @blk.vector
                def _(v, t=t):
                    if t == 0:
                        v.wait_ge(LF2.h, lf_n)
                        v.wait_ge(LF2g.h, lf_ng)
                    if t >= 2 and st_counts[t % 2]:
                        v.wait_ge(STP[t % 2].h, st_counts[t % 2][-1])
                    mu = stat1[:, 2 * t:2 * t + 1]
                    rs = rstd1[:, t:t + 1]
                    v.tensor_scalar(xn1t_f, x_sb[:, t * D:(t + 1) * D], mu, rs,
                                    op0=ALU.subtract, op1=ALU.mult)
                    v.drain()
                    v.tensor_reduce(small[:, 0:1], xn1t_f, axis=AX.X, op=ALU.max,
                                    apply_absolute_value=True)
                    v.drain()
                    v.tensor_scalar(small[:, 1:2], small[:, 0:1], 1e-6, 1.0 / 127.0,
                                    op0=ALU.max, op1=ALU.mult)
                    v.drain()
                    v.reciprocal(small[:, 2:3], small[:, 1:2])
                    v.drain()
                    v.tensor_scalar(xn1t_f, xn1t_f, small[:, 2:3], None, op0=ALU.mult)
                    v.drain()
                    v.tensor_scalar(xn1t_f, xn1t_f, 12582912.0, 12582912.0,
                                    op0=ALU.add, op1=ALU.subtract)
                    v.drain()
                    v.tensor_scalar(xn1t_f, xn1t_f, 127.0, -127.0, op0=ALU.min, op1=ALU.max)
                    v.drain()
                    ins = v.tensor_copy(xqt, xn1t_f)
                    VE_F1.inc(ins)
                    v.drain()

                @blk.tensor
                def _(t_, t=t):
                    t_.wait_ge(VE_F1.h, VE_F1.n)
                    ins = None
                    for c in range(NC):
                        ins = t_.transpose(p_tf[:, c * 128:(c + 1) * 128],
                                           xqt[:, c * 128:(c + 1) * 128], ident[:])
                    PE_F1.inc(ins)

                @blk.vector
                def _(v, t=t):
                    v.wait_ge(PE_F1.h, PE_F1.n)
                    ins = v.tensor_copy(xqf, p_tf[:])
                    VE_F1.inc(ins)
                    v.drain()

                @blk.tensor
                def _(t_, t=t):
                    t_.wait_ge(VE_F1.h, VE_F1.n)
                    ins = None
                    for ns in range(2):
                        for c in range(NC):
                            ins = t_.matmul(p_q[:, ns * 512:(ns + 1) * 512],
                                            xqf[:, c * 128:(c + 1) * 128],
                                            wqtb[:, c * 1024 + ns * 512: c * 1024 + (ns + 1) * 512],
                                            start=(c == 0), stop=(c == NC - 1))
                    PE_F1.inc(ins)

                @blk.vector
                def _(v, t=t):
                    v.wait_ge(PE_F1.h, PE_F1.n)
                    v.tensor_scalar(t1f, p_q[:], small[:, 1:2], None, op0=ALU.mult)
                    v.drain()
                    v.tensor_mul(t1f, t1f, wscb)
                    v.drain()
                    ins = v.tensor_add(t1f, t1f, wqbb)
                    VE_F1.inc(ins)
                    v.drain()

                @blk.scalar
                def _(s, t=t):
                    s.wait_ge(VE_F1.h, VE_F1.n)
                    ins = s.activation(out=g1t, in_=t1f, func=AF.Silu)
                    SC_F1.inc(ins)

                @blk.vector
                def _(v, t=t):
                    v.wait_ge(SC_F1.h, SC_F1.n)
                    v.tensor_mul(s1g, s1tm[:, t * D:(t + 1) * D], g1t)
                    v.drain()
                    ins = v.tensor_add(xof[t % 2], x_sb[:, t * D:(t + 1) * D], s1g)
                    VE_F1.inc(ins)
                    v.drain()

                @blk.sync
                def _(sync, t=t):
                    sync.wait_ge(VE_F1.h, VE_F1.n)
                    dst = out_hbm.rearrange("(t p) d -> t p d", p=128)[t]
                    dmaN(sync, STP[t % 2], dst, xof[t % 2])
                    st_counts[t % 2].append(STP[t % 2].n)

            final_waits += [STP[0], STP[1]]

        @blk.sync
        def _(sync):
            for s_ in final_waits:
                sync.wait_ge(s_.h, s_.n)

            def store(dst, src_ap):
                dmaN(sync, ST, dst, src_ap)
            if t_xn0f is not None:
                store(t_xn0f, xn0f[:])
            if t_negc is not None:
                store(t_negc, negc[:])
            if t_comb is not None:
                store(t_comb, comb[:])
            if t_g is not None and stage >= 2:
                store(t_g, ar2[:, 0:NT * D])
            if t_x1f is not None and stage >= 3:
                store(t_x1f, fm2[:])
            if t_vals is not None and stage >= 5:
                store(t_vals, ar2[:])
            if t_s1 is not None and stage >= 6:
                store(t_s1, fm2[:])
            if stage < 7:
                store(out_hbm.rearrange("(t p) d -> p t d", p=128),
                      x_sb[:].rearrange("p (t d) -> p t d", d=D))
            sync.wait_ge(ST.h, ST.n)

    return nc, tap_outs



def _host_inputs(core, inp):
    bf = ml_dtypes.bfloat16
    f32 = np.float32
    A = lambda x: np.ascontiguousarray(np.asarray(x, f32))
    d = {}
    d["x"] = A(inp["L0_attn_src"][core])
    d["ident"] = np.eye(128, dtype=bf)
    jj = np.arange(128)[:, None]; ii = np.arange(128)[None, :]
    d["masks"] = np.concatenate([(jj >= ii).astype(bf), (jj <= ii).astype(bf)], axis=1)
    d["gate_w"] = A(inp["L0_ffn_gate_w"])
    bcast = lambda vec: np.ascontiguousarray(np.broadcast_to(np.asarray(vec, f32)[None, :], (128, 1024)))
    bcastb = lambda vec: np.ascontiguousarray(np.broadcast_to(
        np.asarray(vec, f32).astype(bf)[None, :], (128, 1024)))
    d["ln1g"] = bcast(inp["L0_attn_ln1_g"]); d["ln1b"] = bcast(inp["L0_attn_ln1_b"])
    d["ln2g"] = bcastb(inp["L0_attn_ln2_g"]); d["ln2b"] = bcastb(inp["L0_attn_ln2_b"])
    d["ffn_w1"] = A(inp["L0_attn_W1"]); d["ffn_w2"] = A(inp["L0_attn_W2"])
    d["ffn_b1"] = np.ascontiguousarray(np.asarray(inp["L0_attn_b1"], f32).reshape(32, 128).T)
    d["ffn_b2"] = bcastb(inp["L0_attn_b2"])
    d["moe_w1"] = A(inp["L0_ffn_w1"]); d["moe_w2"] = A(inp["L0_ffn_w2"])
    d["moe_b1"] = np.ascontiguousarray(np.asarray(inp["L0_ffn_b1"], f32).reshape(8, 32, 128).transpose(0, 2, 1))
    d["moe_b2"] = A(inp["L0_ffn_b2"])
    d["wk"] = A(inp["L1_attn_W_K"]); d["wv"] = A(inp["L1_attn_W_V"]); d["wq"] = A(inp["L1_attn_W_Q"])
    fmcols = lambda vec: np.ascontiguousarray(np.asarray(vec, f32).reshape(8, 128).T)
    d["snw"] = fmcols(inp["L1_attn_store_norm_w"])
    d["rnw"] = fmcols(inp["L1_attn_retrieve_norm_w"])
    d["knw"] = np.ascontiguousarray(np.broadcast_to(
        np.tile(np.asarray(inp["L1_attn_k_norm_w"], f32), 16)[None, :], (128, 1024)))
    d["qnw"] = np.ascontiguousarray(np.broadcast_to(
        np.tile(np.asarray(inp["L1_attn_q_norm_w"], f32), 16)[None, :], (128, 1024)))
    d["mw0"] = A(inp["L1_attn_mem_W0"]); d["mw1"] = A(inp["L1_attn_mem_W1"])
    d["atw"] = np.ascontiguousarray(np.concatenate(
        [np.asarray(inp["L1_attn_alpha_w"], f32), np.asarray(inp["L1_attn_theta_w"], f32)], axis=1))
    atb = np.concatenate([np.asarray(inp["L1_attn_alpha_b"], f32),
                          np.asarray(inp["L1_attn_theta_b"], f32)])
    d["atb"] = np.ascontiguousarray(np.broadcast_to(atb[None, :], (128, 32)))
    d["wqt"] = np.ascontiguousarray(np.asarray(inp["L1_ffn_weight_q"]).T)
    d["wsc"] = np.ascontiguousarray(np.broadcast_to(
        np.asarray(inp["L1_ffn_weight_scale"], f32)[None, :], (128, 1024)))
    d["wqb"] = np.ascontiguousarray(np.broadcast_to(
        np.asarray(inp["L1_ffn_bias"], f32)[None, :], (128, 1024)))
    return d


def kernel(**inputs):
    nc, _ = build_program()
    in_maps = [_host_inputs(c, inputs) for c in range(8)]
    res = run_bass_kernel_spmd(nc, in_maps, core_ids=list(range(8)))
    return np.stack([r["out"] for r in res.results]).astype(np.float32)
